# version 32
# baseline (speedup 1.0000x reference)
import numpy as np
from contextlib import ExitStack

TOK = 2048
D = 1024
DC = 8
NG = 4
GS = 512
NT = 16
NCORE = 8
E = 8
HID = 4096
CAP = 4301
CAPPAD = 4608
NST = CAPPAD // 128
THALF = CAPPAD * 4 + 64
GARB = CAPPAD * 4
TSZP = 2 * THALF
GN_N = float(8192 * 1024)
EPS = 1e-5
NPAIR = 8
NSTEP = 64

STAGE = 9


def _build(stage=STAGE):
    import concourse.bass as bass
    import concourse.mybir as mybir
    from concourse.alu_op_type import AluOpType as AO
    F32 = mybir.dt.float32
    BF16 = mybir.dt.bfloat16
    U32 = mybir.dt.uint32
    AF = mybir.ActivationFunctionType
    AX = mybir.AxisListType
    IOOA = bass.IndirectOffsetOnAxis

    nc = bass.Bass("TRN2", target_bir_lowering=False)

    xin = nc.dram_tensor("xin", [TOK, D], F32, kind="ExternalInput")
    wq_d = nc.dram_tensor("wq", [D, D], F32, kind="ExternalInput")
    wk_d = nc.dram_tensor("wk", [D, D], F32, kind="ExternalInput")
    wv_d = nc.dram_tensor("wv", [D, D], F32, kind="ExternalInput")
    wg_d = nc.dram_tensor("wg", [D, E], F32, kind="ExternalInput")
    w1_d = nc.dram_tensor("w1", [D, HID], F32, kind="ExternalInput")
    b1_d = nc.dram_tensor("b1", [1, HID], F32, kind="ExternalInput")
    w2_d = nc.dram_tensor("w2", [HID, D], F32, kind="ExternalInput")
    b2_d = nc.dram_tensor("b2", [1, D], F32, kind="ExternalInput")
    idf_d = nc.dram_tensor("idf", [128, 128], F32, kind="ExternalInput")
    idb_d = nc.dram_tensor("idb", [128, 128], BF16, kind="ExternalInput")
    tri_d = nc.dram_tensor("tri", [128, 128], BF16, kind="ExternalInput")
    trib_d = nc.dram_tensor("trib", [16, 2048], BF16, kind="ExternalInput")
    pmask_d = nc.dram_tensor("pmask", [8, 128], BF16, kind="ExternalInput")
    bmask_d = nc.dram_tensor("bmask", [8, 1], F32, kind="ExternalInput")
    eoh_d = nc.dram_tensor("eoh", [8, 1], F32, kind="ExternalInput")
    iota8_d = nc.dram_tensor("iota8", [128, 128], F32, kind="ExternalInput")
    tokrow_d = nc.dram_tensor("tokrow", [128, 16], F32, kind="ExternalInput")
    garbA_d = nc.dram_tensor("garbA", [128, 16], F32, kind="ExternalInput")
    garbB_d = nc.dram_tensor("garbB", [128, 16], F32, kind="ExternalInput")
    out_d = nc.dram_tensor("out", [TOK, D], F32, kind="ExternalOutput")
    dbgA = dbgB = None
    if stage < 9:
        dbgA = nc.dram_tensor("dbgA", [128, DC * TOK], F32, kind="ExternalOutput")
        dbgB = nc.dram_tensor("dbgB", [128, DC * TOK], F32, kind="ExternalOutput")
    st_in = nc.dram_tensor("st_in", [1, 2], F32)
    st_out = nc.dram_tensor("st_out", [8, 2], F32)
    cnt_in = nc.dram_tensor("cnt_in", [1, 8], F32)
    cnt_out = nc.dram_tensor("cnt_out", [8, 8], F32)
    t_in = nc.dram_tensor("t_in", [TSZP], F32)
    t_out = nc.dram_tensor("t_out", [TSZP], F32)
    xt_in = nc.dram_tensor("xt_in", [TOK, D], BF16)
    xt_out = nc.dram_tensor("xt_out", [NCORE * TOK, D], BF16)
    y_in = nc.dram_tensor("y_in", [CAPPAD, D], BF16)
    yo1 = nc.dram_tensor("yo1", [NCORE * 2048, D], BF16)
    yo2 = nc.dram_tensor("yo2", [NCORE * 2048, D], BF16)
    yo3 = nc.dram_tensor("yo3", [NCORE * 512, D], BF16)
    yab = nc.dram_tensor("yab", [2 * TOK, D], BF16)
    w1s = nc.dram_tensor("w1s", [128, DC * HID], BF16)
    w2s = nc.dram_tensor("w2s", [128, 32 * D], BF16)
    xspill = nc.dram_tensor("xspill", [TOK, D], F32)
    tex_d = nc.dram_tensor("texd", [CAPPAD], F32)

    es = ExitStack()
    esA = ExitStack()

    def sb(stk, name, shape, dt):
        return stk.enter_context(nc.sbuf_tensor(name, shape, dt))

    stg = sb(es, "stg", [128, 1024], F32)
    onesw = sb(es, "onesw", [1, 512], BF16)
    idbs = sb(es, "idbs", [128, 128], BF16)
    sidg = sb(es, "sidg", [128, 96], U32)
    cfs = sb(es, "cfs", [128, 32], F32)
    xidxf = sb(es, "xidxf", [128, 36], F32)
    xidxu = sb(es, "xidxu", [128, 36], U32)

    xl = sb(esA, "xl", [128, DC * TOK], BF16)
    xf = sb(esA, "xf", [128, DC * TOK], F32)
    wqT = sb(esA, "wqT", [128, DC * D], BF16)
    wkT = sb(esA, "wkT", [128, DC * D], BF16)
    wvT = sb(esA, "wvT", [128, DC * D], BF16)
    stgb = sb(esA, "stgb", [128, 1024], BF16)
    usb = sb(esA, "usb", [128, 1024], F32)
    rcp = sb(esA, "rcp", [1, 1024], F32)
    t1s = sb(esA, "t1s", [128, 512], F32)
    murs = sb(esA, "murs", [1, 1024], BF16)
    sc2 = sb(esA, "sc2", [128, 2], F32)
    rsm = sb(esA, "rsm", [128, 2], F32)
    ssb = sb(esA, "ssb", [8, 2], F32)
    cnts = sb(esA, "cnts", [8, 8], F32)
    cntsb = sb(esA, "cntsb", [8, 8], BF16)
    tot16 = sb(esA, "tot16", [16, 8], F32)
    tot16b = sb(esA, "tot16b", [16, 8], BF16)
    sg1 = sb(esA, "sg1", [4, 16], F32)
    onesc = sb(esA, "onesc", [128, 1], BF16)
    onescf = sb(esA, "onescf", [128, 1], F32)
    onesr = sb(esA, "onesr", [1, 128], BF16)
    onesrf = sb(esA, "onesrf", [1, 128], F32)
    garbv = sb(esA, "garbv", [128, 1], F32)
    epsv = sb(esA, "epsv", [1, 1], F32)
    idfs = sb(esA, "idfs", [128, 128], F32)
    tris = sb(esA, "tris", [128, 128], BF16)
    pms = sb(esA, "pms", [8, 128], BF16)
    bms = sb(esA, "bms", [8, 1], F32)
    eohs = sb(esA, "eohs", [8, 1], F32)
    io8 = sb(esA, "io8", [128, 128], F32)
    tkr = sb(esA, "tkr", [128, 16], F32)
    garbA = sb(esA, "garbAs", [128, 16], F32)
    garbB = sb(esA, "garbBs", [128, 16], F32)
    wgb = sb(esA, "wgb", [128, DC * E], BF16)
    rt = sb(esA, "rt", [128, 128], F32)
    r1 = sb(esA, "r1", [128, 16], F32)
    r2 = sb(esA, "r2", [128, 16], F32)
    rg1 = sb(esA, "rg1", [128, 16], F32)
    rg2 = sb(esA, "rg2", [128, 16], F32)
    re1 = sb(esA, "re1", [128, 128], F32)
    re2 = sb(esA, "re2", [128, 128], F32)
    roh = sb(esA, "roh", [128, 128], BF16)
    rsc = sb(esA, "rsc", [128, 8], F32)
    rsd = sb(esA, "rsd", [128, 8], F32)
    rse = sb(esA, "rse", [128, 8], F32)
    posj = sb(esA, "posj", [128, 64], F32)
    tokv = sb(esA, "tokv", [128, 16], F32)
    sidu = sb(esA, "sidu", [128, 32], U32)

    esA2 = ExitStack()
    ka = sb(esA2, "ka", [128, DC * GS], BF16)
    kb = sb(esA2, "kb", [128, DC * GS], BF16)
    qa = sb(esA2, "qa", [128, DC * GS], BF16)
    va = sb(esA2, "va", [128, 4 * 1040], BF16)
    exb = sb(esA2, "exb", [128, 3 * 1024], BF16)

    sps = [es.enter_context(nc.psum_tensor("sps0", [128, 1024], F32)),
           es.enter_context(nc.psum_tensor("sps1", [128, 1024], F32))]
    pue = es.enter_context(nc.psum_tensor("pue", [128, 512], F32))
    puo = es.enter_context(nc.psum_tensor("puo", [128, 512], F32))
    pbf = es.enter_context(nc.psum_tensor("pbf", [128, 512], BF16))
    pun = es.enter_context(nc.psum_tensor("pun", [128, 512], F32))

    snames = ["ld", "te", "ve", "se", "gp", "pm", "mm", "exq", "uc", "fin", "rc",
              "bc", "yd", "pe", "cc", "st", "g2", "v2", "t2", "s2", "v3", "t3",
              "g3", "s3", "io", "wc", "ws", "wv", "wc2", "ws2", "wc3"]
    sems = {n: es.enter_context(nc.semaphore("s_" + n)) for n in snames}
    cnt = {n: 0 for n in snames}

    class Plan:
        def __init__(self):
            self.ev = {k: [] for k in ("sync", "tensor", "vector", "scalar", "gpsimd")}

        def add(self, eng, fn, waits=(), inc=None, k=1):
            self.ev[eng].append((fn, list(waits), inc, k))
            if inc is not None:
                cnt[inc] += k

        def emit(self):
            with nc.Block() as block:
                for name in ("sync", "tensor", "vector", "scalar", "gpsimd"):
                    evs = self.ev[name]
                    if not evs:
                        continue

                    def make(evs):
                        def emit_fn(eng):
                            for fn, w, inc, k in evs:
                                for s, v in w:
                                    eng.wait_ge(sems[s], v)
                                ins = fn(eng)
                                if inc is not None:
                                    ins.then_inc(sems[inc], k)
                        return emit_fn
                    getattr(block, name)(make(evs))

    def now(s):
        return cnt[s]

    def xfc(d, c0, n):
        return xf[:, 2048 * d + c0: 2048 * d + c0 + n]

    def xlc(d, c0, n):
        return xl[:, 2048 * d + c0: 2048 * d + c0 + n]

    p = Plan()
    for t, d in [(idfs, idf_d), (idbs, idb_d), (tris, tri_d),
                 (pms, pmask_d), (bms, bmask_d), (eohs, eoh_d), (io8, iota8_d),
                 (tkr, tokrow_d), (garbA, garbA_d), (garbB, garbB_d)]:
        p.add("sync", lambda e, t=t, d=d: e.dma_start(out=t[:], in_=d.ap()[:, :]), inc="ld", k=16)
    NCONST = cnt["ld"]
    for mset in [lambda e: e.memset(onesc[:], 1.0), lambda e: e.memset(onescf[:], 1.0),
                 lambda e: e.memset(onesr[:], 1.0), lambda e: e.memset(onesrf[:], 1.0),
                 lambda e: e.memset(onesw[:], 1.0),
                 lambda e: e.memset(garbv[:], float(GARB)),
                 lambda e: e.memset(epsv[:], EPS),
                 lambda e: e.memset(va[:, 64::65], 1.0), lambda e: e.memset(usb[:], 0.0)]:
        p.add("gpsimd", mset, inc="gp")
    NGP = cnt["gp"]

    for i in range(NT):
        p.add("sync", lambda e, i=i: e.dma_start(out=stg[:, :], in_=xin.ap()[128 * i:128 * i + 128, :]),
              waits=([("te", 2 * i)] if i >= 1 else []), inc="ld", k=16)
        for d4 in range(2):
            for j in range(4):
                dd = 4 * d4 + j
                w = []
                if j == 0:
                    w = [("ld", NCONST + 16 * (i + 1))]
                    if i >= 1:
                        w.append(("ve", 2 * (i - 1) + d4 + 1))
                p.add("tensor", lambda e, d4=d4, j=j, dd=dd: e.transpose(
                    sps[0][:, 512 * d4 + 128 * j: 512 * d4 + 128 * j + 128],
                    stg[:, 128 * dd: 128 * dd + 128], idfs[:, :]),
                    waits=w, inc=("te" if j == 3 else None))
            p.add("vector", lambda e, d4=d4, i=i: e.tensor_copy(
                xf[:, :].rearrange("p (d i c) -> p d i c", d=DC, i=NT, c=128)[:, 4 * d4:4 * d4 + 4, i, :],
                sps[0][:, 512 * d4: 512 * d4 + 512].rearrange("p (a c) -> p a c", a=4)),
                waits=[("te", 2 * i + d4 + 1)], inc="ve")
    TE_XIN = cnt["te"]

    CAST_VE = {}
    for wi, (wd, wt) in enumerate([(wq_d, wqT), (wk_d, wkT), (wv_d, wvT)]):
        for r in range(8):
            idx = 8 * wi + r
            sbuf = stgb[:, 0:1024] if idx % 2 == 0 else exb[:, 0:1024]
            wldw = [("te", TE_XIN)] if idx == 0 else [("ve", CAST_VE[idx - 1])]
            p.add("sync", lambda e, wd=wd, r=r: e.dma_start(
                out=stg[:, :], in_=wd.ap()[128 * r:128 * r + 128, :]),
                waits=wldw, inc="ld", k=16)
            wcst = [("ld", now("ld"))]
            if idx >= 2:
                wcst.append(("te", TE_XIN + 2 * (idx - 1)))
            p.add("vector", lambda e, sbuf=sbuf: e.tensor_copy(sbuf, stg[:, :]),
                  waits=wcst, inc="ve")
            cast_ve = now("ve")
            CAST_VE[idx] = cast_ve
            for d4 in range(2):
                for j in range(4):
                    dd = 4 * d4 + j
                    w = []
                    if j == 0:
                        w = [("ve", cast_ve), ("ve", now("ve"))]
                    p.add("tensor", lambda e, dd=dd, j=j, wi=wi, r=r: e.transpose(
                        pbf[:, 128 * j: 128 * j + 128],
                        (stgb if (8 * wi + r) % 2 == 0 else exb)[:, 128 * dd: 128 * dd + 128], idbs[:, :]),
                        waits=w, inc=("te" if j == 3 else None))
                p.add("vector", lambda e, d4=d4, r=r, wt=wt: e.tensor_copy(
                    wt[:, :].rearrange("p (d c) -> p d c", d=DC)[:, 4 * d4:4 * d4 + 4, 128 * r:128 * r + 128],
                    pbf[:, :].rearrange("p (a c) -> p a c", a=4)),
                    waits=[("te", now("te"))], inc="ve")
    p.add("sync", lambda e: e.dma_start(
        out=stg[:, 0:64].rearrange("p (d c) -> p d c", d=DC),
        in_=wg_d.ap()[:, :].rearrange("(d p) c -> p d c", d=DC)),
        waits=[("te", now("te"))], inc="ld", k=16)
    p.add("vector", lambda e: e.tensor_copy(wgb[:, :], stg[:, 0:64]),
          waits=[("ld", now("ld")), ("te", now("te"))], inc="ve")

    def emit_ln(p):
        for tt in range(4):
            c0 = 512 * tt
            for d in range(DC):
                w_cast = [("te", now("te")), ("ve", now("ve"))]
                if tt == 0 and d == 0:
                    w_cast.append(("gp", NGP))
                p.add("vector", lambda e, d=d, c0=c0: e.tensor_copy(stgb[:, 0:512], xfc(d, c0, 512)),
                      waits=w_cast, inc="ve")
                p.add("scalar", lambda e, d=d, c0=c0: e.activation(
                    stgb[:, 512:1024], xfc(d, c0, 512), AF.Square),
                    waits=[("te", now("te"))], inc="se")
                p.add("tensor", lambda e, d=d: e.matmul(
                    pue[0:1, :], onesc[:, :], stgb[:, 0:512], start=(d == 0), stop=(d == DC - 1)),
                    waits=[("ve", now("ve")), ("se", now("se"))] +
                          ([("ve", now("ve"))] if (tt > 0 and d == 0) else []),
                    inc=None)
                p.add("tensor", lambda e, d=d: e.matmul(
                    puo[0:1, :], onesc[:, :], stgb[:, 512:1024], start=(d == 0), stop=(d == DC - 1)),
                    inc="te")
            p.add("vector", lambda e, c0=c0: e.tensor_single_scalar(murs[0:1, 0:512], pue[0:1, :], 1.0 / D, AO.mult),
                waits=[("te", now("te"))], inc="ve")
            p.add("vector", lambda e: e.tensor_single_scalar(t1s[0:1, 0:512], puo[0:1, :], 1.0 / D, AO.mult),
                waits=[("ve", now("ve"))], inc="ve")
            p.add("scalar", lambda e, c0=c0: e.activation(
                rcp[0:1, 0:512], murs[0:1, 0:512], AF.Square),
                waits=[("ve", now("ve"))], inc="se")
            p.add("vector", lambda e: e.tensor_tensor(
                t1s[0:1, 0:512], t1s[0:1, 0:512], rcp[0:1, 0:512], AO.subtract),
                waits=[("se", now("se")), ("ve", now("ve"))], inc="ve")
            p.add("scalar", lambda e: e.activation(
                t1s[64:65, 0:512], t1s[0:1, 0:512], AF.Sqrt, bias=epsv[0:1, 0:1]),
                waits=[("ve", now("ve"))], inc="se")
            p.add("vector", lambda e: e.reciprocal(t1s[96:97, 0:512], t1s[64:65, 0:512]),
                  waits=[("se", now("se"))], inc="ve")
            p.add("vector", lambda e, c0=c0: e.tensor_copy(murs[0:1, 512:1024], t1s[96:97, 0:512]),
                  waits=[("ve", now("ve"))], inc="ve")
            p.add("tensor", lambda e, c0=c0: e.matmul(
                sps[0][:, 0:512], onesr[:, :], murs[0:1, 0:512], start=True, stop=True),
                waits=[("ve", now("ve"))], inc=None)
            p.add("tensor", lambda e, c0=c0: e.matmul(
                sps[0][:, 512:1024], onesr[:, :], murs[0:1, 512:1024], start=True, stop=True),
                inc="te")
            for d in range(DC):
                p.add("vector", lambda e, d=d, c0=c0: e.tensor_tensor(
                    t1s[:, 0:512], xfc(d, c0, 512), sps[0][:, 0:512], AO.subtract),
                    waits=[("te", now("te")), ("ve", now("ve"))], inc="ve")
                p.add("vector", lambda e, d=d, c0=c0: e.tensor_tensor(
                    xlc(d, c0, 512), t1s[:, 0:512], sps[0][:, 512:1024], AO.mult),
                    waits=[("ve", now("ve"))], inc="ve")

    emit_ln(p)
    p.emit()

    def emit_dump(srcA_f32, srcB_bf16):
        pd = Plan()
        if srcA_f32 is not None:
            pd.add("sync", lambda e: e.dma_start(out=dbgA.ap()[:, :], in_=srcA_f32[:, :]),
                   inc="st", k=16)
        for ch in range(16):
            pd.add("vector", lambda e, ch=ch: e.tensor_copy(
                stg[:, :], srcB_bf16[:, 1024 * ch: 1024 * ch + 1024]),
                waits=[("st", now("st"))], inc="ve")
            pd.add("sync", lambda e, ch=ch: e.dma_start(
                out=dbgB.ap()[:, 1024 * ch: 1024 * ch + 1024], in_=stg[:, :]),
                waits=[("ve", now("ve"))], inc="st", k=16)
        pd.add("sync", lambda e: e.dma_start(out=out_d.ap()[0:128, :], in_=stg[:, :]),
               waits=[("st", now("st"))], inc="st", k=16)
        pd.add("vector", lambda e: e.tensor_copy(t1s[0:1, 0:1], t1s[0:1, 0:1]),
               waits=[("st", now("st"))])
        pd.emit()

    if stage == 1:
        emit_dump(xf, xl)
        esA2.close(); esA.close(); es.close()
        return nc

    p = Plan()
    pslots = [(0, 0), (0, 1), (1, 0), (1, 1)]

    def proj_psum(slot):
        si, hh = pslots[slot % 4]
        return sps[si][:, 512 * hh: 512 * hh + 512]

    def emit_u(gstep, first, last):
        par3 = gstep % 3
        k = gstep % NSTEP
        ck, tkb = k // 4, k % 4
        w = [("exq", gstep + 1)]
        if first:
            w.append(("fin", gstep // NSTEP))
        p.add("tensor", lambda e, ck=ck, tkb=tkb, par3=par3, first=first, last=last: e.matmul(
            pue[0:65, :], va[:, 1040 * tkb + 65 * ck: 1040 * tkb + 65 * ck + 65],
            exb[:, 1024 * par3: 1024 * par3 + 512], start=first, stop=last),
            waits=w, inc=None)
        p.add("tensor", lambda e, ck=ck, tkb=tkb, par3=par3, first=first, last=last: e.matmul(
            puo[0:65, :], va[:, 1040 * tkb + 65 * ck: 1040 * tkb + 65 * ck + 65],
            exb[:, 1024 * par3 + 512: 1024 * par3 + 1024], start=first, stop=last),
            inc="uc")

    nproj = 0
    MM_END = 0
    pending_fin = [None]
    for g in range(NG):
        gc = 512 * g
        if pending_fin[0] is not None:
            pending_fin[0]()
            pending_fin[0] = None
        for kind in range(3):
            for u in range(8):
                slot = nproj % 4
                w0 = []
                if nproj >= 4:
                    w0.append(("pe", cnt["pe"] - 3))
                if g > 0 and kind == 0 and u == 0:
                    w0.append(("yd", NPAIR * g))
                    w0.append(("exq", NSTEP * NPAIR * g))
                for d in range(DC):
                    if kind == 0:
                        fn = lambda e, u=u, d=d, gc=gc, slot=slot: e.matmul(
                            proj_psum(slot), wqT[:, 1024 * d + 128 * u: 1024 * d + 128 * u + 128],
                            xl[:, 2048 * d + gc: 2048 * d + gc + 512],
                            start=(d == 0), stop=(d == DC - 1))
                    elif kind == 1:
                        fn = lambda e, u=u, d=d, gc=gc, slot=slot: e.matmul(
                            proj_psum(slot), wkT[:, 1024 * d + 128 * u: 1024 * d + 128 * u + 128],
                            xl[:, 2048 * d + gc: 2048 * d + gc + 512],
                            start=(d == 0), stop=(d == DC - 1))
                    else:
                        tc, dvh = u // 2, u % 2
                        fn = lambda e, tc=tc, dvh=dvh, d=d, gc=gc, slot=slot: e.matmul(
                            proj_psum(slot), xl[:, 2048 * d + gc + 128 * tc: 2048 * d + gc + 128 * tc + 128],
                            wvT[:, 1024 * d + 512 * dvh: 1024 * d + 512 * dvh + 512],
                            start=(d == 0), stop=(d == DC - 1))
                    p.add("tensor", fn, waits=(w0 if d == 0 else []),
                          inc=("pm" if d == DC - 1 else None))
                wev = [("pm", now("pm"))]
                if g > 0:
                    wev.append(("mm", MM_END) if kind != 2 else ("uc", NSTEP * NPAIR * g))
                if kind == 0:
                    p.add("vector", lambda e, u=u, slot=slot: e.tensor_copy(
                        qa[:, 512 * u: 512 * u + 512], proj_psum(slot)),
                        waits=wev, inc="pe")
                elif kind == 1:
                    p.add("vector", lambda e, u=u, slot=slot: e.tensor_copy(
                        ka[:, 512 * u: 512 * u + 512], proj_psum(slot)),
                        waits=wev, inc="v2")
                    p.add("vector", lambda e, u=u, slot=slot: e.tensor_copy(
                        kb[64:128, 512 * u: 512 * u + 512], proj_psum(slot)[0:64, :]),
                        waits=[("v2", now("v2"))], inc="v2")
                    p.add("vector", lambda e, u=u, slot=slot: e.tensor_copy(
                        kb[0:64, 512 * u: 512 * u + 512], proj_psum(slot)[64:128, :]),
                        waits=[("v2", now("v2"))], inc="pe")
                else:
                    tc, dvh = u // 2, u % 2
                    p.add("vector", lambda e, tc=tc, dvh=dvh, slot=slot: e.tensor_copy(
                        va[:, :].rearrange("p (t k c) -> p t k c", t=4, k=16, c=65)[:, tc, 8 * dvh:8 * dvh + 8, 0:64],
                        proj_psum(slot).rearrange("p (k c) -> p k c", k=8)),
                        waits=wev, inc="pe")
                nproj += 1
        PEG = cnt["pe"]

        for pr in range(NPAIR):
            P = NPAIR * g + pr
            b = pr
            for k in range(NSTEP):
                gstep = NSTEP * P + k
                ck, tkb = k // 4, k % 4
                kt, j2 = ck // 2, ck % 2
                par = gstep % 2
                wts = []
                if pr == 0 and k <= 1:
                    wts.append(("pe", PEG))
                if gstep >= 2:
                    wts.append(("exq", gstep - 1))
                top = ka if j2 == 0 else kb
                bot = kb if j2 == 0 else ka
                p.add("tensor", lambda e, top=top, kt=kt, tkb=tkb, b=b, par=par: e.matmul(
                    sps[par][:, 0:512],
                    top[0:64, 512 * kt + 128 * tkb: 512 * kt + 128 * tkb + 128],
                    qa[0:64, 512 * b: 512 * b + 512],
                    start=True, stop=True, tile_position=(0, 0)),
                    waits=wts, inc=None)
                p.add("tensor", lambda e, bot=bot, kt=kt, tkb=tkb, b=b, par=par: e.matmul(
                    sps[par][:, 512:1024],
                    bot[64:128, 512 * kt + 128 * tkb: 512 * kt + 128 * tkb + 128],
                    qa[64:128, 512 * b: 512 * b + 512],
                    start=True, stop=True, tile_position=(64, 0)),
                    inc="mm")
                if k >= 1:
                    emit_u(gstep - 1, first=(k == 1), last=False)
                if k == 12 and pending_fin[0] is not None:
                    pending_fin[0]()
                    pending_fin[0] = None
                ws = [("mm", now("mm"))]
                if gstep >= 3:
                    ws.append(("uc", gstep - 2))
                p.add("scalar", lambda e, par=par, slot3=gstep % 3: e.activation(
                    exb[:, 1024 * slot3: 1024 * slot3 + 1024], sps[par][:, :], AF.Exp),
                    waits=ws, inc="exq")
            emit_u(NSTEP * P + NSTEP - 1, first=False, last=True)
            p.add("vector", lambda e: e.tensor_copy(usb[0:65, 0:512], pue[0:65, :]),
                  waits=[("uc", NSTEP * (P + 1)), ("v2", now("v2"))], inc="v2")
            p.add("vector", lambda e: e.tensor_copy(usb[0:65, 512:1024], puo[0:65, :]),
                  waits=[("v2", now("v2"))], inc="fin")
            p.add("vector", lambda e: e.reciprocal(rcp[0:1, 0:512], usb[64:65, 0:512]),
                  waits=[("fin", P + 1), ("bc", 2 * P)], inc="v2")
            p.add("vector", lambda e: e.reciprocal(rcp[0:1, 512:1024], usb[64:65, 512:1024]),
                  waits=[("v2", now("v2"))], inc="rc")

            def mk_finb(P=P, b=b, gc=gc):
                def finb():
                    p.add("tensor", lambda e: e.matmul(
                        pun[:, :], onesrf[:, :], rcp[0:1, 0:512], start=True, stop=True),
                        waits=[("rc", P + 1), ("yd", P)], inc="bc")
                    p.add("vector", lambda e: e.tensor_tensor(
                        t1s[0:64, :], usb[0:64, 0:512], pun[0:64, :], AO.mult),
                        waits=[("bc", now("bc")), ("v2", now("v2"))], inc="v2")
                    p.add("vector", lambda e: e.tensor_tensor(
                        xl[0:64, 2048 * b + gc: 2048 * b + gc + 512],
                        t1s[0:64, :], xl[0:64, 2048 * b + gc: 2048 * b + gc + 512], AO.add),
                        waits=[("v2", now("v2"))], inc="v2")
                    p.add("tensor", lambda e: e.matmul(
                        pun[:, :], onesrf[:, :], rcp[0:1, 512:1024], start=True, stop=True),
                        waits=[("v2", now("v2"))], inc="bc")
                    p.add("vector", lambda e: e.tensor_copy(t1s[64:128, :], usb[0:64, 512:1024]),
                          waits=[("bc", now("bc")), ("v2", now("v2"))], inc="v2")
                    p.add("vector", lambda e: e.tensor_tensor(
                        t1s[64:128, :], t1s[64:128, :], pun[64:128, :], AO.mult),
                        waits=[("v2", now("v2"))], inc="v2")
                    p.add("vector", lambda e, b=b, gc=gc: e.tensor_tensor(
                        xl[64:128, 2048 * b + gc: 2048 * b + gc + 512],
                        t1s[64:128, :], xl[64:128, 2048 * b + gc: 2048 * b + gc + 512], AO.add),
                        waits=[("v2", now("v2"))], inc="yd")
                return finb
            if P == NG * NPAIR - 1:
                mk_finb()()
            else:
                pending_fin[0] = mk_finb()
        MM_END = cnt["mm"]
    p.emit()
    esA2.close()

    if stage == 2:
        emit_dump(xf, xl)
        esA.close(); es.close()
        return nc

    esA3 = ExitStack()
    wcf = sb(esA3, "wcf", [128, 2 * 2048], F32)
    wcb = sb(esA3, "wcb", [128, 2 * 2048], BF16)
    tribs = sb(esA3, "tribs", [16, 2048], BF16)
    tsb8c = sb(esA3, "tsb8c", [8, 512], F32)
    tex = sb(esA3, "tex", [1, 512], F32)
    zz = sb(esA3, "zz", [128, 289], F32)

    p = Plan()
    p.add("sync", lambda e: e.dma_start(out=tribs[:], in_=trib_d.ap()[:, :]), inc="st", k=16)
    TRIBS_ST = cnt["st"]
    p.add("gpsimd", lambda e: e.memset(zz[:], 0.0), inc="gp")
    NGPZ = cnt["gp"]
    p.add("vector", lambda e: e.reduce_sum(sc2[:, 0:1], xl[:, :], axis=AX.X), inc="ve")
    p.add("scalar", lambda e: e.activation(stgb[:, 0:1], xl[:, 0:1], AF.Square,
                                           accum_out=sc2[:, 1:2]), inc="se")
    p.ev["scalar"].pop(); cnt["se"] -= 1
    for chx in range(16):
        p.add("scalar", lambda e, chx=chx: e.activation(
            stgb[:, :], xl[:, 1024 * chx: 1024 * chx + 1024], AF.Square,
            accum_out=rt[:, chx:chx + 1]), inc="se")
    p.add("vector", lambda e: e.reduce_sum(sc2[:, 1:2], rt[:, 0:16], axis=AX.X),
          waits=[("se", now("se"))], inc="ve")
    p.add("tensor", lambda e: e.matmul(pue[0:1, 0:2], onescf[:, :], sc2[:, :],
                                       start=True, stop=True),
          waits=[("ve", now("ve"))], inc="te")
    p.add("vector", lambda e: e.tensor_copy(sg1[0:1, 0:2], pue[0:1, 0:2]),
          waits=[("te", now("te"))], inc="ve")
    p.add("sync", lambda e: e.dma_start(out=st_in.ap()[:, :], in_=sg1[0:1, 0:2]),
          waits=[("ve", now("ve"))], inc="st", k=16)
    p.add("gpsimd", lambda e: e.collective_compute(
        "AllGather", mybir.AluOpType.bypass, replica_groups=[list(range(NCORE))],
        ins=[st_in.ap().opt()], outs=[st_out.ap().opt()]),
        waits=[("st", now("st"))], inc="cc")
    p.add("sync", lambda e: e.dma_start(out=ssb[:, :], in_=st_out.ap()[:, :]),
          waits=[("cc", now("cc"))], inc="st", k=16)
    p.add("tensor", lambda e: e.matmul(pue[0:1, 0:2], bms[:, :], ssb[:, :],
                                       start=True, stop=True),
          waits=[("st", now("st"))], inc="te")
    p.add("vector", lambda e: e.tensor_single_scalar(sg1[0:1, 2:4], pue[0:1, 0:2],
                                              1.0 / GN_N, AO.mult),
          waits=[("te", now("te"))], inc="ve")
    p.add("scalar", lambda e: e.activation(sg1[0:1, 4:5], sg1[0:1, 2:3], AF.Square),
          waits=[("ve", now("ve"))], inc="se")
    p.add("vector", lambda e: e.tensor_tensor(sg1[0:1, 5:6], sg1[0:1, 3:4],
                                              sg1[0:1, 4:5], AO.subtract),
          waits=[("se", now("se"))], inc="ve")
    p.add("scalar", lambda e: e.activation(sg1[0:1, 6:7], sg1[0:1, 5:6], AF.Sqrt,
                                           bias=epsv[0:1, 0:1]),
          waits=[("ve", now("ve"))], inc="se")
    p.add("vector", lambda e: e.reciprocal(sg1[0:1, 8:9], sg1[0:1, 6:7]),
          waits=[("se", now("se"))], inc="ve")
    p.add("vector", lambda e: e.tensor_tensor(sg1[0:1, 9:10], sg1[0:1, 2:3],
                                              sg1[0:1, 8:9], AO.mult),
          waits=[("ve", now("ve"))], inc="ve")
    p.add("tensor", lambda e: e.matmul(pue[:, 0:2], onesrf[:, :], sg1[0:1, 8:10],
                                       start=True, stop=True),
          waits=[("ve", now("ve"))], inc="te")
    p.add("vector", lambda e: e.tensor_copy(rsm[:, :], pue[:, 0:2]),
          waits=[("te", now("te"))], inc="ve")
    for d in range(DC):
        p.add("vector", lambda e, d=d: e.scalar_tensor_tensor(
            xfc(d, 0, 2048), xlc(d, 0, 2048), rsm[:, 0:1], xfc(d, 0, 2048),
            AO.mult, AO.add),
            waits=[("ve", now("ve"))], inc="ve")
        p.add("vector", lambda e, d=d: e.tensor_single_scalar(xfc(d, 0, 2048), xfc(d, 0, 2048), rsm[:, 1:2], AO.subtract),
            waits=[("ve", now("ve"))], inc="ve")
    emit_ln(p)
    VE_LN2 = cnt["ve"]

    for i in range(NT):
        for d4 in range(2):
            for j in range(4):
                dd = 4 * d4 + j
                w = []
                if j == 0:
                    w = [("ve", now("ve"))]
                    if i > 0 or d4 > 0:
                        w.append(("st", now("st")))
                p.add("tensor", lambda e, d4=d4, j=j, dd=dd, i=i: e.transpose(
                    sps[1][:, 512 * d4 + 128 * j: 512 * d4 + 128 * j + 128],
                    xf[:, 2048 * dd + 128 * i: 2048 * dd + 128 * i + 128], idfs[:, :]),
                    waits=w, inc=("te" if j == 3 else None))
            p.add("vector", lambda e, d4=d4: e.tensor_copy(
                stg[:, 512 * d4: 512 * d4 + 512], sps[1][:, 512 * d4: 512 * d4 + 512]),
                waits=[("te", now("te"))], inc="ve")
        p.add("sync", lambda e, i=i: e.dma_start(
            out=xspill.ap()[128 * i: 128 * i + 128, :], in_=stg[:, :]),
            waits=[("ve", now("ve"))], inc="st", k=16)
        for d4 in range(2):
            for j in range(4):
                dd = 4 * d4 + j
                w = []
                if j == 0:
                    w = [("ve", now("ve"))]
                    if i > 0 or d4 > 0:
                        w.append(("g2", now("g2")))
                p.add("tensor", lambda e, dd=dd, j=j, i=i: e.transpose(
                    pbf[:, 128 * j: 128 * j + 128],
                    xl[:, 2048 * dd + 128 * i: 2048 * dd + 128 * i + 128], idbs[:, :]),
                    waits=w, inc=("te" if j == 3 else None))
            p.add("vector", lambda e, d4=d4: e.tensor_copy(
                stgb[:, 512 * d4: 512 * d4 + 512], pbf[:, :]),
                waits=[("te", now("te"))], inc="ve")
        p.add("scalar", lambda e, i=i: e.dma_start(
            out=xt_in.ap()[128 * i: 128 * i + 128, :], in_=stgb[:, :]),
            waits=[("ve", now("ve"))], inc="g2", k=16)

    G2_PUB = cnt["g2"]

    p.add("gpsimd", lambda e: e.dma_start(
        out=t_in.ap().rearrange("(p f) -> p f", p=128), in_=zz[:, :]),
        waits=[("gp", NGPZ)], inc="g2", k=16)
    G2_TZ = cnt["g2"]

    for i in range(NT):
        for d in range(DC):
            w = [("ve", VE_LN2)] if d == 0 else []
            p.add("tensor", lambda e, d=d, i=i: e.matmul(
                puo[:, 8 * i: 8 * i + 8],
                xl[:, 2048 * d + 128 * i: 2048 * d + 128 * i + 128],
                wgb[:, 8 * d: 8 * d + 8],
                start=(d == 0), stop=(d == DC - 1)),
                waits=w, inc=("te" if d == DC - 1 else None))
    TE_LOG = cnt["te"]
    def b3d(t):
        return t[:, :].rearrange("p (t e) -> p t e", e=8)
    def bc16(t, c0):
        return bass.AP(t, c0, [[16, 128], [1, 16], [0, 8]])
    p.add("scalar", lambda e: e.activation(rt[:, :], puo[:, 0:128], AF.Exp),
          waits=[("te", TE_LOG)], inc="se")
    p.add("vector", lambda e: e.reduce_max(r1[:, :], b3d(rt), axis=AX.X),
          waits=[("se", now("se")), ("v2", now("v2"))], inc="v2")
    p.add("vector", lambda e: e.tensor_tensor(b3d(re1), b3d(rt), bc16(r1, 0), AO.is_equal),
          waits=[("v2", now("v2"))], inc="v2")
    p.add("vector", lambda e: e.tensor_tensor(re2[:, :], re1[:, :], rt[:, :], AO.mult),
          waits=[("v2", now("v2"))], inc="v2")
    p.add("vector", lambda e: e.tensor_tensor(re2[:, :], rt[:, :], re2[:, :], AO.subtract),
          waits=[("v2", now("v2"))], inc="v2")
    p.add("vector", lambda e: e.reduce_max(r2[:, :], b3d(re2), axis=AX.X),
          waits=[("v2", now("v2"))], inc="v2")
    p.add("vector", lambda e: e.tensor_tensor(b3d(re2), b3d(re2), bc16(r2, 0), AO.is_equal),
          waits=[("v2", now("v2"))], inc="v2")
    p.add("vector", lambda e: e.tensor_tensor(rg1[:, :], r1[:, :], r2[:, :], AO.add),
          waits=[("v2", now("v2"))], inc="v2")
    p.add("vector", lambda e: e.reciprocal(rg2[:, :], rg1[:, :]),
          waits=[("v2", now("v2"))], inc="v2")
    p.add("vector", lambda e: e.tensor_tensor(rg1[:, :], r1[:, :], rg2[:, :], AO.mult),
          waits=[("v2", now("v2"))], inc="v2")
    p.add("vector", lambda e: e.tensor_tensor(rg2[:, :], r2[:, :], rg2[:, :], AO.mult),
          waits=[("v2", now("v2"))], inc="v2")
    p.add("vector", lambda e: e.tensor_tensor(roh[:, :], re1[:, :], re2[:, :], AO.add),
          waits=[("v2", now("v2"))], inc="v2")
    V2_OH = cnt["v2"]
    for i in range(NT):
        p.add("tensor", lambda e, i=i: e.matmul(
            pue[:, 8 * i: 8 * i + 8], tris[:, :], roh[:, 8 * i: 8 * i + 8],
            start=(i == 0), stop=False),
            waits=[("v2", V2_OH)] + ([("ve", VE_LN2)] if i == 0 else []), inc=None)
        p.add("tensor", lambda e, i=i: e.matmul(
            sps[0][0:1, 8 * i: 8 * i + 8], onesc[:, :], roh[:, 8 * i: 8 * i + 8],
            start=True, stop=True),
            inc="te")
    TE_TOT = cnt["te"]
    p.add("vector", lambda e: e.tensor_copy(tex[0:1, 0:128], sps[0][0:1, 0:128]),
          waits=[("te", TE_TOT)], inc="ve")
    p.add("sync", lambda e: e.dma_start(
        out=tex_d.ap()[0:128].rearrange("(a f) -> a f", a=1), in_=tex[0:1, 0:128]),
        waits=[("ve", now("ve"))], inc="st", k=16)
    p.add("sync", lambda e: e.dma_start(
        out=tot16[:, :], in_=tex_d.ap()[0:128].rearrange("(i c) -> i c", i=16)),
        waits=[("st", now("st"))], inc="st", k=16)
    p.add("vector", lambda e: e.tensor_copy(tot16b[:, :], tot16[:, :]),
          waits=[("st", now("st"))], inc="ve")
    p.add("tensor", lambda e: e.matmul(sps[0][0:1, 128:136], onescf[0:16, :], tot16[:, :],
                                       start=True, stop=True),
          waits=[("ve", now("ve"))], inc="te")
    p.add("vector", lambda e: e.tensor_copy(sg1[0:1, 0:8], sps[0][0:1, 128:136]),
          waits=[("te", now("te"))], inc="ve")
    p.add("sync", lambda e: e.dma_start(out=cnt_in.ap()[:, :], in_=sg1[0:1, 0:8]),
          waits=[("ve", now("ve"))], inc="st", k=16)
    p.add("gpsimd", lambda e: e.collective_compute(
        "AllGather", mybir.AluOpType.bypass, replica_groups=[list(range(NCORE))],
        ins=[cnt_in.ap().opt()], outs=[cnt_out.ap().opt()]),
        waits=[("st", now("st"))], inc="cc")
    p.add("sync", lambda e: e.dma_start(out=cnts[:, :], in_=cnt_out.ap()[:, :]),
          waits=[("cc", now("cc"))], inc="st", k=16)
    p.add("vector", lambda e: e.tensor_copy(cntsb[:, :], cnts[:, :]),
          waits=[("st", now("st"))], inc="ve")
    VE_CNTS = cnt["ve"]
    for i in range(NT):
        p.add("tensor", lambda e, i=i: e.matmul(
            pue[:, 8 * i: 8 * i + 8], tribs[:, 128 * i: 128 * i + 128], tot16b[:, :],
            start=False, stop=False),
            waits=([("ve", VE_CNTS), ("st", TRIBS_ST)] if i == 0 else []), inc=None)
        p.add("tensor", lambda e, i=i: e.matmul(
            pue[:, 8 * i: 8 * i + 8], pms[:, :], cntsb[:, :],
            start=False, stop=True),
            inc="te")
    TE_CFULL = cnt["te"]
    for j, (eq, pcol, ecol) in enumerate([(re1, 0, 32), (re2, 16, 48)]):
        p.add("vector", lambda e, eq=eq: e.tensor_tensor(rt[:, :], eq[:, :], pue[:, 0:128], AO.mult),
              waits=[("te", TE_CFULL), ("v2", now("v2"))], inc="v2")
        p.add("vector", lambda e, pcol=pcol: e.reduce_sum(
            posj[:, pcol:pcol + 16], b3d(rt), axis=AX.X),
            waits=[("v2", now("v2"))], inc="v2")
        p.add("vector", lambda e, eq=eq: e.tensor_tensor(rt[:, :], eq[:, :], io8[:, :], AO.mult),
              waits=[("v2", now("v2"))], inc="v2")
        p.add("vector", lambda e, ecol=ecol: e.reduce_sum(
            posj[:, ecol:ecol + 16], b3d(rt), axis=AX.X),
            waits=[("v2", now("v2"))], inc="v2")
    for j, (pcol, ecol, gcol, garbT) in enumerate([(0, 32, rg1, garbA), (16, 48, rg2, garbB)]):
        p.add("vector", lambda e, pcol=pcol: e.tensor_single_scalar(
            re1[:, 0:16], posj[:, pcol:pcol + 16], float(CAP), AO.is_lt),
            waits=[("v2", now("v2"))], inc="v2")
        p.add("vector", lambda e, gcol=gcol, j=j: e.tensor_tensor(
            cfs[:, 16 * j: 16 * j + 16], gcol[:, :], re1[:, 0:16], AO.mult),
            waits=[("v2", now("v2"))], inc="v2")
        p.add("vector", lambda e, pcol=pcol, ecol=ecol: e.scalar_tensor_tensor(
            re1[:, 16:32], posj[:, ecol:ecol + 16], float(CAPPAD), posj[:, pcol:pcol + 16],
            AO.mult, AO.add),
            waits=[("v2", now("v2"))], inc="v2")
        p.add("vector", lambda e, pcol=pcol: e.tensor_single_scalar(
            re2[:, 64:80], posj[:, pcol:pcol + 16], 2048.0, AO.is_lt),
            waits=[("v2", now("v2"))], inc="v2")
        p.add("vector", lambda e: e.tensor_tensor(
            re2[:, 64:80], re2[:, 64:80], re1[:, 0:16], AO.mult),
            waits=[("v2", now("v2"))], inc="v2")
        p.add("vector", lambda e, pcol=pcol, ecol=ecol: e.scalar_tensor_tensor(
            re2[:, 80:96], posj[:, ecol:ecol + 16], 2048.0, posj[:, pcol:pcol + 16],
            AO.mult, AO.add),
            waits=[("v2", now("v2"))], inc="v2")
        p.add("vector", lambda e: e.tensor_tensor(
            re2[:, 80:96], re2[:, 80:96], re2[:, 64:80], AO.mult),
            waits=[("v2", now("v2"))], inc="v2")
        p.add("vector", lambda e, j=j: e.tensor_copy(
            sidg[:, 16 * j: 16 * j + 16], re2[:, 80:96]),
            waits=[("v2", now("v2"))], inc="v2")
        for (kk, b0, b1v, ll) in [(1, 2048.0, 4096.0, 2048.0), (2, 4096.0, None, 512.0)]:
            p.add("vector", lambda e, pcol=pcol, b0=b0: e.tensor_single_scalar(
                re2[:, 64:80], posj[:, pcol:pcol + 16], b0, AO.is_ge),
                waits=[("v2", now("v2"))], inc="v2")
            if b1v is not None:
                p.add("vector", lambda e, pcol=pcol, b1v=b1v: e.tensor_single_scalar(
                    re2[:, 96:112], posj[:, pcol:pcol + 16], b1v, AO.is_lt),
                    waits=[("v2", now("v2"))], inc="v2")
                p.add("vector", lambda e: e.tensor_tensor(
                    re2[:, 64:80], re2[:, 64:80], re2[:, 96:112], AO.mult),
                    waits=[("v2", now("v2"))], inc="v2")
            p.add("vector", lambda e: e.tensor_tensor(
                re2[:, 64:80], re2[:, 64:80], re1[:, 0:16], AO.mult),
                waits=[("v2", now("v2"))], inc="v2")
            p.add("vector", lambda e, pcol=pcol, b0=b0: e.tensor_single_scalar(
                re2[:, 80:96], posj[:, pcol:pcol + 16], b0, AO.subtract),
                waits=[("v2", now("v2"))], inc="v2")
            p.add("vector", lambda e, ecol=ecol, ll=ll: e.scalar_tensor_tensor(
                re2[:, 80:96], posj[:, ecol:ecol + 16], ll, re2[:, 80:96],
                AO.mult, AO.add),
                waits=[("v2", now("v2"))], inc="v2")
            p.add("vector", lambda e: e.tensor_single_scalar(
                re2[:, 80:96], re2[:, 80:96], 65536.0, AO.subtract),
                waits=[("v2", now("v2"))], inc="v2")
            p.add("vector", lambda e: e.tensor_tensor(
                re2[:, 80:96], re2[:, 80:96], re2[:, 64:80], AO.mult),
                waits=[("v2", now("v2"))], inc="v2")
            p.add("vector", lambda e: e.tensor_single_scalar(
                re2[:, 80:96], re2[:, 80:96], 65536.0, AO.add),
                waits=[("v2", now("v2"))], inc="v2")
            p.add("vector", lambda e, j=j, kk=kk: e.tensor_copy(
                sidg[:, 32 * kk + 16 * j: 32 * kk + 16 * j + 16], re2[:, 80:96]),
                waits=[("v2", now("v2"))], inc="v2")
        p.add("vector", lambda e, ecol=ecol: e.tensor_single_scalar(
            re1[:, 48:64], posj[:, ecol:ecol + 16], 4.0, AO.is_ge),
            waits=[("v2", now("v2"))], inc="v2")
        p.add("vector", lambda e: e.scalar_tensor_tensor(
            re1[:, 16:32], re1[:, 48:64], 64.0, re1[:, 16:32], AO.mult, AO.add),
            waits=[("v2", now("v2"))], inc="v2")
        p.add("vector", lambda e, garbT=garbT: e.tensor_tensor(
            re1[:, 16:32], re1[:, 16:32], garbT[:, :], AO.subtract),
            waits=[("v2", now("v2"))], inc="v2")
        p.add("vector", lambda e: e.tensor_tensor(
            re1[:, 16:32], re1[:, 16:32], re1[:, 0:16], AO.mult),
            waits=[("v2", now("v2"))], inc="v2")
        p.add("vector", lambda e, garbT=garbT: e.tensor_tensor(
            re1[:, 16:32], re1[:, 16:32], garbT[:, :], AO.add),
            waits=[("v2", now("v2"))], inc="v2")
        p.add("vector", lambda e, j=j: e.tensor_copy(
            sidu[:, 16 * j: 16 * j + 16], re1[:, 16:32]),
            waits=[("v2", now("v2"))], inc="v2")
    V2S = cnt["v2"]
    for i in range(NT):
        for j in range(2):
            p.add("gpsimd", lambda e, i=i, j=j: e.indirect_dma_start(
                out=t_in.ap()[:, None],
                out_offset=IOOA(ap=sidu[:, 16 * j + i: 16 * j + i + 1], axis=0),
                in_=tkr[:, i:i + 1], in_offset=None, compute_op=AO.add),
                waits=[("v2", V2S), ("g2", G2_TZ)], inc="g2", k=16)
    p.add("gpsimd", lambda e: e.collective_compute(
        "AllReduce", mybir.AluOpType.add, replica_groups=[list(range(NCORE))],
        ins=[t_in.ap().opt()], outs=[t_out.ap().opt()]),
        waits=[("g2", now("g2"))], inc="cc")
    CC_T = cnt["cc"]
    p.add("gpsimd", lambda e: e.collective_compute(
        "AllGather", mybir.AluOpType.bypass, replica_groups=[list(range(NCORE))],
        ins=[xt_in.ap().opt()], outs=[xt_out.ap().opt()]),
        waits=[("g2", G2_PUB)], inc="cc")
    CC_XT = cnt["cc"]
    NWCH = 32
    WSEM = [("wc", "ws"), ("wc2", "ws2")]

    def wsrc_ap(kk):
        if kk < 16:
            r, h = kk // 2, kk % 2
            return w1_d.ap()[128 * r: 128 * r + 128, 2048 * h: 2048 * h + 2048]
        m = kk - 16
        return w2_d.ap()[256 * m: 256 * m + 256, :].rearrange("(a p) d -> p a d", a=2)

    def wdst(kk):
        return (w1s, 2048 * kk) if kk < 16 else (w2s, 2048 * (kk - 16))

    def wload(kk):
        sl = kk % 2
        outap = wcf[:, 2048 * sl: 2048 * sl + 2048]
        if kk >= 16:
            outap = outap.rearrange("p (a d) -> p a d", a=2)
        p.add("gpsimd", lambda e, kk=kk, outap=outap: e.dma_start(
            out=outap, in_=wsrc_ap(kk)),
            waits=([("wv", kk - 1)] if kk >= 2 else []), inc=WSEM[sl][0], k=16)

    wload(0)
    wload(1)
    for kk in range(NWCH):
        sl = kk % 2
        dT, do = wdst(kk)
        p.add("gpsimd", lambda e, dT=dT, do=do, sl=sl: e.dma_start(
            out=dT.ap()[:, do: do + 2048], in_=wcb[:, 2048 * sl: 2048 * sl + 2048]),
            waits=[("wv", kk + 1)], inc=WSEM[sl][1], k=16)
        if kk + 2 < NWCH:
            wload(kk + 2)

    def emit_wcast(kks):
        for kk in kks:
            sl = kk % 2
            w = [(WSEM[sl][0], 16 * (kk // 2 + 1))]
            if kk >= 2:
                w.append((WSEM[sl][1], 16 * ((kk - 2) // 2 + 1)))
            p.add("vector", lambda e, sl=sl: e.tensor_copy(
                wcb[:, 2048 * sl: 2048 * sl + 2048], wcf[:, 2048 * sl: 2048 * sl + 2048]),
                waits=w, inc="wv")

    te_prev_mm = now("te")
    for c in range(9):
        if c < 8:
            emit_wcast(range(4 * c, 4 * c + 4))
        p.add("sync", lambda e, c=c: e.dma_start(
            out=tsb8c[0:4, :],
            in_=bass.AP(t_out, 512 * c, [[CAPPAD, 4], [1, 512]])),
            waits=[("cc", CC_T)] + ([("te", te_prev_mm)] if c > 0 else []), inc="st", k=16)
        p.add("sync", lambda e, c=c: e.dma_start(
            out=tsb8c[4:8, :],
            in_=bass.AP(t_out, THALF + 512 * c, [[CAPPAD, 4], [1, 512]])),
            inc="st", k=16)
        p.add("tensor", lambda e, c=c: e.matmul(
            sps[1][0:1, 0:512],
            eohs[:, :], tsb8c[:, :], start=True, stop=True),
            waits=[("st", now("st"))] + ([("ve", now("ve"))] if c > 0 else []), inc="te")
        te_prev_mm = now("te")
        p.add("vector", lambda e: e.tensor_single_scalar(tex[0:1, 0:512], sps[1][0:1, 0:512], float(TOK * NCORE - 1), AO.min),
            waits=[("te", now("te"))], inc="ve")
        p.add("sync", lambda e, c=c: e.dma_start(
            out=tex_d.ap()[512 * c: 512 * c + 512].rearrange("(a f) -> a f", a=1),
            in_=tex[0:1, 0:512]),
            waits=[("ve", now("ve"))], inc="st", k=16)
    p.add("sync", lambda e: e.dma_start(
        out=xidxf[:, :], in_=tex_d.ap()[:].rearrange("(s p) -> p s", p=128),
        allow_slow_non_contiguous=True),
        waits=[("st", now("st"))], inc="st", k=16)
    p.add("vector", lambda e: e.tensor_copy(xidxu[:, :], xidxf[:, :]),
          waits=[("st", now("st"))], inc="ve")
    p.emit()

    if stage == 3:
        emit_dump(xf, xl)
        pd = Plan()
        pd.add("sync", lambda e: e.dma_start(out=out_d.ap()[0:128, 0:36], in_=xidxf[:, :]),
               inc="st", k=16)
        pd.add("sync", lambda e: e.dma_start(out=out_d.ap()[128:256, 0:64], in_=posj[:, :]),
               inc="st", k=16)
        pd.add("sync", lambda e: e.dma_start(out=out_d.ap()[256:384, 0:32], in_=cfs[:, :]),
               inc="st", k=16)
        pd.add("sync", lambda e: e.dma_start(out=out_d.ap()[384:512, 0:16].rearrange("p c -> p c"), in_=tot16[:, :].rearrange("a b -> a b") if False else cnts[:, :]),
               inc="st", k=16) if False else None
        pd.add("sync", lambda e: e.dma_start(out=out_d.ap()[384:392, 0:8], in_=cnts[:, :]),
               inc="st", k=16)
        pd.add("sync", lambda e: e.dma_start(
            out=stg[:, 0:36], in_=bass.AP(t_in, 0, [[36, 128], [1, 36]])),
            inc="st", k=16)
        pd.add("sync", lambda e: e.dma_start(out=out_d.ap()[512:640, 0:36], in_=stg[:, 0:36]),
               waits=[("st", now("st"))], inc="st", k=16)
        pd.add("sync", lambda e: e.dma_start(
            out=stg[:, 0:36], in_=bass.AP(t_out, 0, [[36, 128], [1, 36]])),
            waits=[("st", now("st"))], inc="st", k=16)
        pd.add("sync", lambda e: e.dma_start(out=out_d.ap()[640:768, 0:36], in_=stg[:, 0:36]),
               waits=[("st", now("st"))], inc="st", k=16)
        pd.add("sync", lambda e: e.dma_start(
            out=stg[:, 0:36], in_=bass.AP(t_out, THALF, [[36, 128], [1, 36]])),
            waits=[("st", now("st"))], inc="st", k=16)
        pd.add("sync", lambda e: e.dma_start(out=out_d.ap()[768:896, 0:36], in_=stg[:, 0:36]),
               waits=[("st", now("st"))], inc="st", k=16)
        pd.add("vector", lambda e: e.tensor_copy(rt[:, 0:32], sidu[:, :]),
               waits=[("st", now("st"))], inc="ve")
        pd.add("sync", lambda e: e.dma_start(out=out_d.ap()[896:1024, 0:32], in_=rt[:, 0:32]),
               waits=[("ve", now("ve"))], inc="st", k=16)
        pd.add("vector", lambda e: e.tensor_copy(rt[:, 32:64], sidg[:, :]),
               waits=[("st", now("st"))], inc="ve")
        pd.add("sync", lambda e: e.dma_start(out=out_d.ap()[896:1024, 32:64], in_=rt[:, 32:64]),
               waits=[("ve", now("ve"))], inc="st", k=16)
        pd.add("sync", lambda e: e.drain(), waits=[("st", now("st"))])
        pd.emit()
        esA3.close(); esA.close(); es.close()
        return nc

    esA3.close()
    esA.close()
    esB = ExitStack()
    b1c = sb(esB, "b1c", [128, 32], F32)
    b2b = sb(esB, "b2b", [1, D], BF16)
    w1b = sb(esB, "w1b", [128, DC * HID], BF16)
    w2b = sb(esB, "w2b", [128, 32 * D], BF16)
    xg = sb(esB, "xg", [128, 6 * D], BF16)
    xfm = sb(esB, "xfm", [128, 2 * DC * 512], BF16)
    hsb = sb(esB, "hsb", [128, 32 * 512], BF16)
    ysb = sb(esB, "ysb", [128, 2048], BF16)
    gbuf = sb(esB, "gbuf", [128, 2 * D], BF16)

    p = Plan()
    BND = {}
    def _mkregs(e):
        BND[2] = e.alloc_register("bnd2")
        e.reg_mov(BND[2], NCORE * 2048 - 1)
        BND[3] = e.alloc_register("bnd3")
        return e.reg_mov(BND[3], NCORE * 512 - 1)
    p.add("gpsimd", _mkregs)
    if stage == 38:
        p.add("gpsimd", lambda e: e.engine_nop())
        p.emit()
        esB.close(); es.close()
        return nc
    p.add("sync", lambda e: e.dma_start(
        out=stg[0:32, 0:128],
        in_=b1_d.ap()[0:1, :].rearrange("a (h p) -> (a h) p", h=32)),
        inc="wc3", k=16)
    B1ROW_WC = cnt["wc3"]
    p.add("tensor", lambda e: e.transpose(pun[:, 0:32], stg[0:32, 0:128], idfs[0:32, 0:32]),
          waits=[("wc3", B1ROW_WC)], inc="te")
    B1TR = cnt["te"]
    p.add("vector", lambda e: e.tensor_copy(b1c[:, :], pun[:, 0:32]),
          waits=[("te", B1TR)], inc="v3")
    B1CV = cnt["v3"]
    W1ST = {}
    for cch in range(4):
        for r in range(8):
            off = 4096 * r + 1024 * cch
            p.add("sync", lambda e, off=off: e.dma_start(
                out=w1b[:, off: off + 1024], in_=w1s.ap()[:, off: off + 1024]),
                waits=([("ws", 16 * 16), ("ws2", 16 * 16)] if (cch == 0 and r == 0) else []), inc="st", k=16)
        W1ST[cch] = cnt["st"]
    p.add("sync", lambda e: e.dma_start(out=stg[0:1, :], in_=b2_d.ap()[:, :]),
          waits=[("te", B1TR)], inc="ws", k=16)
    p.add("vector", lambda e: e.tensor_copy(b2b[:, :], stg[0:1, :]),
          waits=[("ws", now("ws"))], inc="v3")
    VW = now("v3")
    p.add("sync", lambda e: e.dma_start(out=w2b[:, :], in_=w2s.ap()[:, :]),
          inc="st", k=16)
    W2ST = cnt["st"]
    if stage == 39:
        p.add("gpsimd", lambda e: e.engine_nop(), waits=[("v3", VW)])
        p.emit()
        esB.close(); es.close()
        return nc

    S3B = cnt["s3"]

    def emit_ag(idx):
        off, ln, blk = [(0, 2048, 4), (2048 * D, 2048, 8), (4096 * D, 512, 9)][idx - 1]
        p.add("gpsimd", lambda e, off=off, ln=ln: e.collective_compute(
            "AllGather", mybir.AluOpType.bypass, replica_groups=[list(range(NCORE))],
            ins=[bass.AP(y_in, off, [[1, ln * D]])],
            outs=[[yo1, yo2, yo3][idx - 1].ap().opt()]),
            waits=[("s3", S3B + 32 * blk)], inc="cc")
        return cnt["cc"]

    bcntA = {}

    def emit_passA(pairs):
        for ii, jj in pairs:
            sl = (2 * ii + jj) % 2
            w = [("cc", CC_Y1)]
            if 2 * ii + jj >= 2:
                w.append(("g3", bcntA[2 * ii + jj - 2]))
            p.add("gpsimd", lambda e, ii=ii, jj=jj, sl=sl: e.indirect_dma_start(
                out=gbuf[:, 1024 * sl: 1024 * sl + 1024], out_offset=None,
                in_=yo1.ap()[:, :],
                in_offset=IOOA(ap=sidg[:, 16 * jj + ii: 16 * jj + ii + 1], axis=0)),
                waits=w, inc="g3", k=16)
            p.add("gpsimd", lambda e, ii=ii, jj=jj, sl=sl: e.dma_start(
                out=yab.ap()[256 * ii + 128 * jj: 256 * ii + 128 * jj + 128, :],
                in_=gbuf[:, 1024 * sl: 1024 * sl + 1024]),
                waits=[("g3", now("g3"))], inc="g3", k=16)
            bcntA[2 * ii + jj] = cnt["g3"]

    PAIRS = [(ii, jj) for ii in range(NT) for jj in range(2)]

    def xg_slice(st_):
        return xg[:, 1024 * (st_ % 6): 1024 * (st_ % 6) + 1024]

    import os as _os
    _ng40 = int(_os.environ.get("NG40", NST))
    XG3 = {}
    for st_ in range(NST if stage != 40 else _ng40):
        w = [("cc", CC_XT), ("ve", now("ve"))]
        if st_ >= 6 and stage != 40:
            w.append(("t3", 2 * (st_ - 5)))
        p.add("gpsimd", lambda e, st_=st_: e.indirect_dma_start(
            out=xg_slice(st_), out_offset=None,
            in_=xt_out.ap()[:, :], in_offset=IOOA(ap=xidxu[:, st_:st_ + 1], axis=0)),
            waits=w, inc="g3", k=16)
        XG3[st_] = cnt["g3"]
        if stage != 40:
            if st_ == 21:
                CC_Y1 = emit_ag(1)
            elif st_ == 25:
                emit_passA(PAIRS[0:16])
            elif st_ == 29:
                emit_passA(PAIRS[16:32])
            elif st_ == 35:
                CC_Y2 = emit_ag(2)
                CC_Y3 = emit_ag(3)

    if stage == 40:
        p.add("gpsimd", lambda e: e.engine_nop(), waits=[("g3", now("g3"))])
        p.emit()
        esB.close(); es.close()
        return nc
    se_gelu = {}
    t2_mm2 = {}
    for sbk in range(9):
        rot = sbk % 2
        for sti in range(4):
            st_ = 4 * sbk + sti
            for d4 in range(2):
                for j in range(4):
                    dd = 4 * d4 + j
                    w = []
                    if j == 0:
                        w = [("g3", XG3[st_])]
                        if sbk > 0 or sti > 0 or d4 > 0:
                            w.append(("v3", now("v3")))
                    p.add("tensor", lambda e, st_=st_, dd=dd, j=j: e.transpose(
                        pbf[:, 128 * j: 128 * j + 128],
                        xg_slice(st_)[:, 128 * dd: 128 * dd + 128], idbs[:, :]),
                        waits=w, inc=("t3" if j == 3 else None))
                wxe = [("t3", now("t3"))]
                if sbk >= 2 and sti == 0 and d4 == 0 and (sbk - 2, "mm1") in t2_mm2:
                    wxe.append(("t2", t2_mm2[(sbk - 2, "mm1")]))
                p.add("vector", lambda e, rot=rot, sti=sti, d4=d4: e.tensor_copy(
                    xfm[:, :].rearrange("p (r d c) -> p r d c", r=2, d=DC)[:, rot, 4 * d4:4 * d4 + 4, 128 * sti:128 * sti + 128],
                    pbf[:, :].rearrange("p (a c) -> p a c", a=4)),
                    waits=wxe, inc="v3")
        XP = now("v3")
        if stage == 41:
            continue
        for hc in range(32):
            hh = hc % 2
            for d in range(DC):
                w = []
                if d == 0:
                    w = [("v3", XP)]
                    if hc >= 2:
                        w.append(("se", se_gelu[(sbk, hc - 2)]))
                    elif sbk > 0:
                        w.append(("se", se_gelu[(sbk - 1, 30 + hc)]))
                    if sbk == 0:
                        w.append(("st", W1ST[hc // 8]))
                        if hc == 0:
                            w.append(("v3", B1CV))
                p.add("tensor", lambda e, hc=hc, d=d, rot=rot, hh=hh: e.matmul(
                    sps[0][:, 512 * hh: 512 * hh + 512],
                    w1b[:, 4096 * d + 128 * hc: 4096 * d + 128 * hc + 128],
                    xfm[:, 4096 * rot + 512 * d: 4096 * rot + 512 * d + 512],
                    start=(d == 0), stop=(d == DC - 1)),
                    waits=w, inc=("t2" if d == DC - 1 else None))
            wg2 = [("t2", now("t2"))]
            if sbk > 0 and (sbk - 1, "mm2") in t2_mm2 and hc == 0:
                wg2.append(("t2", t2_mm2[(sbk - 1, "mm2")]))
            p.add("scalar", lambda e, hc=hc, hh=hh: e.activation(
                hsb[:, 512 * hc: 512 * hc + 512], sps[0][:, 512 * hh: 512 * hh + 512],
                AF.Gelu_apprx_tanh, bias=b1c[:, hc:hc + 1]),
                waits=wg2, inc="se")
            se_gelu[(sbk, hc)] = cnt["se"]
        t2_mm2[(sbk, "mm1")] = cnt["t2"]
        for dh in range(2):
            for hc in range(32):
                for sti in range(4):
                    outp = [sps[1][:, 0:512], sps[1][:, 512:1024], pue[:, :], puo[:, :]][sti]
                    w = []
                    if hc == 0 and sti == 0:
                        w = [("se", se_gelu[(sbk, 31)]), ("v3", now("v3"))]
                        if sbk == 0 and dh == 0:
                            w.append(("v3", VW))
                            w.append(("st", W2ST))
                    p.add("tensor", lambda e, outp=outp, hc=hc, sti=sti, dh=dh: e.matmul(
                        outp,
                        hsb[:, 512 * hc + 128 * sti: 512 * hc + 128 * sti + 128],
                        w2b[:, 1024 * hc + 512 * dh: 1024 * hc + 512 * dh + 512],
                        start=(hc == 0), stop=False),
                        waits=w, inc=None)
            for sti in range(4):
                outp = [sps[1][:, 0:512], sps[1][:, 512:1024], pue[:, :], puo[:, :]][sti]
                p.add("tensor", lambda e, outp=outp, dh=dh: e.matmul(
                    outp, onesw[0:1, 0:128], b2b[0:1, 512 * dh: 512 * dh + 512],
                    start=False, stop=True),
                    inc=("t2" if sti == 3 else None))
            for sti in range(4):
                outp = [sps[1][:, 0:512], sps[1][:, 512:1024], pue[:, :], puo[:, :]][sti]
                p.add("vector", lambda e, outp=outp, sti=sti: e.tensor_copy(
                    ysb[:, 512 * sti: 512 * sti + 512], outp),
                    waits=([("t2", now("t2")), ("s3", now("s3"))] if sti == 0 else [("v3", now("v3"))]),
                    inc="v3")
            p.add("sync", lambda e, sbk=sbk, dh=dh: e.dma_start(
                out=y_in.ap()[512 * sbk: 512 * sbk + 512, 512 * dh: 512 * dh + 512].rearrange("(s p) c -> p s c", s=4),
                in_=ysb[:, :].rearrange("p (s c) -> p s c", s=4)),
                waits=[("v3", now("v3"))], inc="s3", k=16)
        t2_mm2[(sbk, "mm2")] = cnt["t2"]
    if stage == 41:
        p.add("gpsimd", lambda e: e.engine_nop(), waits=[("v3", now("v3"))])
        p.emit()
        esB.close(); es.close()
        return nc
    if stage == 4:
        p.add("gpsimd", lambda e: e.engine_nop(), waits=[("cc", CC_Y3), ("g3", now("g3"))])
        p.emit()
        esB.close(); es.close()
        return nc
    p.emit()

    esB.close()
    esC = ExitStack()
    ygt = sb(esC, "ygt", [128, 8 * D], BF16)
    xtb = sb(esC, "xtb", [128, 4 * D], F32)
    ob = sb(esC, "ob", [128, 4 * D], F32)

    p = Plan()
    V3B = cnt["v3"]
    for i in range(NT):
        rot = i % 4
        wld = [("g3", bcntA[2 * i + 1])]
        if i >= 4:
            wld.append(("v3", V3B + 2 * (i - 3)))
        p.add("sync", lambda e, i=i, rot=rot: e.dma_start(
            out=ygt[:, 2048 * rot: 2048 * rot + 2048],
            in_=yab.ap()[256 * i: 256 * i + 256, :].rearrange("(a p) d -> p a d", a=2)),
            waits=wld, inc="st", k=16)
        ldc = cnt["st"]
        for j in range(2):
            p.add("gpsimd", lambda e, i=i, j=j, rot=rot: e.indirect_dma_start(
                out=ygt[:, 2048 * rot + 1024 * j: 2048 * rot + 1024 * j + 1024],
                out_offset=None, in_=yo2.ap()[:, :],
                in_offset=IOOA(ap=sidg[:, 32 + 16 * j + i: 33 + 16 * j + i], axis=0),
                bounds_check=BND[2], oob_is_err=False),
                waits=[("cc", CC_Y2), ("st", ldc), ("g3", now("g3"))], inc="g3", k=16)
            p.add("gpsimd", lambda e, i=i, j=j, rot=rot: e.indirect_dma_start(
                out=ygt[:, 2048 * rot + 1024 * j: 2048 * rot + 1024 * j + 1024],
                out_offset=None, in_=yo3.ap()[:, :],
                in_offset=IOOA(ap=sidg[:, 64 + 16 * j + i: 65 + 16 * j + i], axis=0),
                bounds_check=BND[3], oob_is_err=False),
                waits=[("cc", CC_Y3), ("g3", now("g3"))], inc="g3", k=16)
        gcc = cnt["g3"]
        p.add("sync", lambda e, i=i, rot=rot: e.dma_start(
            out=xtb[:, 1024 * rot: 1024 * rot + 1024],
            in_=xspill.ap()[128 * i: 128 * i + 128, :]),
            waits=([("v3", V3B + 2 * (i - 3))] if i >= 4 else []), inc="st", k=16)
        p.add("vector", lambda e, i=i, rot=rot: e.scalar_tensor_tensor(
            ob[:, 1024 * rot: 1024 * rot + 1024],
            ygt[:, 2048 * rot: 2048 * rot + 1024], cfs[:, i:i + 1],
            xtb[:, 1024 * rot: 1024 * rot + 1024], AO.mult, AO.add),
            waits=[("g3", gcc), ("st", now("st")), ("v3", now("v3"))], inc="v3")
        p.add("vector", lambda e, i=i, rot=rot: e.scalar_tensor_tensor(
            ob[:, 1024 * rot: 1024 * rot + 1024],
            ygt[:, 2048 * rot + 1024: 2048 * rot + 2048], cfs[:, 16 + i:17 + i],
            ob[:, 1024 * rot: 1024 * rot + 1024], AO.mult, AO.add),
            waits=[("v3", now("v3"))], inc="v3")
        p.add("sync", lambda e, i=i, rot=rot: e.dma_start(
            out=out_d.ap()[128 * i: 128 * i + 128, :],
            in_=ob[:, 1024 * rot: 1024 * rot + 1024]),
            waits=[("v3", now("v3"))], inc="io", k=16)
    p.add("sync", lambda e: e.drain(), waits=[("io", now("io"))])
    p.emit()

    esC.close(); es.close()
    return nc


_CACHE = {}


def _get_nc(stage=STAGE):
    if stage not in _CACHE:
        _CACHE[stage] = _build(stage)
    return _CACHE[stage]


def make_consts(core):
    import ml_dtypes
    bf = ml_dtypes.bfloat16
    tri = (np.arange(128)[:, None] < np.arange(128)[None, :]).astype(bf)
    trib = np.zeros((16, 2048), bf)
    for i in range(16):
        trib[:, 128 * i:128 * i + 128] = (np.arange(16)[:, None] < i).astype(bf)
    pmask = np.broadcast_to((np.arange(8) < core)[:, None], (8, 128)).astype(bf)
    bmask = ((np.arange(8) // 4) == (core // 4)).astype(np.float32)[:, None]
    eoh = (np.arange(8) == core).astype(np.float32)[:, None]
    iota8 = np.broadcast_to(np.tile(np.arange(8, dtype=np.float32), 16)[None, :], (128, 128)).copy()
    tokrow = (TOK * core + 128 * np.arange(16, dtype=np.float32)[None, :]
              + np.arange(128, dtype=np.float32)[:, None]).astype(np.float32)
    garbA = np.broadcast_to((GARB + np.arange(16, dtype=np.float32))[None, :], (128, 16)).copy()
    garbB = garbA + 16
    return dict(
        idf=np.eye(128, dtype=np.float32),
        idb=np.eye(128, dtype=np.float32).astype(bf),
        tri=np.ascontiguousarray(tri),
        trib=trib,
        pmask=np.ascontiguousarray(pmask),
        bmask=bmask, eoh=eoh, iota8=iota8, tokrow=np.ascontiguousarray(tokrow),
        garbA=garbA, garbB=garbB,
    )


def kernel(L0_attn_x, L0_attn_Wq, L0_attn_Wk, L0_attn_Wv,
           L0_ffn_Wg, L0_ffn_W1, L0_ffn_b1, L0_ffn_W2, L0_ffn_b2):
    from concourse.bass_utils import run_bass_kernel_spmd
    x = np.ascontiguousarray(np.asarray(L0_attn_x, dtype=np.float32)).reshape(NCORE * TOK, D)
    wq = np.ascontiguousarray(np.asarray(L0_attn_Wq, dtype=np.float32))
    wk = np.ascontiguousarray(np.asarray(L0_attn_Wk, dtype=np.float32))
    wv = np.ascontiguousarray(np.asarray(L0_attn_Wv, dtype=np.float32))
    wg = np.ascontiguousarray(np.asarray(L0_ffn_Wg, dtype=np.float32))
    w1 = np.asarray(L0_ffn_W1, dtype=np.float32)
    b1 = np.asarray(L0_ffn_b1, dtype=np.float32)
    w2 = np.asarray(L0_ffn_W2, dtype=np.float32)
    b2 = np.asarray(L0_ffn_b2, dtype=np.float32)
    nc = _get_nc(STAGE)
    in_maps = []
    for c in range(NCORE):
        m = dict(xin=np.ascontiguousarray(x[TOK * c: TOK * (c + 1)]),
                 wq=wq, wk=wk, wv=wv, wg=wg,
                 w1=np.ascontiguousarray(w1[c]),
                 b1=np.ascontiguousarray(b1[c])[None, :],
                 w2=np.ascontiguousarray(w2[c]),
                 b2=np.ascontiguousarray(b2[c])[None, :])
        m.update(make_consts(c))
        in_maps.append(m)
    res = run_bass_kernel_spmd(nc, in_maps, core_ids=list(range(NCORE)))
    out = np.concatenate([np.asarray(res.results[c]["out"]) for c in range(NCORE)], 0)
    return out.reshape(2, 8192, D).astype(np.float32)


# version 33 (final)
# speedup vs baseline: 1.0151x; 1.0151x over previous
import numpy as np
from contextlib import ExitStack

TOK = 2048
D = 1024
DC = 8
NG = 4
GS = 512
NT = 16
NCORE = 8
E = 8
HID = 4096
CAP = 4301
CAPPAD = 4608
NST = CAPPAD // 128
THALF = CAPPAD * 4 + 64
GARB = CAPPAD * 4
TSZP = 2 * THALF
GN_N = float(8192 * 1024)
EPS = 1e-5
NPAIR = 8
NSTEP = 64

STAGE = 9


def _build(stage=STAGE):
    import concourse.bass as bass
    import concourse.mybir as mybir
    from concourse.alu_op_type import AluOpType as AO
    F32 = mybir.dt.float32
    BF16 = mybir.dt.bfloat16
    U32 = mybir.dt.uint32
    AF = mybir.ActivationFunctionType
    AX = mybir.AxisListType
    IOOA = bass.IndirectOffsetOnAxis

    nc = bass.Bass("TRN2", target_bir_lowering=False)

    xin = nc.dram_tensor("xin", [TOK, D], F32, kind="ExternalInput")
    wq_d = nc.dram_tensor("wq", [D, D], F32, kind="ExternalInput")
    wk_d = nc.dram_tensor("wk", [D, D], F32, kind="ExternalInput")
    wv_d = nc.dram_tensor("wv", [D, D], F32, kind="ExternalInput")
    wg_d = nc.dram_tensor("wg", [D, E], F32, kind="ExternalInput")
    w1_d = nc.dram_tensor("w1", [D, HID], F32, kind="ExternalInput")
    b1_d = nc.dram_tensor("b1", [1, HID], F32, kind="ExternalInput")
    w2_d = nc.dram_tensor("w2", [HID, D], F32, kind="ExternalInput")
    b2_d = nc.dram_tensor("b2", [1, D], F32, kind="ExternalInput")
    idf_d = nc.dram_tensor("idf", [128, 128], F32, kind="ExternalInput")
    idb_d = nc.dram_tensor("idb", [128, 128], BF16, kind="ExternalInput")
    tri_d = nc.dram_tensor("tri", [128, 128], BF16, kind="ExternalInput")
    trib_d = nc.dram_tensor("trib", [16, 2048], BF16, kind="ExternalInput")
    pmask_d = nc.dram_tensor("pmask", [8, 128], BF16, kind="ExternalInput")
    bmask_d = nc.dram_tensor("bmask", [8, 1], F32, kind="ExternalInput")
    eoh_d = nc.dram_tensor("eoh", [8, 1], F32, kind="ExternalInput")
    iota8_d = nc.dram_tensor("iota8", [128, 128], F32, kind="ExternalInput")
    tokrow_d = nc.dram_tensor("tokrow", [128, 16], F32, kind="ExternalInput")
    garbA_d = nc.dram_tensor("garbA", [128, 16], F32, kind="ExternalInput")
    garbB_d = nc.dram_tensor("garbB", [128, 16], F32, kind="ExternalInput")
    out_d = nc.dram_tensor("out", [TOK, D], F32, kind="ExternalOutput")
    dbgA = dbgB = None
    if stage < 9:
        dbgA = nc.dram_tensor("dbgA", [128, DC * TOK], F32, kind="ExternalOutput")
        dbgB = nc.dram_tensor("dbgB", [128, DC * TOK], F32, kind="ExternalOutput")
    st_in = nc.dram_tensor("st_in", [1, 2], F32)
    st_out = nc.dram_tensor("st_out", [8, 2], F32)
    cnt_in = nc.dram_tensor("cnt_in", [1, 8], F32)
    cnt_out = nc.dram_tensor("cnt_out", [8, 8], F32)
    t_in = nc.dram_tensor("t_in", [TSZP], F32)
    t_out = nc.dram_tensor("t_out", [TSZP], F32)
    xt_in = nc.dram_tensor("xt_in", [TOK, D], BF16)
    xt_out = nc.dram_tensor("xt_out", [NCORE * TOK, D], BF16)
    y_in = nc.dram_tensor("y_in", [CAPPAD, D], BF16)
    yo1 = nc.dram_tensor("yo1", [NCORE * 2048, D], BF16)
    yo2 = nc.dram_tensor("yo2", [NCORE * 2048, D], BF16)
    yo3 = nc.dram_tensor("yo3", [NCORE * 512, D], BF16)
    yab = nc.dram_tensor("yab", [2 * TOK, D], BF16)
    w1s = nc.dram_tensor("w1s", [128, DC * HID], BF16)
    w2s = nc.dram_tensor("w2s", [128, 32 * D], BF16)
    xspill = nc.dram_tensor("xspill", [TOK, D], F32)
    tex_d = nc.dram_tensor("texd", [CAPPAD], F32)

    es = ExitStack()
    esA = ExitStack()

    def sb(stk, name, shape, dt):
        return stk.enter_context(nc.sbuf_tensor(name, shape, dt))

    stg = sb(es, "stg", [128, 1024], F32)
    onesw = sb(es, "onesw", [1, 512], BF16)
    idbs = sb(es, "idbs", [128, 128], BF16)
    sidg = sb(es, "sidg", [128, 96], U32)
    cfs = sb(es, "cfs", [128, 32], F32)
    xidxf = sb(es, "xidxf", [128, 36], F32)
    xidxu = sb(es, "xidxu", [128, 36], U32)

    xl = sb(esA, "xl", [128, DC * TOK], BF16)
    xf = sb(esA, "xf", [128, DC * TOK], F32)
    wqT = sb(esA, "wqT", [128, DC * D], BF16)
    wkT = sb(esA, "wkT", [128, DC * D], BF16)
    wvT = sb(esA, "wvT", [128, DC * D], BF16)
    stgb = sb(esA, "stgb", [128, 1024], BF16)
    usb = sb(esA, "usb", [128, 1024], F32)
    rcp = sb(esA, "rcp", [1, 1024], F32)
    t1s = sb(esA, "t1s", [128, 512], F32)
    murs = sb(esA, "murs", [1, 1024], BF16)
    sc2 = sb(esA, "sc2", [128, 2], F32)
    rsm = sb(esA, "rsm", [128, 2], F32)
    ssb = sb(esA, "ssb", [8, 2], F32)
    cnts = sb(esA, "cnts", [8, 8], F32)
    cntsb = sb(esA, "cntsb", [8, 8], BF16)
    tot16 = sb(esA, "tot16", [16, 8], F32)
    tot16b = sb(esA, "tot16b", [16, 8], BF16)
    sg1 = sb(esA, "sg1", [4, 16], F32)
    onesc = sb(esA, "onesc", [128, 1], BF16)
    onescf = sb(esA, "onescf", [128, 1], F32)
    onesr = sb(esA, "onesr", [1, 128], BF16)
    onesrf = sb(esA, "onesrf", [1, 128], F32)
    garbv = sb(esA, "garbv", [128, 1], F32)
    epsv = sb(esA, "epsv", [1, 1], F32)
    idfs = sb(esA, "idfs", [128, 128], F32)
    tris = sb(esA, "tris", [128, 128], BF16)
    pms = sb(esA, "pms", [8, 128], BF16)
    bms = sb(esA, "bms", [8, 1], F32)
    eohs = sb(esA, "eohs", [8, 1], F32)
    io8 = sb(esA, "io8", [128, 128], F32)
    tkr = sb(esA, "tkr", [128, 16], F32)
    garbA = sb(esA, "garbAs", [128, 16], F32)
    garbB = sb(esA, "garbBs", [128, 16], F32)
    wgb = sb(esA, "wgb", [128, DC * E], BF16)
    rt = sb(esA, "rt", [128, 128], F32)
    r1 = sb(esA, "r1", [128, 16], F32)
    r2 = sb(esA, "r2", [128, 16], F32)
    rg1 = sb(esA, "rg1", [128, 16], F32)
    rg2 = sb(esA, "rg2", [128, 16], F32)
    re1 = sb(esA, "re1", [128, 128], F32)
    re2 = sb(esA, "re2", [128, 128], F32)
    roh = sb(esA, "roh", [128, 128], BF16)
    rsc = sb(esA, "rsc", [128, 8], F32)
    rsd = sb(esA, "rsd", [128, 8], F32)
    rse = sb(esA, "rse", [128, 8], F32)
    posj = sb(esA, "posj", [128, 64], F32)
    tokv = sb(esA, "tokv", [128, 16], F32)
    sidu = sb(esA, "sidu", [128, 32], U32)

    esA2 = ExitStack()
    ka = sb(esA2, "ka", [128, DC * GS], BF16)
    kb = sb(esA2, "kb", [128, DC * GS], BF16)
    qa = sb(esA2, "qa", [128, DC * GS], BF16)
    va = sb(esA2, "va", [128, 4 * 1040], BF16)
    exb = sb(esA2, "exb", [128, 3 * 1024], BF16)

    sps = [es.enter_context(nc.psum_tensor("sps0", [128, 1024], F32)),
           es.enter_context(nc.psum_tensor("sps1", [128, 1024], F32))]
    pue = es.enter_context(nc.psum_tensor("pue", [128, 512], F32))
    puo = es.enter_context(nc.psum_tensor("puo", [128, 512], F32))
    pbf = es.enter_context(nc.psum_tensor("pbf", [128, 512], BF16))
    pun = es.enter_context(nc.psum_tensor("pun", [128, 512], F32))

    snames = ["ld", "te", "ve", "se", "gp", "pm", "mm", "exq", "uc", "fin", "rc",
              "bc", "yd", "pe", "cc", "st", "g2", "v2", "t2", "s2", "v3", "t3",
              "g3", "s3", "io", "wc", "ws", "wv", "wc2", "ws2", "wc3"]
    sems = {n: es.enter_context(nc.semaphore("s_" + n)) for n in snames}
    cnt = {n: 0 for n in snames}

    class Plan:
        def __init__(self):
            self.ev = {k: [] for k in ("sync", "tensor", "vector", "scalar", "gpsimd")}

        def add(self, eng, fn, waits=(), inc=None, k=1):
            self.ev[eng].append((fn, list(waits), inc, k))
            if inc is not None:
                cnt[inc] += k

        def emit(self):
            with nc.Block() as block:
                for name in ("sync", "tensor", "vector", "scalar", "gpsimd"):
                    evs = self.ev[name]
                    if not evs:
                        continue

                    def make(evs):
                        def emit_fn(eng):
                            for fn, w, inc, k in evs:
                                for s, v in w:
                                    eng.wait_ge(sems[s], v)
                                ins = fn(eng)
                                if inc is not None:
                                    ins.then_inc(sems[inc], k)
                        return emit_fn
                    getattr(block, name)(make(evs))

    def now(s):
        return cnt[s]

    def xfc(d, c0, n):
        return xf[:, 2048 * d + c0: 2048 * d + c0 + n]

    def xlc(d, c0, n):
        return xl[:, 2048 * d + c0: 2048 * d + c0 + n]

    p = Plan()
    for t, d in [(idfs, idf_d), (idbs, idb_d), (tris, tri_d),
                 (pms, pmask_d), (bms, bmask_d), (eohs, eoh_d), (io8, iota8_d),
                 (tkr, tokrow_d), (garbA, garbA_d), (garbB, garbB_d)]:
        p.add("sync", lambda e, t=t, d=d: e.dma_start(out=t[:], in_=d.ap()[:, :]), inc="ld", k=16)
    NCONST = cnt["ld"]
    for mset in [lambda e: e.memset(onesc[:], 1.0), lambda e: e.memset(onescf[:], 1.0),
                 lambda e: e.memset(onesr[:], 1.0), lambda e: e.memset(onesrf[:], 1.0),
                 lambda e: e.memset(onesw[:], 1.0),
                 lambda e: e.memset(garbv[:], float(GARB)),
                 lambda e: e.memset(epsv[:], EPS),
                 lambda e: e.memset(va[:, 64::65], 1.0), lambda e: e.memset(usb[:], 0.0)]:
        p.add("gpsimd", mset, inc="gp")
    NGP = cnt["gp"]

    for i in range(NT):
        p.add("sync", lambda e, i=i: e.dma_start(out=stg[:, :], in_=xin.ap()[128 * i:128 * i + 128, :]),
              waits=([("te", 2 * i)] if i >= 1 else []), inc="ld", k=16)
        for d4 in range(2):
            for j in range(4):
                dd = 4 * d4 + j
                w = []
                if j == 0:
                    w = [("ld", NCONST + 16 * (i + 1))]
                    if i >= 1:
                        w.append(("ve", 2 * (i - 1) + d4 + 1))
                p.add("tensor", lambda e, d4=d4, j=j, dd=dd: e.transpose(
                    sps[0][:, 512 * d4 + 128 * j: 512 * d4 + 128 * j + 128],
                    stg[:, 128 * dd: 128 * dd + 128], idfs[:, :]),
                    waits=w, inc=("te" if j == 3 else None))
            p.add("vector", lambda e, d4=d4, i=i: e.tensor_copy(
                xf[:, :].rearrange("p (d i c) -> p d i c", d=DC, i=NT, c=128)[:, 4 * d4:4 * d4 + 4, i, :],
                sps[0][:, 512 * d4: 512 * d4 + 512].rearrange("p (a c) -> p a c", a=4)),
                waits=[("te", 2 * i + d4 + 1)], inc="ve")
    TE_XIN = cnt["te"]

    CAST_VE = {}
    for wi, (wd, wt) in enumerate([(wq_d, wqT), (wk_d, wkT), (wv_d, wvT)]):
        for r in range(8):
            idx = 8 * wi + r
            sbuf = stgb[:, 0:1024] if idx % 2 == 0 else exb[:, 0:1024]
            wldw = [("te", TE_XIN)] if idx == 0 else [("ve", CAST_VE[idx - 1])]
            p.add("sync", lambda e, wd=wd, r=r: e.dma_start(
                out=stg[:, :], in_=wd.ap()[128 * r:128 * r + 128, :]),
                waits=wldw, inc="ld", k=16)
            wcst = [("ld", now("ld"))]
            if idx >= 2:
                wcst.append(("te", TE_XIN + 2 * (idx - 1)))
            p.add("vector", lambda e, sbuf=sbuf: e.tensor_copy(sbuf, stg[:, :]),
                  waits=wcst, inc="ve")
            cast_ve = now("ve")
            CAST_VE[idx] = cast_ve
            for d4 in range(2):
                for j in range(4):
                    dd = 4 * d4 + j
                    w = []
                    if j == 0:
                        w = [("ve", cast_ve), ("ve", now("ve"))]
                    p.add("tensor", lambda e, dd=dd, j=j, wi=wi, r=r: e.transpose(
                        pbf[:, 128 * j: 128 * j + 128],
                        (stgb if (8 * wi + r) % 2 == 0 else exb)[:, 128 * dd: 128 * dd + 128], idbs[:, :]),
                        waits=w, inc=("te" if j == 3 else None))
                p.add("vector", lambda e, d4=d4, r=r, wt=wt: e.tensor_copy(
                    wt[:, :].rearrange("p (d c) -> p d c", d=DC)[:, 4 * d4:4 * d4 + 4, 128 * r:128 * r + 128],
                    pbf[:, :].rearrange("p (a c) -> p a c", a=4)),
                    waits=[("te", now("te"))], inc="ve")
    p.add("sync", lambda e: e.dma_start(
        out=stg[:, 0:64].rearrange("p (d c) -> p d c", d=DC),
        in_=wg_d.ap()[:, :].rearrange("(d p) c -> p d c", d=DC)),
        waits=[("te", now("te"))], inc="ld", k=16)
    p.add("vector", lambda e: e.tensor_copy(wgb[:, :], stg[:, 0:64]),
          waits=[("ld", now("ld")), ("te", now("te"))], inc="ve")

    def emit_ln(p):
        for tt in range(4):
            c0 = 512 * tt
            for d in range(DC):
                w_cast = [("te", now("te")), ("ve", now("ve"))]
                if tt == 0 and d == 0:
                    w_cast.append(("gp", NGP))
                p.add("vector", lambda e, d=d, c0=c0: e.tensor_copy(stgb[:, 0:512], xfc(d, c0, 512)),
                      waits=w_cast, inc="ve")
                p.add("scalar", lambda e, d=d, c0=c0: e.activation(
                    stgb[:, 512:1024], xfc(d, c0, 512), AF.Square),
                    waits=[("te", now("te"))], inc="se")
                p.add("tensor", lambda e, d=d: e.matmul(
                    pue[0:1, :], onesc[:, :], stgb[:, 0:512], start=(d == 0), stop=(d == DC - 1)),
                    waits=[("ve", now("ve")), ("se", now("se"))] +
                          ([("ve", now("ve"))] if (tt > 0 and d == 0) else []),
                    inc=None)
                p.add("tensor", lambda e, d=d: e.matmul(
                    puo[0:1, :], onesc[:, :], stgb[:, 512:1024], start=(d == 0), stop=(d == DC - 1)),
                    inc="te")
            p.add("vector", lambda e, c0=c0: e.tensor_single_scalar(murs[0:1, 0:512], pue[0:1, :], 1.0 / D, AO.mult),
                waits=[("te", now("te"))], inc="ve")
            p.add("vector", lambda e: e.tensor_single_scalar(t1s[0:1, 0:512], puo[0:1, :], 1.0 / D, AO.mult),
                waits=[("ve", now("ve"))], inc="ve")
            p.add("scalar", lambda e, c0=c0: e.activation(
                rcp[0:1, 0:512], murs[0:1, 0:512], AF.Square),
                waits=[("ve", now("ve"))], inc="se")
            p.add("vector", lambda e: e.tensor_tensor(
                t1s[0:1, 0:512], t1s[0:1, 0:512], rcp[0:1, 0:512], AO.subtract),
                waits=[("se", now("se")), ("ve", now("ve"))], inc="ve")
            p.add("scalar", lambda e: e.activation(
                t1s[64:65, 0:512], t1s[0:1, 0:512], AF.Sqrt, bias=epsv[0:1, 0:1]),
                waits=[("ve", now("ve"))], inc="se")
            p.add("vector", lambda e: e.reciprocal(t1s[96:97, 0:512], t1s[64:65, 0:512]),
                  waits=[("se", now("se"))], inc="ve")
            p.add("vector", lambda e, c0=c0: e.tensor_copy(murs[0:1, 512:1024], t1s[96:97, 0:512]),
                  waits=[("ve", now("ve"))], inc="ve")
            p.add("tensor", lambda e, c0=c0: e.matmul(
                sps[0][:, 0:512], onesr[:, :], murs[0:1, 0:512], start=True, stop=True),
                waits=[("ve", now("ve"))], inc=None)
            p.add("tensor", lambda e, c0=c0: e.matmul(
                sps[0][:, 512:1024], onesr[:, :], murs[0:1, 512:1024], start=True, stop=True),
                inc="te")
            for d in range(DC):
                p.add("vector", lambda e, d=d, c0=c0: e.tensor_tensor(
                    t1s[:, 0:512], xfc(d, c0, 512), sps[0][:, 0:512], AO.subtract),
                    waits=[("te", now("te")), ("ve", now("ve"))], inc="ve")
                p.add("vector", lambda e, d=d, c0=c0: e.tensor_tensor(
                    xlc(d, c0, 512), t1s[:, 0:512], sps[0][:, 512:1024], AO.mult),
                    waits=[("ve", now("ve"))], inc="ve")

    emit_ln(p)
    p.emit()

    def emit_dump(srcA_f32, srcB_bf16):
        pd = Plan()
        if srcA_f32 is not None:
            pd.add("sync", lambda e: e.dma_start(out=dbgA.ap()[:, :], in_=srcA_f32[:, :]),
                   inc="st", k=16)
        for ch in range(16):
            pd.add("vector", lambda e, ch=ch: e.tensor_copy(
                stg[:, :], srcB_bf16[:, 1024 * ch: 1024 * ch + 1024]),
                waits=[("st", now("st"))], inc="ve")
            pd.add("sync", lambda e, ch=ch: e.dma_start(
                out=dbgB.ap()[:, 1024 * ch: 1024 * ch + 1024], in_=stg[:, :]),
                waits=[("ve", now("ve"))], inc="st", k=16)
        pd.add("sync", lambda e: e.dma_start(out=out_d.ap()[0:128, :], in_=stg[:, :]),
               waits=[("st", now("st"))], inc="st", k=16)
        pd.add("vector", lambda e: e.tensor_copy(t1s[0:1, 0:1], t1s[0:1, 0:1]),
               waits=[("st", now("st"))])
        pd.emit()

    if stage == 1:
        emit_dump(xf, xl)
        esA2.close(); esA.close(); es.close()
        return nc

    p = Plan()
    pslots = [(0, 0), (0, 1), (1, 0), (1, 1)]

    def proj_psum(slot):
        si, hh = pslots[slot % 4]
        return sps[si][:, 512 * hh: 512 * hh + 512]

    def emit_u(gstep, first, last):
        par3 = gstep % 3
        k = gstep % NSTEP
        ck, tkb = k // 4, k % 4
        w = [("exq", gstep + 1)]
        if first:
            w.append(("fin", gstep // NSTEP))
        p.add("tensor", lambda e, ck=ck, tkb=tkb, par3=par3, first=first, last=last: e.matmul(
            pue[0:65, :], va[:, 1040 * tkb + 65 * ck: 1040 * tkb + 65 * ck + 65],
            exb[:, 1024 * par3: 1024 * par3 + 512], start=first, stop=last),
            waits=w, inc=None)
        p.add("tensor", lambda e, ck=ck, tkb=tkb, par3=par3, first=first, last=last: e.matmul(
            puo[0:65, :], va[:, 1040 * tkb + 65 * ck: 1040 * tkb + 65 * ck + 65],
            exb[:, 1024 * par3 + 512: 1024 * par3 + 1024], start=first, stop=last),
            inc="uc")

    nproj = 0
    MM_END = 0
    pending_fin = [None]
    for g in range(NG):
        gc = 512 * g
        for kind in range(3):
            for u in range(8):
                slot = nproj % 4
                w0 = []
                if nproj >= 4:
                    w0.append(("pe", cnt["pe"] - 3))
                if g > 0 and kind == 0 and u == 0:
                    w0.append(("exq", NSTEP * NPAIR * g))
                for d in range(DC):
                    if kind == 0:
                        fn = lambda e, u=u, d=d, gc=gc, slot=slot: e.matmul(
                            proj_psum(slot), wqT[:, 1024 * d + 128 * u: 1024 * d + 128 * u + 128],
                            xl[:, 2048 * d + gc: 2048 * d + gc + 512],
                            start=(d == 0), stop=(d == DC - 1))
                    elif kind == 1:
                        fn = lambda e, u=u, d=d, gc=gc, slot=slot: e.matmul(
                            proj_psum(slot), wkT[:, 1024 * d + 128 * u: 1024 * d + 128 * u + 128],
                            xl[:, 2048 * d + gc: 2048 * d + gc + 512],
                            start=(d == 0), stop=(d == DC - 1))
                    else:
                        tc, dvh = u // 2, u % 2
                        fn = lambda e, tc=tc, dvh=dvh, d=d, gc=gc, slot=slot: e.matmul(
                            proj_psum(slot), xl[:, 2048 * d + gc + 128 * tc: 2048 * d + gc + 128 * tc + 128],
                            wvT[:, 1024 * d + 512 * dvh: 1024 * d + 512 * dvh + 512],
                            start=(d == 0), stop=(d == DC - 1))
                    p.add("tensor", fn, waits=(w0 if d == 0 else []),
                          inc=("pm" if d == DC - 1 else None))
                wev = [("pm", now("pm"))]
                if g > 0:
                    wev.append(("mm", MM_END) if kind != 2 else ("uc", NSTEP * NPAIR * g))
                if kind == 0:
                    p.add("vector", lambda e, u=u, slot=slot: e.tensor_copy(
                        qa[:, 512 * u: 512 * u + 512], proj_psum(slot)),
                        waits=wev, inc="pe")
                elif kind == 1:
                    p.add("vector", lambda e, u=u, slot=slot: e.tensor_copy(
                        ka[:, 512 * u: 512 * u + 512], proj_psum(slot)),
                        waits=wev, inc="v2")
                    p.add("vector", lambda e, u=u, slot=slot: e.tensor_copy(
                        kb[64:128, 512 * u: 512 * u + 512], proj_psum(slot)[0:64, :]),
                        waits=[("v2", now("v2"))], inc="v2")
                    p.add("vector", lambda e, u=u, slot=slot: e.tensor_copy(
                        kb[0:64, 512 * u: 512 * u + 512], proj_psum(slot)[64:128, :]),
                        waits=[("v2", now("v2"))], inc="pe")
                else:
                    tc, dvh = u // 2, u % 2
                    p.add("vector", lambda e, tc=tc, dvh=dvh, slot=slot: e.tensor_copy(
                        va[:, :].rearrange("p (t k c) -> p t k c", t=4, k=16, c=65)[:, tc, 8 * dvh:8 * dvh + 8, 0:64],
                        proj_psum(slot).rearrange("p (k c) -> p k c", k=8)),
                        waits=wev, inc="pe")
                nproj += 1
        PEG = cnt["pe"]

        for pr in range(NPAIR):
            P = NPAIR * g + pr
            b = pr
            for k in range(NSTEP):
                gstep = NSTEP * P + k
                ck, tkb = k // 4, k % 4
                kt, j2 = ck // 2, ck % 2
                par = gstep % 2
                wts = []
                if pr == 0 and k <= 1:
                    wts.append(("pe", PEG))
                if gstep >= 2:
                    wts.append(("exq", gstep - 1))
                top = ka if j2 == 0 else kb
                bot = kb if j2 == 0 else ka
                p.add("tensor", lambda e, top=top, kt=kt, tkb=tkb, b=b, par=par: e.matmul(
                    sps[par][:, 0:512],
                    top[0:64, 512 * kt + 128 * tkb: 512 * kt + 128 * tkb + 128],
                    qa[0:64, 512 * b: 512 * b + 512],
                    start=True, stop=True, tile_position=(0, 0)),
                    waits=wts, inc=None)
                p.add("tensor", lambda e, bot=bot, kt=kt, tkb=tkb, b=b, par=par: e.matmul(
                    sps[par][:, 512:1024],
                    bot[64:128, 512 * kt + 128 * tkb: 512 * kt + 128 * tkb + 128],
                    qa[64:128, 512 * b: 512 * b + 512],
                    start=True, stop=True, tile_position=(64, 0)),
                    inc="mm")
                if k >= 1:
                    emit_u(gstep - 1, first=(k == 1), last=False)
                if k == 12 and pending_fin[0] is not None:
                    pending_fin[0]()
                    pending_fin[0] = None
                ws = [("mm", now("mm"))]
                if gstep >= 3:
                    ws.append(("uc", gstep - 2))
                p.add("scalar", lambda e, par=par, slot3=gstep % 3: e.activation(
                    exb[:, 1024 * slot3: 1024 * slot3 + 1024], sps[par][:, :], AF.Exp),
                    waits=ws, inc="exq")
            emit_u(NSTEP * P + NSTEP - 1, first=False, last=True)
            p.add("vector", lambda e: e.tensor_copy(usb[0:65, 0:512], pue[0:65, :]),
                  waits=[("uc", NSTEP * (P + 1)), ("v2", now("v2"))], inc="v2")
            p.add("vector", lambda e: e.tensor_copy(usb[0:65, 512:1024], puo[0:65, :]),
                  waits=[("v2", now("v2"))], inc="fin")

            def mk_finb(P=P, b=b, gc=gc):
                def finb():
                    p.add("vector", lambda e: e.reciprocal(rcp[0:1, 0:512], usb[64:65, 0:512]),
                          waits=[("fin", P + 1), ("bc", 2 * P), ("v2", now("v2"))], inc="v2")
                    p.add("vector", lambda e: e.reciprocal(rcp[0:1, 512:1024], usb[64:65, 512:1024]),
                          waits=[("v2", now("v2"))], inc="rc")
                    p.add("tensor", lambda e: e.matmul(
                        pun[:, :], onesrf[:, :], rcp[0:1, 0:512], start=True, stop=True),
                        waits=[("rc", P + 1), ("yd", P)], inc="bc")
                    p.add("vector", lambda e: e.tensor_tensor(
                        t1s[0:64, :], usb[0:64, 0:512], pun[0:64, :], AO.mult),
                        waits=[("bc", now("bc")), ("v2", now("v2"))], inc="v2")
                    p.add("vector", lambda e: e.tensor_tensor(
                        xl[0:64, 2048 * b + gc: 2048 * b + gc + 512],
                        t1s[0:64, :], xl[0:64, 2048 * b + gc: 2048 * b + gc + 512], AO.add),
                        waits=[("v2", now("v2"))], inc="v2")
                    p.add("tensor", lambda e: e.matmul(
                        pun[:, :], onesrf[:, :], rcp[0:1, 512:1024], start=True, stop=True),
                        waits=[("v2", now("v2"))], inc="bc")
                    p.add("vector", lambda e: e.tensor_copy(t1s[64:128, :], usb[0:64, 512:1024]),
                          waits=[("bc", now("bc")), ("v2", now("v2"))], inc="v2")
                    p.add("vector", lambda e: e.tensor_tensor(
                        t1s[64:128, :], t1s[64:128, :], pun[64:128, :], AO.mult),
                        waits=[("v2", now("v2"))], inc="v2")
                    p.add("vector", lambda e, b=b, gc=gc: e.tensor_tensor(
                        xl[64:128, 2048 * b + gc: 2048 * b + gc + 512],
                        t1s[64:128, :], xl[64:128, 2048 * b + gc: 2048 * b + gc + 512], AO.add),
                        waits=[("v2", now("v2"))], inc="yd")
                return finb
            if P == NG * NPAIR - 1:
                mk_finb()()
            else:
                pending_fin[0] = mk_finb()
        MM_END = cnt["mm"]
    p.emit()
    esA2.close()

    if stage == 2:
        emit_dump(xf, xl)
        esA.close(); es.close()
        return nc

    esA3 = ExitStack()
    wcf = sb(esA3, "wcf", [128, 2 * 2048], F32)
    wcb = sb(esA3, "wcb", [128, 2 * 2048], BF16)
    tribs = sb(esA3, "tribs", [16, 2048], BF16)
    tsb8c = sb(esA3, "tsb8c", [8, 512], F32)
    tex = sb(esA3, "tex", [1, 512], F32)
    zz = sb(esA3, "zz", [128, 289], F32)

    p = Plan()
    p.add("sync", lambda e: e.dma_start(out=tribs[:], in_=trib_d.ap()[:, :]), inc="st", k=16)
    TRIBS_ST = cnt["st"]
    p.add("gpsimd", lambda e: e.memset(zz[:], 0.0), inc="gp")
    NGPZ = cnt["gp"]
    p.add("vector", lambda e: e.reduce_sum(sc2[:, 0:1], xl[:, :], axis=AX.X), inc="ve")
    p.add("scalar", lambda e: e.activation(stgb[:, 0:1], xl[:, 0:1], AF.Square,
                                           accum_out=sc2[:, 1:2]), inc="se")
    p.ev["scalar"].pop(); cnt["se"] -= 1
    for chx in range(16):
        p.add("scalar", lambda e, chx=chx: e.activation(
            stgb[:, :], xl[:, 1024 * chx: 1024 * chx + 1024], AF.Square,
            accum_out=rt[:, chx:chx + 1]), inc="se")
    p.add("vector", lambda e: e.reduce_sum(sc2[:, 1:2], rt[:, 0:16], axis=AX.X),
          waits=[("se", now("se"))], inc="ve")
    p.add("tensor", lambda e: e.matmul(pue[0:1, 0:2], onescf[:, :], sc2[:, :],
                                       start=True, stop=True),
          waits=[("ve", now("ve"))], inc="te")
    p.add("vector", lambda e: e.tensor_copy(sg1[0:1, 0:2], pue[0:1, 0:2]),
          waits=[("te", now("te"))], inc="ve")
    p.add("sync", lambda e: e.dma_start(out=st_in.ap()[:, :], in_=sg1[0:1, 0:2]),
          waits=[("ve", now("ve"))], inc="st", k=16)
    p.add("gpsimd", lambda e: e.collective_compute(
        "AllGather", mybir.AluOpType.bypass, replica_groups=[list(range(NCORE))],
        ins=[st_in.ap().opt()], outs=[st_out.ap().opt()]),
        waits=[("st", now("st"))], inc="cc")
    p.add("sync", lambda e: e.dma_start(out=ssb[:, :], in_=st_out.ap()[:, :]),
          waits=[("cc", now("cc"))], inc="st", k=16)
    p.add("tensor", lambda e: e.matmul(pue[0:1, 0:2], bms[:, :], ssb[:, :],
                                       start=True, stop=True),
          waits=[("st", now("st"))], inc="te")
    p.add("vector", lambda e: e.tensor_single_scalar(sg1[0:1, 2:4], pue[0:1, 0:2],
                                              1.0 / GN_N, AO.mult),
          waits=[("te", now("te"))], inc="ve")
    p.add("scalar", lambda e: e.activation(sg1[0:1, 4:5], sg1[0:1, 2:3], AF.Square),
          waits=[("ve", now("ve"))], inc="se")
    p.add("vector", lambda e: e.tensor_tensor(sg1[0:1, 5:6], sg1[0:1, 3:4],
                                              sg1[0:1, 4:5], AO.subtract),
          waits=[("se", now("se"))], inc="ve")
    p.add("scalar", lambda e: e.activation(sg1[0:1, 6:7], sg1[0:1, 5:6], AF.Sqrt,
                                           bias=epsv[0:1, 0:1]),
          waits=[("ve", now("ve"))], inc="se")
    p.add("vector", lambda e: e.reciprocal(sg1[0:1, 8:9], sg1[0:1, 6:7]),
          waits=[("se", now("se"))], inc="ve")
    p.add("vector", lambda e: e.tensor_tensor(sg1[0:1, 9:10], sg1[0:1, 2:3],
                                              sg1[0:1, 8:9], AO.mult),
          waits=[("ve", now("ve"))], inc="ve")
    p.add("tensor", lambda e: e.matmul(pue[:, 0:2], onesrf[:, :], sg1[0:1, 8:10],
                                       start=True, stop=True),
          waits=[("ve", now("ve"))], inc="te")
    p.add("vector", lambda e: e.tensor_copy(rsm[:, :], pue[:, 0:2]),
          waits=[("te", now("te"))], inc="ve")
    for d in range(DC):
        p.add("vector", lambda e, d=d: e.scalar_tensor_tensor(
            xfc(d, 0, 2048), xlc(d, 0, 2048), rsm[:, 0:1], xfc(d, 0, 2048),
            AO.mult, AO.add),
            waits=[("ve", now("ve"))], inc="ve")
        p.add("vector", lambda e, d=d: e.tensor_single_scalar(xfc(d, 0, 2048), xfc(d, 0, 2048), rsm[:, 1:2], AO.subtract),
            waits=[("ve", now("ve"))], inc="ve")
    emit_ln(p)
    VE_LN2 = cnt["ve"]

    for i in range(NT):
        for d4 in range(2):
            for j in range(4):
                dd = 4 * d4 + j
                w = []
                if j == 0:
                    w = [("ve", now("ve"))]
                    if i > 0 or d4 > 0:
                        w.append(("st", now("st")))
                p.add("tensor", lambda e, d4=d4, j=j, dd=dd, i=i: e.transpose(
                    sps[1][:, 512 * d4 + 128 * j: 512 * d4 + 128 * j + 128],
                    xf[:, 2048 * dd + 128 * i: 2048 * dd + 128 * i + 128], idfs[:, :]),
                    waits=w, inc=("te" if j == 3 else None))
            p.add("vector", lambda e, d4=d4: e.tensor_copy(
                stg[:, 512 * d4: 512 * d4 + 512], sps[1][:, 512 * d4: 512 * d4 + 512]),
                waits=[("te", now("te"))], inc="ve")
        p.add("sync", lambda e, i=i: e.dma_start(
            out=xspill.ap()[128 * i: 128 * i + 128, :], in_=stg[:, :]),
            waits=[("ve", now("ve"))], inc="st", k=16)
        for d4 in range(2):
            for j in range(4):
                dd = 4 * d4 + j
                w = []
                if j == 0:
                    w = [("ve", now("ve"))]
                    if i > 0 or d4 > 0:
                        w.append(("g2", now("g2")))
                p.add("tensor", lambda e, dd=dd, j=j, i=i: e.transpose(
                    pbf[:, 128 * j: 128 * j + 128],
                    xl[:, 2048 * dd + 128 * i: 2048 * dd + 128 * i + 128], idbs[:, :]),
                    waits=w, inc=("te" if j == 3 else None))
            p.add("vector", lambda e, d4=d4: e.tensor_copy(
                stgb[:, 512 * d4: 512 * d4 + 512], pbf[:, :]),
                waits=[("te", now("te"))], inc="ve")
        p.add("scalar", lambda e, i=i: e.dma_start(
            out=xt_in.ap()[128 * i: 128 * i + 128, :], in_=stgb[:, :]),
            waits=[("ve", now("ve"))], inc="g2", k=16)

    G2_PUB = cnt["g2"]

    p.add("gpsimd", lambda e: e.dma_start(
        out=t_in.ap().rearrange("(p f) -> p f", p=128), in_=zz[:, :]),
        waits=[("gp", NGPZ)], inc="g2", k=16)
    G2_TZ = cnt["g2"]

    for i in range(NT):
        for d in range(DC):
            w = [("ve", VE_LN2)] if d == 0 else []
            p.add("tensor", lambda e, d=d, i=i: e.matmul(
                puo[:, 8 * i: 8 * i + 8],
                xl[:, 2048 * d + 128 * i: 2048 * d + 128 * i + 128],
                wgb[:, 8 * d: 8 * d + 8],
                start=(d == 0), stop=(d == DC - 1)),
                waits=w, inc=("te" if d == DC - 1 else None))
    TE_LOG = cnt["te"]
    def b3d(t):
        return t[:, :].rearrange("p (t e) -> p t e", e=8)
    def bc16(t, c0):
        return bass.AP(t, c0, [[16, 128], [1, 16], [0, 8]])
    p.add("scalar", lambda e: e.activation(rt[:, :], puo[:, 0:128], AF.Exp),
          waits=[("te", TE_LOG)], inc="se")
    p.add("vector", lambda e: e.reduce_max(r1[:, :], b3d(rt), axis=AX.X),
          waits=[("se", now("se")), ("v2", now("v2"))], inc="v2")
    p.add("vector", lambda e: e.tensor_tensor(b3d(re1), b3d(rt), bc16(r1, 0), AO.is_equal),
          waits=[("v2", now("v2"))], inc="v2")
    p.add("vector", lambda e: e.tensor_tensor(re2[:, :], re1[:, :], rt[:, :], AO.mult),
          waits=[("v2", now("v2"))], inc="v2")
    p.add("vector", lambda e: e.tensor_tensor(re2[:, :], rt[:, :], re2[:, :], AO.subtract),
          waits=[("v2", now("v2"))], inc="v2")
    p.add("vector", lambda e: e.reduce_max(r2[:, :], b3d(re2), axis=AX.X),
          waits=[("v2", now("v2"))], inc="v2")
    p.add("vector", lambda e: e.tensor_tensor(b3d(re2), b3d(re2), bc16(r2, 0), AO.is_equal),
          waits=[("v2", now("v2"))], inc="v2")
    p.add("vector", lambda e: e.tensor_tensor(rg1[:, :], r1[:, :], r2[:, :], AO.add),
          waits=[("v2", now("v2"))], inc="v2")
    p.add("vector", lambda e: e.reciprocal(rg2[:, :], rg1[:, :]),
          waits=[("v2", now("v2"))], inc="v2")
    p.add("vector", lambda e: e.tensor_tensor(rg1[:, :], r1[:, :], rg2[:, :], AO.mult),
          waits=[("v2", now("v2"))], inc="v2")
    p.add("vector", lambda e: e.tensor_tensor(rg2[:, :], r2[:, :], rg2[:, :], AO.mult),
          waits=[("v2", now("v2"))], inc="v2")
    p.add("vector", lambda e: e.tensor_tensor(roh[:, :], re1[:, :], re2[:, :], AO.add),
          waits=[("v2", now("v2"))], inc="v2")
    V2_OH = cnt["v2"]
    for i in range(NT):
        p.add("tensor", lambda e, i=i: e.matmul(
            pue[:, 8 * i: 8 * i + 8], tris[:, :], roh[:, 8 * i: 8 * i + 8],
            start=(i == 0), stop=False),
            waits=[("v2", V2_OH)] + ([("ve", VE_LN2)] if i == 0 else []), inc=None)
        p.add("tensor", lambda e, i=i: e.matmul(
            sps[0][0:1, 8 * i: 8 * i + 8], onesc[:, :], roh[:, 8 * i: 8 * i + 8],
            start=True, stop=True),
            inc="te")
    TE_TOT = cnt["te"]
    p.add("vector", lambda e: e.tensor_copy(tex[0:1, 0:128], sps[0][0:1, 0:128]),
          waits=[("te", TE_TOT)], inc="ve")
    p.add("sync", lambda e: e.dma_start(
        out=tex_d.ap()[0:128].rearrange("(a f) -> a f", a=1), in_=tex[0:1, 0:128]),
        waits=[("ve", now("ve"))], inc="st", k=16)
    p.add("sync", lambda e: e.dma_start(
        out=tot16[:, :], in_=tex_d.ap()[0:128].rearrange("(i c) -> i c", i=16)),
        waits=[("st", now("st"))], inc="st", k=16)
    p.add("vector", lambda e: e.tensor_copy(tot16b[:, :], tot16[:, :]),
          waits=[("st", now("st"))], inc="ve")
    p.add("tensor", lambda e: e.matmul(sps[0][0:1, 128:136], onescf[0:16, :], tot16[:, :],
                                       start=True, stop=True),
          waits=[("ve", now("ve"))], inc="te")
    p.add("vector", lambda e: e.tensor_copy(sg1[0:1, 0:8], sps[0][0:1, 128:136]),
          waits=[("te", now("te"))], inc="ve")
    p.add("sync", lambda e: e.dma_start(out=cnt_in.ap()[:, :], in_=sg1[0:1, 0:8]),
          waits=[("ve", now("ve"))], inc="st", k=16)
    p.add("gpsimd", lambda e: e.collective_compute(
        "AllGather", mybir.AluOpType.bypass, replica_groups=[list(range(NCORE))],
        ins=[cnt_in.ap().opt()], outs=[cnt_out.ap().opt()]),
        waits=[("st", now("st"))], inc="cc")
    p.add("sync", lambda e: e.dma_start(out=cnts[:, :], in_=cnt_out.ap()[:, :]),
          waits=[("cc", now("cc"))], inc="st", k=16)
    p.add("vector", lambda e: e.tensor_copy(cntsb[:, :], cnts[:, :]),
          waits=[("st", now("st"))], inc="ve")
    VE_CNTS = cnt["ve"]
    for i in range(NT):
        p.add("tensor", lambda e, i=i: e.matmul(
            pue[:, 8 * i: 8 * i + 8], tribs[:, 128 * i: 128 * i + 128], tot16b[:, :],
            start=False, stop=False),
            waits=([("ve", VE_CNTS), ("st", TRIBS_ST)] if i == 0 else []), inc=None)
        p.add("tensor", lambda e, i=i: e.matmul(
            pue[:, 8 * i: 8 * i + 8], pms[:, :], cntsb[:, :],
            start=False, stop=True),
            inc="te")
    TE_CFULL = cnt["te"]
    for j, (eq, pcol, ecol) in enumerate([(re1, 0, 32), (re2, 16, 48)]):
        p.add("vector", lambda e, eq=eq: e.tensor_tensor(rt[:, :], eq[:, :], pue[:, 0:128], AO.mult),
              waits=[("te", TE_CFULL), ("v2", now("v2"))], inc="v2")
        p.add("vector", lambda e, pcol=pcol: e.reduce_sum(
            posj[:, pcol:pcol + 16], b3d(rt), axis=AX.X),
            waits=[("v2", now("v2"))], inc="v2")
        p.add("vector", lambda e, eq=eq: e.tensor_tensor(rt[:, :], eq[:, :], io8[:, :], AO.mult),
              waits=[("v2", now("v2"))], inc="v2")
        p.add("vector", lambda e, ecol=ecol: e.reduce_sum(
            posj[:, ecol:ecol + 16], b3d(rt), axis=AX.X),
            waits=[("v2", now("v2"))], inc="v2")
    for j, (pcol, ecol, gcol, garbT) in enumerate([(0, 32, rg1, garbA), (16, 48, rg2, garbB)]):
        p.add("vector", lambda e, pcol=pcol: e.tensor_single_scalar(
            re1[:, 0:16], posj[:, pcol:pcol + 16], float(CAP), AO.is_lt),
            waits=[("v2", now("v2"))], inc="v2")
        p.add("vector", lambda e, gcol=gcol, j=j: e.tensor_tensor(
            cfs[:, 16 * j: 16 * j + 16], gcol[:, :], re1[:, 0:16], AO.mult),
            waits=[("v2", now("v2"))], inc="v2")
        p.add("vector", lambda e, pcol=pcol, ecol=ecol: e.scalar_tensor_tensor(
            re1[:, 16:32], posj[:, ecol:ecol + 16], float(CAPPAD), posj[:, pcol:pcol + 16],
            AO.mult, AO.add),
            waits=[("v2", now("v2"))], inc="v2")
        p.add("vector", lambda e, pcol=pcol: e.tensor_single_scalar(
            re2[:, 64:80], posj[:, pcol:pcol + 16], 2048.0, AO.is_lt),
            waits=[("v2", now("v2"))], inc="v2")
        p.add("vector", lambda e: e.tensor_tensor(
            re2[:, 64:80], re2[:, 64:80], re1[:, 0:16], AO.mult),
            waits=[("v2", now("v2"))], inc="v2")
        p.add("vector", lambda e, pcol=pcol, ecol=ecol: e.scalar_tensor_tensor(
            re2[:, 80:96], posj[:, ecol:ecol + 16], 2048.0, posj[:, pcol:pcol + 16],
            AO.mult, AO.add),
            waits=[("v2", now("v2"))], inc="v2")
        p.add("vector", lambda e: e.tensor_tensor(
            re2[:, 80:96], re2[:, 80:96], re2[:, 64:80], AO.mult),
            waits=[("v2", now("v2"))], inc="v2")
        p.add("vector", lambda e, j=j: e.tensor_copy(
            sidg[:, 16 * j: 16 * j + 16], re2[:, 80:96]),
            waits=[("v2", now("v2"))], inc="v2")
        for (kk, b0, b1v, ll) in [(1, 2048.0, 4096.0, 2048.0), (2, 4096.0, None, 512.0)]:
            p.add("vector", lambda e, pcol=pcol, b0=b0: e.tensor_single_scalar(
                re2[:, 64:80], posj[:, pcol:pcol + 16], b0, AO.is_ge),
                waits=[("v2", now("v2"))], inc="v2")
            if b1v is not None:
                p.add("vector", lambda e, pcol=pcol, b1v=b1v: e.tensor_single_scalar(
                    re2[:, 96:112], posj[:, pcol:pcol + 16], b1v, AO.is_lt),
                    waits=[("v2", now("v2"))], inc="v2")
                p.add("vector", lambda e: e.tensor_tensor(
                    re2[:, 64:80], re2[:, 64:80], re2[:, 96:112], AO.mult),
                    waits=[("v2", now("v2"))], inc="v2")
            p.add("vector", lambda e: e.tensor_tensor(
                re2[:, 64:80], re2[:, 64:80], re1[:, 0:16], AO.mult),
                waits=[("v2", now("v2"))], inc="v2")
            p.add("vector", lambda e, pcol=pcol, b0=b0: e.tensor_single_scalar(
                re2[:, 80:96], posj[:, pcol:pcol + 16], b0, AO.subtract),
                waits=[("v2", now("v2"))], inc="v2")
            p.add("vector", lambda e, ecol=ecol, ll=ll: e.scalar_tensor_tensor(
                re2[:, 80:96], posj[:, ecol:ecol + 16], ll, re2[:, 80:96],
                AO.mult, AO.add),
                waits=[("v2", now("v2"))], inc="v2")
            p.add("vector", lambda e: e.tensor_single_scalar(
                re2[:, 80:96], re2[:, 80:96], 65536.0, AO.subtract),
                waits=[("v2", now("v2"))], inc="v2")
            p.add("vector", lambda e: e.tensor_tensor(
                re2[:, 80:96], re2[:, 80:96], re2[:, 64:80], AO.mult),
                waits=[("v2", now("v2"))], inc="v2")
            p.add("vector", lambda e: e.tensor_single_scalar(
                re2[:, 80:96], re2[:, 80:96], 65536.0, AO.add),
                waits=[("v2", now("v2"))], inc="v2")
            p.add("vector", lambda e, j=j, kk=kk: e.tensor_copy(
                sidg[:, 32 * kk + 16 * j: 32 * kk + 16 * j + 16], re2[:, 80:96]),
                waits=[("v2", now("v2"))], inc="v2")
        p.add("vector", lambda e, ecol=ecol: e.tensor_single_scalar(
            re1[:, 48:64], posj[:, ecol:ecol + 16], 4.0, AO.is_ge),
            waits=[("v2", now("v2"))], inc="v2")
        p.add("vector", lambda e: e.scalar_tensor_tensor(
            re1[:, 16:32], re1[:, 48:64], 64.0, re1[:, 16:32], AO.mult, AO.add),
            waits=[("v2", now("v2"))], inc="v2")
        p.add("vector", lambda e, garbT=garbT: e.tensor_tensor(
            re1[:, 16:32], re1[:, 16:32], garbT[:, :], AO.subtract),
            waits=[("v2", now("v2"))], inc="v2")
        p.add("vector", lambda e: e.tensor_tensor(
            re1[:, 16:32], re1[:, 16:32], re1[:, 0:16], AO.mult),
            waits=[("v2", now("v2"))], inc="v2")
        p.add("vector", lambda e, garbT=garbT: e.tensor_tensor(
            re1[:, 16:32], re1[:, 16:32], garbT[:, :], AO.add),
            waits=[("v2", now("v2"))], inc="v2")
        p.add("vector", lambda e, j=j: e.tensor_copy(
            sidu[:, 16 * j: 16 * j + 16], re1[:, 16:32]),
            waits=[("v2", now("v2"))], inc="v2")
    V2S = cnt["v2"]
    for i in range(NT):
        for j in range(2):
            p.add("gpsimd", lambda e, i=i, j=j: e.indirect_dma_start(
                out=t_in.ap()[:, None],
                out_offset=IOOA(ap=sidu[:, 16 * j + i: 16 * j + i + 1], axis=0),
                in_=tkr[:, i:i + 1], in_offset=None, compute_op=AO.add),
                waits=[("v2", V2S), ("g2", G2_TZ)], inc="g2", k=16)
    p.add("gpsimd", lambda e: e.collective_compute(
        "AllReduce", mybir.AluOpType.add, replica_groups=[list(range(NCORE))],
        ins=[t_in.ap().opt()], outs=[t_out.ap().opt()]),
        waits=[("g2", now("g2"))], inc="cc")
    CC_T = cnt["cc"]
    p.add("gpsimd", lambda e: e.collective_compute(
        "AllGather", mybir.AluOpType.bypass, replica_groups=[list(range(NCORE))],
        ins=[xt_in.ap().opt()], outs=[xt_out.ap().opt()]),
        waits=[("g2", G2_PUB)], inc="cc")
    CC_XT = cnt["cc"]
    NWCH = 32
    WSEM = [("wc", "ws"), ("wc2", "ws2")]

    def wsrc_ap(kk):
        if kk < 16:
            r, h = kk // 2, kk % 2
            return w1_d.ap()[128 * r: 128 * r + 128, 2048 * h: 2048 * h + 2048]
        m = kk - 16
        return w2_d.ap()[256 * m: 256 * m + 256, :].rearrange("(a p) d -> p a d", a=2)

    def wdst(kk):
        return (w1s, 2048 * kk) if kk < 16 else (w2s, 2048 * (kk - 16))

    def wload(kk):
        sl = kk % 2
        outap = wcf[:, 2048 * sl: 2048 * sl + 2048]
        if kk >= 16:
            outap = outap.rearrange("p (a d) -> p a d", a=2)
        p.add("gpsimd", lambda e, kk=kk, outap=outap: e.dma_start(
            out=outap, in_=wsrc_ap(kk)),
            waits=([("wv", kk - 1)] if kk >= 2 else []), inc=WSEM[sl][0], k=16)

    wload(0)
    wload(1)
    for kk in range(NWCH):
        sl = kk % 2
        dT, do = wdst(kk)
        p.add("gpsimd", lambda e, dT=dT, do=do, sl=sl: e.dma_start(
            out=dT.ap()[:, do: do + 2048], in_=wcb[:, 2048 * sl: 2048 * sl + 2048]),
            waits=[("wv", kk + 1)], inc=WSEM[sl][1], k=16)
        if kk + 2 < NWCH:
            wload(kk + 2)

    def emit_wcast(kks):
        for kk in kks:
            sl = kk % 2
            w = [(WSEM[sl][0], 16 * (kk // 2 + 1))]
            if kk >= 2:
                w.append((WSEM[sl][1], 16 * ((kk - 2) // 2 + 1)))
            p.add("vector", lambda e, sl=sl: e.tensor_copy(
                wcb[:, 2048 * sl: 2048 * sl + 2048], wcf[:, 2048 * sl: 2048 * sl + 2048]),
                waits=w, inc="wv")

    te_prev_mm = now("te")
    for c in range(9):
        if c < 8:
            emit_wcast(range(4 * c, 4 * c + 4))
        p.add("sync", lambda e, c=c: e.dma_start(
            out=tsb8c[0:4, :],
            in_=bass.AP(t_out, 512 * c, [[CAPPAD, 4], [1, 512]])),
            waits=[("cc", CC_T)] + ([("te", te_prev_mm)] if c > 0 else []), inc="st", k=16)
        p.add("sync", lambda e, c=c: e.dma_start(
            out=tsb8c[4:8, :],
            in_=bass.AP(t_out, THALF + 512 * c, [[CAPPAD, 4], [1, 512]])),
            inc="st", k=16)
        p.add("tensor", lambda e, c=c: e.matmul(
            sps[1][0:1, 0:512],
            eohs[:, :], tsb8c[:, :], start=True, stop=True),
            waits=[("st", now("st"))] + ([("ve", now("ve"))] if c > 0 else []), inc="te")
        te_prev_mm = now("te")
        p.add("vector", lambda e: e.tensor_single_scalar(tex[0:1, 0:512], sps[1][0:1, 0:512], float(TOK * NCORE - 1), AO.min),
            waits=[("te", now("te"))], inc="ve")
        p.add("sync", lambda e, c=c: e.dma_start(
            out=tex_d.ap()[512 * c: 512 * c + 512].rearrange("(a f) -> a f", a=1),
            in_=tex[0:1, 0:512]),
            waits=[("ve", now("ve"))], inc="st", k=16)
    p.add("sync", lambda e: e.dma_start(
        out=xidxf[:, :], in_=tex_d.ap()[:].rearrange("(s p) -> p s", p=128),
        allow_slow_non_contiguous=True),
        waits=[("st", now("st"))], inc="st", k=16)
    p.add("vector", lambda e: e.tensor_copy(xidxu[:, :], xidxf[:, :]),
          waits=[("st", now("st"))], inc="ve")
    p.emit()

    if stage == 3:
        emit_dump(xf, xl)
        pd = Plan()
        pd.add("sync", lambda e: e.dma_start(out=out_d.ap()[0:128, 0:36], in_=xidxf[:, :]),
               inc="st", k=16)
        pd.add("sync", lambda e: e.dma_start(out=out_d.ap()[128:256, 0:64], in_=posj[:, :]),
               inc="st", k=16)
        pd.add("sync", lambda e: e.dma_start(out=out_d.ap()[256:384, 0:32], in_=cfs[:, :]),
               inc="st", k=16)
        pd.add("sync", lambda e: e.dma_start(out=out_d.ap()[384:512, 0:16].rearrange("p c -> p c"), in_=tot16[:, :].rearrange("a b -> a b") if False else cnts[:, :]),
               inc="st", k=16) if False else None
        pd.add("sync", lambda e: e.dma_start(out=out_d.ap()[384:392, 0:8], in_=cnts[:, :]),
               inc="st", k=16)
        pd.add("sync", lambda e: e.dma_start(
            out=stg[:, 0:36], in_=bass.AP(t_in, 0, [[36, 128], [1, 36]])),
            inc="st", k=16)
        pd.add("sync", lambda e: e.dma_start(out=out_d.ap()[512:640, 0:36], in_=stg[:, 0:36]),
               waits=[("st", now("st"))], inc="st", k=16)
        pd.add("sync", lambda e: e.dma_start(
            out=stg[:, 0:36], in_=bass.AP(t_out, 0, [[36, 128], [1, 36]])),
            waits=[("st", now("st"))], inc="st", k=16)
        pd.add("sync", lambda e: e.dma_start(out=out_d.ap()[640:768, 0:36], in_=stg[:, 0:36]),
               waits=[("st", now("st"))], inc="st", k=16)
        pd.add("sync", lambda e: e.dma_start(
            out=stg[:, 0:36], in_=bass.AP(t_out, THALF, [[36, 128], [1, 36]])),
            waits=[("st", now("st"))], inc="st", k=16)
        pd.add("sync", lambda e: e.dma_start(out=out_d.ap()[768:896, 0:36], in_=stg[:, 0:36]),
               waits=[("st", now("st"))], inc="st", k=16)
        pd.add("vector", lambda e: e.tensor_copy(rt[:, 0:32], sidu[:, :]),
               waits=[("st", now("st"))], inc="ve")
        pd.add("sync", lambda e: e.dma_start(out=out_d.ap()[896:1024, 0:32], in_=rt[:, 0:32]),
               waits=[("ve", now("ve"))], inc="st", k=16)
        pd.add("vector", lambda e: e.tensor_copy(rt[:, 32:64], sidg[:, :]),
               waits=[("st", now("st"))], inc="ve")
        pd.add("sync", lambda e: e.dma_start(out=out_d.ap()[896:1024, 32:64], in_=rt[:, 32:64]),
               waits=[("ve", now("ve"))], inc="st", k=16)
        pd.add("sync", lambda e: e.drain(), waits=[("st", now("st"))])
        pd.emit()
        esA3.close(); esA.close(); es.close()
        return nc

    esA3.close()
    esA.close()
    esB = ExitStack()
    b1c = sb(esB, "b1c", [128, 32], F32)
    b2b = sb(esB, "b2b", [1, D], BF16)
    w1b = sb(esB, "w1b", [128, DC * HID], BF16)
    w2b = sb(esB, "w2b", [128, 32 * D], BF16)
    xg = sb(esB, "xg", [128, 6 * D], BF16)
    xfm = sb(esB, "xfm", [128, 2 * DC * 512], BF16)
    hsb = sb(esB, "hsb", [128, 32 * 512], BF16)
    ysb = sb(esB, "ysb", [128, 2048], BF16)
    gbuf = sb(esB, "gbuf", [128, 2 * D], BF16)

    p = Plan()
    BND = {}
    def _mkregs(e):
        BND[2] = e.alloc_register("bnd2")
        e.reg_mov(BND[2], NCORE * 2048 - 1)
        BND[3] = e.alloc_register("bnd3")
        return e.reg_mov(BND[3], NCORE * 512 - 1)
    p.add("gpsimd", _mkregs)
    if stage == 38:
        p.add("gpsimd", lambda e: e.engine_nop())
        p.emit()
        esB.close(); es.close()
        return nc
    p.add("sync", lambda e: e.dma_start(
        out=stg[0:32, 0:128],
        in_=b1_d.ap()[0:1, :].rearrange("a (h p) -> (a h) p", h=32)),
        inc="wc3", k=16)
    B1ROW_WC = cnt["wc3"]
    p.add("tensor", lambda e: e.transpose(pun[:, 0:32], stg[0:32, 0:128], idfs[0:32, 0:32]),
          waits=[("wc3", B1ROW_WC)], inc="te")
    B1TR = cnt["te"]
    p.add("vector", lambda e: e.tensor_copy(b1c[:, :], pun[:, 0:32]),
          waits=[("te", B1TR)], inc="v3")
    B1CV = cnt["v3"]
    W1ST = {}
    for cch in range(4):
        for r in range(8):
            off = 4096 * r + 1024 * cch
            p.add("sync", lambda e, off=off: e.dma_start(
                out=w1b[:, off: off + 1024], in_=w1s.ap()[:, off: off + 1024]),
                waits=([("ws", 16 * 16), ("ws2", 16 * 16)] if (cch == 0 and r == 0) else []), inc="st", k=16)
        W1ST[cch] = cnt["st"]
    p.add("sync", lambda e: e.dma_start(out=stg[0:1, :], in_=b2_d.ap()[:, :]),
          waits=[("te", B1TR)], inc="ws", k=16)
    p.add("vector", lambda e: e.tensor_copy(b2b[:, :], stg[0:1, :]),
          waits=[("ws", now("ws"))], inc="v3")
    VW = now("v3")
    p.add("sync", lambda e: e.dma_start(out=w2b[:, :], in_=w2s.ap()[:, :]),
          inc="st", k=16)
    W2ST = cnt["st"]
    if stage == 39:
        p.add("gpsimd", lambda e: e.engine_nop(), waits=[("v3", VW)])
        p.emit()
        esB.close(); es.close()
        return nc

    S3B = cnt["s3"]

    def emit_ag(idx):
        off, ln, blk = [(0, 2048, 4), (2048 * D, 2048, 8), (4096 * D, 512, 9)][idx - 1]
        p.add("gpsimd", lambda e, off=off, ln=ln: e.collective_compute(
            "AllGather", mybir.AluOpType.bypass, replica_groups=[list(range(NCORE))],
            ins=[bass.AP(y_in, off, [[1, ln * D]])],
            outs=[[yo1, yo2, yo3][idx - 1].ap().opt()]),
            waits=[("s3", S3B + 32 * blk)], inc="cc")
        return cnt["cc"]

    bcntA = {}

    def emit_passA(pairs):
        for ii, jj in pairs:
            sl = (2 * ii + jj) % 2
            w = [("cc", CC_Y1)]
            if 2 * ii + jj >= 2:
                w.append(("g3", bcntA[2 * ii + jj - 2]))
            p.add("gpsimd", lambda e, ii=ii, jj=jj, sl=sl: e.indirect_dma_start(
                out=gbuf[:, 1024 * sl: 1024 * sl + 1024], out_offset=None,
                in_=yo1.ap()[:, :],
                in_offset=IOOA(ap=sidg[:, 16 * jj + ii: 16 * jj + ii + 1], axis=0)),
                waits=w, inc="g3", k=16)
            p.add("gpsimd", lambda e, ii=ii, jj=jj, sl=sl: e.dma_start(
                out=yab.ap()[256 * ii + 128 * jj: 256 * ii + 128 * jj + 128, :],
                in_=gbuf[:, 1024 * sl: 1024 * sl + 1024]),
                waits=[("g3", now("g3"))], inc="g3", k=16)
            bcntA[2 * ii + jj] = cnt["g3"]

    PAIRS = [(ii, jj) for ii in range(NT) for jj in range(2)]

    def xg_slice(st_):
        return xg[:, 1024 * (st_ % 6): 1024 * (st_ % 6) + 1024]

    import os as _os
    _ng40 = int(_os.environ.get("NG40", NST))
    XG3 = {}
    for st_ in range(NST if stage != 40 else _ng40):
        w = [("cc", CC_XT), ("ve", now("ve"))]
        if st_ >= 6 and stage != 40:
            w.append(("t3", 2 * (st_ - 5)))
        p.add("gpsimd", lambda e, st_=st_: e.indirect_dma_start(
            out=xg_slice(st_), out_offset=None,
            in_=xt_out.ap()[:, :], in_offset=IOOA(ap=xidxu[:, st_:st_ + 1], axis=0)),
            waits=w, inc="g3", k=16)
        XG3[st_] = cnt["g3"]
        if stage != 40:
            if st_ == 21:
                CC_Y1 = emit_ag(1)
            elif st_ == 25:
                emit_passA(PAIRS[0:16])
            elif st_ == 29:
                emit_passA(PAIRS[16:32])
            elif st_ == 35:
                CC_Y2 = emit_ag(2)
                CC_Y3 = emit_ag(3)

    if stage == 40:
        p.add("gpsimd", lambda e: e.engine_nop(), waits=[("g3", now("g3"))])
        p.emit()
        esB.close(); es.close()
        return nc
    se_gelu = {}
    t2_mm2 = {}
    for sbk in range(9):
        rot = sbk % 2
        for sti in range(4):
            st_ = 4 * sbk + sti
            for d4 in range(2):
                for j in range(4):
                    dd = 4 * d4 + j
                    w = []
                    if j == 0:
                        w = [("g3", XG3[st_])]
                        if sbk > 0 or sti > 0 or d4 > 0:
                            w.append(("v3", now("v3")))
                    p.add("tensor", lambda e, st_=st_, dd=dd, j=j: e.transpose(
                        pbf[:, 128 * j: 128 * j + 128],
                        xg_slice(st_)[:, 128 * dd: 128 * dd + 128], idbs[:, :]),
                        waits=w, inc=("t3" if j == 3 else None))
                wxe = [("t3", now("t3"))]
                if sbk >= 2 and sti == 0 and d4 == 0 and (sbk - 2, "mm1") in t2_mm2:
                    wxe.append(("t2", t2_mm2[(sbk - 2, "mm1")]))
                p.add("vector", lambda e, rot=rot, sti=sti, d4=d4: e.tensor_copy(
                    xfm[:, :].rearrange("p (r d c) -> p r d c", r=2, d=DC)[:, rot, 4 * d4:4 * d4 + 4, 128 * sti:128 * sti + 128],
                    pbf[:, :].rearrange("p (a c) -> p a c", a=4)),
                    waits=wxe, inc="v3")
        XP = now("v3")
        if stage == 41:
            continue
        for hc in range(32):
            hh = hc % 2
            for d in range(DC):
                w = []
                if d == 0:
                    w = [("v3", XP)]
                    if hc >= 2:
                        w.append(("se", se_gelu[(sbk, hc - 2)]))
                    elif sbk > 0:
                        w.append(("se", se_gelu[(sbk - 1, 30 + hc)]))
                    if sbk == 0:
                        w.append(("st", W1ST[hc // 8]))
                        if hc == 0:
                            w.append(("v3", B1CV))
                p.add("tensor", lambda e, hc=hc, d=d, rot=rot, hh=hh: e.matmul(
                    sps[0][:, 512 * hh: 512 * hh + 512],
                    w1b[:, 4096 * d + 128 * hc: 4096 * d + 128 * hc + 128],
                    xfm[:, 4096 * rot + 512 * d: 4096 * rot + 512 * d + 512],
                    start=(d == 0), stop=(d == DC - 1)),
                    waits=w, inc=("t2" if d == DC - 1 else None))
            wg2 = [("t2", now("t2"))]
            if sbk > 0 and (sbk - 1, "mm2") in t2_mm2 and hc == 0:
                wg2.append(("t2", t2_mm2[(sbk - 1, "mm2")]))
            p.add("scalar", lambda e, hc=hc, hh=hh: e.activation(
                hsb[:, 512 * hc: 512 * hc + 512], sps[0][:, 512 * hh: 512 * hh + 512],
                AF.Gelu_apprx_tanh, bias=b1c[:, hc:hc + 1]),
                waits=wg2, inc="se")
            se_gelu[(sbk, hc)] = cnt["se"]
        t2_mm2[(sbk, "mm1")] = cnt["t2"]
        for dh in range(2):
            for hc in range(32):
                for sti in range(4):
                    outp = [sps[1][:, 0:512], sps[1][:, 512:1024], pue[:, :], puo[:, :]][sti]
                    w = []
                    if hc == 0 and sti == 0:
                        w = [("se", se_gelu[(sbk, 31)]), ("v3", now("v3"))]
                        if sbk == 0 and dh == 0:
                            w.append(("v3", VW))
                            w.append(("st", W2ST))
                    p.add("tensor", lambda e, outp=outp, hc=hc, sti=sti, dh=dh: e.matmul(
                        outp,
                        hsb[:, 512 * hc + 128 * sti: 512 * hc + 128 * sti + 128],
                        w2b[:, 1024 * hc + 512 * dh: 1024 * hc + 512 * dh + 512],
                        start=(hc == 0), stop=False),
                        waits=w, inc=None)
            for sti in range(4):
                outp = [sps[1][:, 0:512], sps[1][:, 512:1024], pue[:, :], puo[:, :]][sti]
                p.add("tensor", lambda e, outp=outp, dh=dh: e.matmul(
                    outp, onesw[0:1, 0:128], b2b[0:1, 512 * dh: 512 * dh + 512],
                    start=False, stop=True),
                    inc=("t2" if sti == 3 else None))
            for sti in range(4):
                outp = [sps[1][:, 0:512], sps[1][:, 512:1024], pue[:, :], puo[:, :]][sti]
                p.add("vector", lambda e, outp=outp, sti=sti: e.tensor_copy(
                    ysb[:, 512 * sti: 512 * sti + 512], outp),
                    waits=([("t2", now("t2")), ("s3", now("s3"))] if sti == 0 else [("v3", now("v3"))]),
                    inc="v3")
            p.add("sync", lambda e, sbk=sbk, dh=dh: e.dma_start(
                out=y_in.ap()[512 * sbk: 512 * sbk + 512, 512 * dh: 512 * dh + 512].rearrange("(s p) c -> p s c", s=4),
                in_=ysb[:, :].rearrange("p (s c) -> p s c", s=4)),
                waits=[("v3", now("v3"))], inc="s3", k=16)
        t2_mm2[(sbk, "mm2")] = cnt["t2"]
    if stage == 41:
        p.add("gpsimd", lambda e: e.engine_nop(), waits=[("v3", now("v3"))])
        p.emit()
        esB.close(); es.close()
        return nc
    if stage == 4:
        p.add("gpsimd", lambda e: e.engine_nop(), waits=[("cc", CC_Y3), ("g3", now("g3"))])
        p.emit()
        esB.close(); es.close()
        return nc
    p.emit()

    esB.close()
    esC = ExitStack()
    ygt = sb(esC, "ygt", [128, 8 * D], BF16)
    xtb = sb(esC, "xtb", [128, 4 * D], F32)
    ob = sb(esC, "ob", [128, 4 * D], F32)

    p = Plan()
    V3B = cnt["v3"]
    for i in range(NT):
        rot = i % 4
        wld = [("g3", bcntA[2 * i + 1])]
        if i >= 4:
            wld.append(("v3", V3B + 2 * (i - 3)))
        p.add("sync", lambda e, i=i, rot=rot: e.dma_start(
            out=ygt[:, 2048 * rot: 2048 * rot + 2048],
            in_=yab.ap()[256 * i: 256 * i + 256, :].rearrange("(a p) d -> p a d", a=2)),
            waits=wld, inc="st", k=16)
        ldc = cnt["st"]
        for j in range(2):
            p.add("gpsimd", lambda e, i=i, j=j, rot=rot: e.indirect_dma_start(
                out=ygt[:, 2048 * rot + 1024 * j: 2048 * rot + 1024 * j + 1024],
                out_offset=None, in_=yo2.ap()[:, :],
                in_offset=IOOA(ap=sidg[:, 32 + 16 * j + i: 33 + 16 * j + i], axis=0),
                bounds_check=BND[2], oob_is_err=False),
                waits=[("cc", CC_Y2), ("st", ldc), ("g3", now("g3"))], inc="g3", k=16)
            p.add("gpsimd", lambda e, i=i, j=j, rot=rot: e.indirect_dma_start(
                out=ygt[:, 2048 * rot + 1024 * j: 2048 * rot + 1024 * j + 1024],
                out_offset=None, in_=yo3.ap()[:, :],
                in_offset=IOOA(ap=sidg[:, 64 + 16 * j + i: 65 + 16 * j + i], axis=0),
                bounds_check=BND[3], oob_is_err=False),
                waits=[("cc", CC_Y3), ("g3", now("g3"))], inc="g3", k=16)
        gcc = cnt["g3"]
        p.add("sync", lambda e, i=i, rot=rot: e.dma_start(
            out=xtb[:, 1024 * rot: 1024 * rot + 1024],
            in_=xspill.ap()[128 * i: 128 * i + 128, :]),
            waits=([("v3", V3B + 2 * (i - 3))] if i >= 4 else []), inc="st", k=16)
        p.add("vector", lambda e, i=i, rot=rot: e.scalar_tensor_tensor(
            ob[:, 1024 * rot: 1024 * rot + 1024],
            ygt[:, 2048 * rot: 2048 * rot + 1024], cfs[:, i:i + 1],
            xtb[:, 1024 * rot: 1024 * rot + 1024], AO.mult, AO.add),
            waits=[("g3", gcc), ("st", now("st")), ("v3", now("v3"))], inc="v3")
        p.add("vector", lambda e, i=i, rot=rot: e.scalar_tensor_tensor(
            ob[:, 1024 * rot: 1024 * rot + 1024],
            ygt[:, 2048 * rot + 1024: 2048 * rot + 2048], cfs[:, 16 + i:17 + i],
            ob[:, 1024 * rot: 1024 * rot + 1024], AO.mult, AO.add),
            waits=[("v3", now("v3"))], inc="v3")
        p.add("sync", lambda e, i=i, rot=rot: e.dma_start(
            out=out_d.ap()[128 * i: 128 * i + 128, :],
            in_=ob[:, 1024 * rot: 1024 * rot + 1024]),
            waits=[("v3", now("v3"))], inc="io", k=16)
    p.add("sync", lambda e: e.drain(), waits=[("io", now("io"))])
    p.emit()

    esC.close(); es.close()
    return nc


_CACHE = {}


def _get_nc(stage=STAGE):
    if stage not in _CACHE:
        _CACHE[stage] = _build(stage)
    return _CACHE[stage]


def make_consts(core):
    import ml_dtypes
    bf = ml_dtypes.bfloat16
    tri = (np.arange(128)[:, None] < np.arange(128)[None, :]).astype(bf)
    trib = np.zeros((16, 2048), bf)
    for i in range(16):
        trib[:, 128 * i:128 * i + 128] = (np.arange(16)[:, None] < i).astype(bf)
    pmask = np.broadcast_to((np.arange(8) < core)[:, None], (8, 128)).astype(bf)
    bmask = ((np.arange(8) // 4) == (core // 4)).astype(np.float32)[:, None]
    eoh = (np.arange(8) == core).astype(np.float32)[:, None]
    iota8 = np.broadcast_to(np.tile(np.arange(8, dtype=np.float32), 16)[None, :], (128, 128)).copy()
    tokrow = (TOK * core + 128 * np.arange(16, dtype=np.float32)[None, :]
              + np.arange(128, dtype=np.float32)[:, None]).astype(np.float32)
    garbA = np.broadcast_to((GARB + np.arange(16, dtype=np.float32))[None, :], (128, 16)).copy()
    garbB = garbA + 16
    return dict(
        idf=np.eye(128, dtype=np.float32),
        idb=np.eye(128, dtype=np.float32).astype(bf),
        tri=np.ascontiguousarray(tri),
        trib=trib,
        pmask=np.ascontiguousarray(pmask),
        bmask=bmask, eoh=eoh, iota8=iota8, tokrow=np.ascontiguousarray(tokrow),
        garbA=garbA, garbB=garbB,
    )


def kernel(L0_attn_x, L0_attn_Wq, L0_attn_Wk, L0_attn_Wv,
           L0_ffn_Wg, L0_ffn_W1, L0_ffn_b1, L0_ffn_W2, L0_ffn_b2):
    from concourse.bass_utils import run_bass_kernel_spmd
    x = np.ascontiguousarray(np.asarray(L0_attn_x, dtype=np.float32)).reshape(NCORE * TOK, D)
    wq = np.ascontiguousarray(np.asarray(L0_attn_Wq, dtype=np.float32))
    wk = np.ascontiguousarray(np.asarray(L0_attn_Wk, dtype=np.float32))
    wv = np.ascontiguousarray(np.asarray(L0_attn_Wv, dtype=np.float32))
    wg = np.ascontiguousarray(np.asarray(L0_ffn_Wg, dtype=np.float32))
    w1 = np.asarray(L0_ffn_W1, dtype=np.float32)
    b1 = np.asarray(L0_ffn_b1, dtype=np.float32)
    w2 = np.asarray(L0_ffn_W2, dtype=np.float32)
    b2 = np.asarray(L0_ffn_b2, dtype=np.float32)
    nc = _get_nc(STAGE)
    in_maps = []
    for c in range(NCORE):
        m = dict(xin=np.ascontiguousarray(x[TOK * c: TOK * (c + 1)]),
                 wq=wq, wk=wk, wv=wv, wg=wg,
                 w1=np.ascontiguousarray(w1[c]),
                 b1=np.ascontiguousarray(b1[c])[None, :],
                 w2=np.ascontiguousarray(w2[c]),
                 b2=np.ascontiguousarray(b2[c])[None, :])
        m.update(make_consts(c))
        in_maps.append(m)
    res = run_bass_kernel_spmd(nc, in_maps, core_ids=list(range(NCORE)))
    out = np.concatenate([np.asarray(res.results[c]["out"]) for c in range(NCORE)], 0)
    return out.reshape(2, 8192, D).astype(np.float32)


# version 34
# speedup vs baseline: 1.0266x; 1.0114x over previous
import numpy as np
from contextlib import ExitStack

TOK = 2048
D = 1024
DC = 8
NG = 4
GS = 512
NT = 16
NCORE = 8
E = 8
HID = 4096
CAP = 4301
CAPPAD = 4608
NST = CAPPAD // 128
THALF = CAPPAD * 4 + 64
GARB = CAPPAD * 4
TSZP = 2 * THALF
GN_N = float(8192 * 1024)
EPS = 1e-5
NPAIR = 8
NSTEP = 64

STAGE = 9


def _build(stage=STAGE):
    import concourse.bass as bass
    import concourse.mybir as mybir
    from concourse.alu_op_type import AluOpType as AO
    F32 = mybir.dt.float32
    BF16 = mybir.dt.bfloat16
    U32 = mybir.dt.uint32
    AF = mybir.ActivationFunctionType
    AX = mybir.AxisListType
    IOOA = bass.IndirectOffsetOnAxis

    nc = bass.Bass("TRN2", target_bir_lowering=False)

    xin = nc.dram_tensor("xin", [TOK, D], F32, kind="ExternalInput")
    wq_d = nc.dram_tensor("wq", [D, D], F32, kind="ExternalInput")
    wk_d = nc.dram_tensor("wk", [D, D], F32, kind="ExternalInput")
    wv_d = nc.dram_tensor("wv", [D, D], F32, kind="ExternalInput")
    wg_d = nc.dram_tensor("wg", [D, E], F32, kind="ExternalInput")
    w1_d = nc.dram_tensor("w1", [D, HID], F32, kind="ExternalInput")
    b1_d = nc.dram_tensor("b1", [1, HID], F32, kind="ExternalInput")
    w2_d = nc.dram_tensor("w2", [HID, D], F32, kind="ExternalInput")
    b2_d = nc.dram_tensor("b2", [1, D], F32, kind="ExternalInput")
    idf_d = nc.dram_tensor("idf", [128, 128], F32, kind="ExternalInput")
    idb_d = nc.dram_tensor("idb", [128, 128], BF16, kind="ExternalInput")
    tri_d = nc.dram_tensor("tri", [128, 128], BF16, kind="ExternalInput")
    trib_d = nc.dram_tensor("trib", [16, 2048], BF16, kind="ExternalInput")
    pmask_d = nc.dram_tensor("pmask", [8, 128], BF16, kind="ExternalInput")
    bmask_d = nc.dram_tensor("bmask", [8, 1], F32, kind="ExternalInput")
    eoh_d = nc.dram_tensor("eoh", [8, 1], F32, kind="ExternalInput")
    iota8_d = nc.dram_tensor("iota8", [128, 128], F32, kind="ExternalInput")
    tokrow_d = nc.dram_tensor("tokrow", [128, 16], F32, kind="ExternalInput")
    garbA_d = nc.dram_tensor("garbA", [128, 16], F32, kind="ExternalInput")
    garbB_d = nc.dram_tensor("garbB", [128, 16], F32, kind="ExternalInput")
    out_d = nc.dram_tensor("out", [TOK, D], F32, kind="ExternalOutput")
    dbgA = dbgB = None
    if stage < 9:
        dbgA = nc.dram_tensor("dbgA", [128, DC * TOK], F32, kind="ExternalOutput")
        dbgB = nc.dram_tensor("dbgB", [128, DC * TOK], F32, kind="ExternalOutput")
    st_in = nc.dram_tensor("st_in", [1, 2], F32)
    st_out = nc.dram_tensor("st_out", [8, 2], F32)
    cnt_in = nc.dram_tensor("cnt_in", [1, 8], F32)
    cnt_out = nc.dram_tensor("cnt_out", [8, 8], F32)
    t_in = nc.dram_tensor("t_in", [TSZP], F32)
    t_out = nc.dram_tensor("t_out", [TSZP], F32)
    xt_in = nc.dram_tensor("xt_in", [TOK, D], BF16)
    xt_out = nc.dram_tensor("xt_out", [NCORE * TOK, D], BF16)
    y_in = nc.dram_tensor("y_in", [CAPPAD, D], BF16)
    yo1 = nc.dram_tensor("yo1", [NCORE * 2048, D], BF16)
    yo2 = nc.dram_tensor("yo2", [NCORE * 2048, D], BF16)
    yo3 = nc.dram_tensor("yo3", [NCORE * 512, D], BF16)
    yab = nc.dram_tensor("yab", [2 * TOK, D], BF16)
    w1s = nc.dram_tensor("w1s", [128, DC * HID], BF16)
    w2s = nc.dram_tensor("w2s", [128, 32 * D], BF16)
    xspill = nc.dram_tensor("xspill", [TOK, D], F32)
    tex_d = nc.dram_tensor("texd", [CAPPAD], F32)

    es = ExitStack()
    esA = ExitStack()

    def sb(stk, name, shape, dt):
        return stk.enter_context(nc.sbuf_tensor(name, shape, dt))

    stg = sb(es, "stg", [128, 1024], F32)
    onesw = sb(es, "onesw", [1, 512], BF16)
    idbs = sb(es, "idbs", [128, 128], BF16)
    sidg = sb(es, "sidg", [128, 96], U32)
    cfs = sb(es, "cfs", [128, 32], F32)
    xidxf = sb(es, "xidxf", [128, 36], F32)
    xidxu = sb(es, "xidxu", [128, 36], U32)

    xl = sb(esA, "xl", [128, DC * TOK], BF16)
    xf = sb(esA, "xf", [128, DC * TOK], F32)
    wqT = sb(esA, "wqT", [128, DC * D], BF16)
    wkT = sb(esA, "wkT", [128, DC * D], BF16)
    wvT = sb(esA, "wvT", [128, DC * D], BF16)
    stgb = sb(esA, "stgb", [128, 1024], BF16)
    usb = sb(esA, "usb", [128, 1024], F32)
    rcp = sb(esA, "rcp", [1, 1024], F32)
    t1s = sb(esA, "t1s", [128, 512], F32)
    murs = sb(esA, "murs", [1, 1024], BF16)
    sc2 = sb(esA, "sc2", [128, 2], F32)
    rsm = sb(esA, "rsm", [128, 2], F32)
    ssb = sb(esA, "ssb", [8, 2], F32)
    cnts = sb(esA, "cnts", [8, 8], F32)
    cntsb = sb(esA, "cntsb", [8, 8], BF16)
    tot16 = sb(esA, "tot16", [16, 8], F32)
    tot16b = sb(esA, "tot16b", [16, 8], BF16)
    sg1 = sb(esA, "sg1", [4, 16], F32)
    onesc = sb(esA, "onesc", [128, 1], BF16)
    onescf = sb(esA, "onescf", [128, 1], F32)
    onesr = sb(esA, "onesr", [1, 128], BF16)
    onesrf = sb(esA, "onesrf", [1, 128], F32)
    garbv = sb(esA, "garbv", [128, 1], F32)
    epsv = sb(esA, "epsv", [1, 1], F32)
    idfs = sb(esA, "idfs", [128, 128], F32)
    tris = sb(esA, "tris", [128, 128], BF16)
    pms = sb(esA, "pms", [8, 128], BF16)
    bms = sb(esA, "bms", [8, 1], F32)
    eohs = sb(esA, "eohs", [8, 1], F32)
    io8 = sb(esA, "io8", [128, 128], F32)
    tkr = sb(esA, "tkr", [128, 16], F32)
    garbA = sb(esA, "garbAs", [128, 16], F32)
    garbB = sb(esA, "garbBs", [128, 16], F32)
    wgb = sb(esA, "wgb", [128, DC * E], BF16)
    rt = sb(esA, "rt", [128, 128], F32)
    r1 = sb(esA, "r1", [128, 16], F32)
    r2 = sb(esA, "r2", [128, 16], F32)
    rg1 = sb(esA, "rg1", [128, 16], F32)
    rg2 = sb(esA, "rg2", [128, 16], F32)
    re1 = sb(esA, "re1", [128, 128], F32)
    re2 = sb(esA, "re2", [128, 128], F32)
    roh = sb(esA, "roh", [128, 128], BF16)
    rsc = sb(esA, "rsc", [128, 8], F32)
    rsd = sb(esA, "rsd", [128, 8], F32)
    rse = sb(esA, "rse", [128, 8], F32)
    posj = sb(esA, "posj", [128, 64], F32)
    tokv = sb(esA, "tokv", [128, 16], F32)
    sidu = sb(esA, "sidu", [128, 32], U32)

    esA2 = ExitStack()
    ka = sb(esA2, "ka", [128, DC * GS], BF16)
    kb = sb(esA2, "kb", [128, DC * GS], BF16)
    qa = sb(esA2, "qa", [128, DC * GS], BF16)
    va = sb(esA2, "va", [128, 4 * 1040], BF16)
    exb = sb(esA2, "exb", [128, 3 * 1024], BF16)

    sps = [es.enter_context(nc.psum_tensor("sps0", [128, 1024], F32)),
           es.enter_context(nc.psum_tensor("sps1", [128, 1024], F32))]
    pue = es.enter_context(nc.psum_tensor("pue", [128, 512], F32))
    puo = es.enter_context(nc.psum_tensor("puo", [128, 512], F32))
    pbf = es.enter_context(nc.psum_tensor("pbf", [128, 512], BF16))
    pun = es.enter_context(nc.psum_tensor("pun", [128, 512], F32))

    snames = ["ld", "te", "ve", "se", "gp", "pm", "mm", "exq", "uc", "fin", "rc",
              "bc", "yd", "pe", "cc", "st", "g2", "v2", "t2", "s2", "v3", "t3",
              "g3", "s3", "io", "wc", "ws", "wv", "wc2", "ws2", "wc3"]
    sems = {n: es.enter_context(nc.semaphore("s_" + n)) for n in snames}
    cnt = {n: 0 for n in snames}

    class Plan:
        def __init__(self):
            self.ev = {k: [] for k in ("sync", "tensor", "vector", "scalar", "gpsimd")}

        def add(self, eng, fn, waits=(), inc=None, k=1):
            self.ev[eng].append((fn, list(waits), inc, k))
            if inc is not None:
                cnt[inc] += k

        def emit(self):
            with nc.Block() as block:
                for name in ("sync", "tensor", "vector", "scalar", "gpsimd"):
                    evs = self.ev[name]
                    if not evs:
                        continue

                    def make(evs):
                        def emit_fn(eng):
                            for fn, w, inc, k in evs:
                                for s, v in w:
                                    eng.wait_ge(sems[s], v)
                                ins = fn(eng)
                                if inc is not None:
                                    ins.then_inc(sems[inc], k)
                        return emit_fn
                    getattr(block, name)(make(evs))

    def now(s):
        return cnt[s]

    def xfc(d, c0, n):
        return xf[:, 2048 * d + c0: 2048 * d + c0 + n]

    def xlc(d, c0, n):
        return xl[:, 2048 * d + c0: 2048 * d + c0 + n]

    p = Plan()
    for t, d in [(idfs, idf_d), (idbs, idb_d), (tris, tri_d),
                 (pms, pmask_d), (bms, bmask_d), (eohs, eoh_d), (io8, iota8_d),
                 (tkr, tokrow_d), (garbA, garbA_d), (garbB, garbB_d)]:
        p.add("sync", lambda e, t=t, d=d: e.dma_start(out=t[:], in_=d.ap()[:, :]), inc="ld", k=16)
    NCONST = cnt["ld"]
    for mset in [lambda e: e.memset(onesc[:], 1.0), lambda e: e.memset(onescf[:], 1.0),
                 lambda e: e.memset(onesr[:], 1.0), lambda e: e.memset(onesrf[:], 1.0),
                 lambda e: e.memset(onesw[:], 1.0),
                 lambda e: e.memset(garbv[:], float(GARB)),
                 lambda e: e.memset(epsv[:], EPS),
                 lambda e: e.memset(va[:, 64::65], 1.0), lambda e: e.memset(usb[:], 0.0)]:
        p.add("gpsimd", mset, inc="gp")
    NGP = cnt["gp"]

    for i in range(NT):
        p.add("sync", lambda e, i=i: e.dma_start(out=stg[:, :], in_=xin.ap()[128 * i:128 * i + 128, :]),
              waits=([("te", 2 * i)] if i >= 1 else []), inc="ld", k=16)
        for d4 in range(2):
            for j in range(4):
                dd = 4 * d4 + j
                w = []
                if j == 0:
                    w = [("ld", NCONST + 16 * (i + 1))]
                    if i >= 1:
                        w.append(("ve", 2 * (i - 1) + d4 + 1))
                p.add("tensor", lambda e, d4=d4, j=j, dd=dd: e.transpose(
                    sps[0][:, 512 * d4 + 128 * j: 512 * d4 + 128 * j + 128],
                    stg[:, 128 * dd: 128 * dd + 128], idfs[:, :]),
                    waits=w, inc=("te" if j == 3 else None))
            p.add("vector", lambda e, d4=d4, i=i: e.tensor_copy(
                xf[:, :].rearrange("p (d i c) -> p d i c", d=DC, i=NT, c=128)[:, 4 * d4:4 * d4 + 4, i, :],
                sps[0][:, 512 * d4: 512 * d4 + 512].rearrange("p (a c) -> p a c", a=4)),
                waits=[("te", 2 * i + d4 + 1)], inc="ve")
    TE_XIN = cnt["te"]

    CAST_VE = {}
    for wi, (wd, wt) in enumerate([(wq_d, wqT), (wk_d, wkT), (wv_d, wvT)]):
        for r in range(8):
            idx = 8 * wi + r
            sbuf = stgb[:, 0:1024] if idx % 2 == 0 else exb[:, 0:1024]
            wldw = [("te", TE_XIN)] if idx == 0 else [("ve", CAST_VE[idx - 1])]
            p.add("sync", lambda e, wd=wd, r=r: e.dma_start(
                out=stg[:, :], in_=wd.ap()[128 * r:128 * r + 128, :]),
                waits=wldw, inc="ld", k=16)
            wcst = [("ld", now("ld"))]
            if idx >= 2:
                wcst.append(("te", TE_XIN + 2 * (idx - 1)))
            p.add("vector", lambda e, sbuf=sbuf: e.tensor_copy(sbuf, stg[:, :]),
                  waits=wcst, inc="ve")
            cast_ve = now("ve")
            CAST_VE[idx] = cast_ve
            for d4 in range(2):
                for j in range(4):
                    dd = 4 * d4 + j
                    w = []
                    if j == 0:
                        w = [("ve", cast_ve), ("ve", now("ve"))]
                    p.add("tensor", lambda e, dd=dd, j=j, wi=wi, r=r: e.transpose(
                        pbf[:, 128 * j: 128 * j + 128],
                        (stgb if (8 * wi + r) % 2 == 0 else exb)[:, 128 * dd: 128 * dd + 128], idbs[:, :]),
                        waits=w, inc=("te" if j == 3 else None))
                p.add("vector", lambda e, d4=d4, r=r, wt=wt: e.tensor_copy(
                    wt[:, :].rearrange("p (d c) -> p d c", d=DC)[:, 4 * d4:4 * d4 + 4, 128 * r:128 * r + 128],
                    pbf[:, :].rearrange("p (a c) -> p a c", a=4)),
                    waits=[("te", now("te"))], inc="ve")
    p.add("sync", lambda e: e.dma_start(
        out=stg[:, 0:64].rearrange("p (d c) -> p d c", d=DC),
        in_=wg_d.ap()[:, :].rearrange("(d p) c -> p d c", d=DC)),
        waits=[("te", now("te"))], inc="ld", k=16)
    p.add("vector", lambda e: e.tensor_copy(wgb[:, :], stg[:, 0:64]),
          waits=[("ld", now("ld")), ("te", now("te"))], inc="ve")

    def emit_ln(p):
        for tt in range(4):
            c0 = 512 * tt
            for d in range(DC):
                w_cast = [("te", now("te")), ("ve", now("ve"))]
                if tt == 0 and d == 0:
                    w_cast.append(("gp", NGP))
                p.add("vector", lambda e, d=d, c0=c0: e.tensor_copy(stgb[:, 0:512], xfc(d, c0, 512)),
                      waits=w_cast, inc="ve")
                p.add("scalar", lambda e, d=d, c0=c0: e.activation(
                    stgb[:, 512:1024], xfc(d, c0, 512), AF.Square),
                    waits=[("te", now("te"))], inc="se")
                p.add("tensor", lambda e, d=d: e.matmul(
                    pue[0:1, :], onesc[:, :], stgb[:, 0:512], start=(d == 0), stop=(d == DC - 1)),
                    waits=[("ve", now("ve")), ("se", now("se"))] +
                          ([("ve", now("ve"))] if (tt > 0 and d == 0) else []),
                    inc=None)
                p.add("tensor", lambda e, d=d: e.matmul(
                    puo[0:1, :], onesc[:, :], stgb[:, 512:1024], start=(d == 0), stop=(d == DC - 1)),
                    inc="te")
            p.add("vector", lambda e, c0=c0: e.tensor_single_scalar(murs[0:1, 0:512], pue[0:1, :], 1.0 / D, AO.mult),
                waits=[("te", now("te"))], inc="ve")
            p.add("vector", lambda e: e.tensor_single_scalar(t1s[0:1, 0:512], puo[0:1, :], 1.0 / D, AO.mult),
                waits=[("ve", now("ve"))], inc="ve")
            p.add("scalar", lambda e, c0=c0: e.activation(
                rcp[0:1, 0:512], murs[0:1, 0:512], AF.Square),
                waits=[("ve", now("ve"))], inc="se")
            p.add("vector", lambda e: e.tensor_tensor(
                t1s[0:1, 0:512], t1s[0:1, 0:512], rcp[0:1, 0:512], AO.subtract),
                waits=[("se", now("se")), ("ve", now("ve"))], inc="ve")
            p.add("scalar", lambda e: e.activation(
                t1s[64:65, 0:512], t1s[0:1, 0:512], AF.Sqrt, bias=epsv[0:1, 0:1]),
                waits=[("ve", now("ve"))], inc="se")
            p.add("vector", lambda e: e.reciprocal(t1s[96:97, 0:512], t1s[64:65, 0:512]),
                  waits=[("se", now("se"))], inc="ve")
            p.add("vector", lambda e, c0=c0: e.tensor_copy(murs[0:1, 512:1024], t1s[96:97, 0:512]),
                  waits=[("ve", now("ve"))], inc="ve")
            p.add("tensor", lambda e, c0=c0: e.matmul(
                sps[0][:, 0:512], onesr[:, :], murs[0:1, 0:512], start=True, stop=True),
                waits=[("ve", now("ve"))], inc=None)
            p.add("tensor", lambda e, c0=c0: e.matmul(
                sps[0][:, 512:1024], onesr[:, :], murs[0:1, 512:1024], start=True, stop=True),
                inc="te")
            for d in range(DC):
                p.add("vector", lambda e, d=d, c0=c0: e.tensor_tensor(
                    t1s[:, 0:512], xfc(d, c0, 512), sps[0][:, 0:512], AO.subtract),
                    waits=[("te", now("te")), ("ve", now("ve"))], inc="ve")
                p.add("vector", lambda e, d=d, c0=c0: e.tensor_tensor(
                    xlc(d, c0, 512), t1s[:, 0:512], sps[0][:, 512:1024], AO.mult),
                    waits=[("ve", now("ve"))], inc="ve")

    emit_ln(p)
    p.emit()

    def emit_dump(srcA_f32, srcB_bf16):
        pd = Plan()
        if srcA_f32 is not None:
            pd.add("sync", lambda e: e.dma_start(out=dbgA.ap()[:, :], in_=srcA_f32[:, :]),
                   inc="st", k=16)
        for ch in range(16):
            pd.add("vector", lambda e, ch=ch: e.tensor_copy(
                stg[:, :], srcB_bf16[:, 1024 * ch: 1024 * ch + 1024]),
                waits=[("st", now("st"))], inc="ve")
            pd.add("sync", lambda e, ch=ch: e.dma_start(
                out=dbgB.ap()[:, 1024 * ch: 1024 * ch + 1024], in_=stg[:, :]),
                waits=[("ve", now("ve"))], inc="st", k=16)
        pd.add("sync", lambda e: e.dma_start(out=out_d.ap()[0:128, :], in_=stg[:, :]),
               waits=[("st", now("st"))], inc="st", k=16)
        pd.add("vector", lambda e: e.tensor_copy(t1s[0:1, 0:1], t1s[0:1, 0:1]),
               waits=[("st", now("st"))])
        pd.emit()

    if stage == 1:
        emit_dump(xf, xl)
        esA2.close(); esA.close(); es.close()
        return nc

    p = Plan()
    pslots = [(0, 0), (0, 1), (1, 0), (1, 1)]

    def proj_psum(slot):
        si, hh = pslots[slot % 4]
        return sps[si][:, 512 * hh: 512 * hh + 512]

    def emit_u(gstep, first, last):
        par3 = gstep % 3
        k = gstep % NSTEP
        ck, tkb = k // 4, k % 4
        w = [("exq", gstep + 1)]
        if first:
            w.append(("fin", gstep // NSTEP))
        p.add("tensor", lambda e, ck=ck, tkb=tkb, par3=par3, first=first, last=last: e.matmul(
            pue[0:65, :], va[:, 1040 * tkb + 65 * ck: 1040 * tkb + 65 * ck + 65],
            exb[:, 1024 * par3: 1024 * par3 + 512], start=first, stop=last),
            waits=w, inc=None)
        p.add("tensor", lambda e, ck=ck, tkb=tkb, par3=par3, first=first, last=last: e.matmul(
            puo[0:65, :], va[:, 1040 * tkb + 65 * ck: 1040 * tkb + 65 * ck + 65],
            exb[:, 1024 * par3 + 512: 1024 * par3 + 1024], start=first, stop=last),
            inc="uc")

    nproj = 0
    MM_END = 0
    pending_fin = [None]
    for g in range(NG):
        gc = 512 * g
        for kind in range(3):
            for u in range(8):
                slot = nproj % 4
                w0 = []
                if nproj >= 4:
                    w0.append(("pe", cnt["pe"] - 3))
                if g > 0 and kind == 0 and u == 0:
                    w0.append(("exq", NSTEP * NPAIR * g))
                for d in range(DC):
                    if kind == 0:
                        fn = lambda e, u=u, d=d, gc=gc, slot=slot: e.matmul(
                            proj_psum(slot), wqT[:, 1024 * d + 128 * u: 1024 * d + 128 * u + 128],
                            xl[:, 2048 * d + gc: 2048 * d + gc + 512],
                            start=(d == 0), stop=(d == DC - 1))
                    elif kind == 1:
                        fn = lambda e, u=u, d=d, gc=gc, slot=slot: e.matmul(
                            proj_psum(slot), wkT[:, 1024 * d + 128 * u: 1024 * d + 128 * u + 128],
                            xl[:, 2048 * d + gc: 2048 * d + gc + 512],
                            start=(d == 0), stop=(d == DC - 1))
                    else:
                        tc, dvh = u // 2, u % 2
                        fn = lambda e, tc=tc, dvh=dvh, d=d, gc=gc, slot=slot: e.matmul(
                            proj_psum(slot), xl[:, 2048 * d + gc + 128 * tc: 2048 * d + gc + 128 * tc + 128],
                            wvT[:, 1024 * d + 512 * dvh: 1024 * d + 512 * dvh + 512],
                            start=(d == 0), stop=(d == DC - 1))
                    p.add("tensor", fn, waits=(w0 if d == 0 else []),
                          inc=("pm" if d == DC - 1 else None))
                wev = [("pm", now("pm"))]
                if g > 0:
                    wev.append(("mm", MM_END) if kind != 2 else ("uc", NSTEP * NPAIR * g))
                if kind == 0:
                    p.add("vector", lambda e, u=u, slot=slot: e.tensor_copy(
                        qa[:, 512 * u: 512 * u + 512], proj_psum(slot)),
                        waits=wev, inc="pe")
                elif kind == 1:
                    p.add("vector", lambda e, u=u, slot=slot: e.tensor_copy(
                        ka[:, 512 * u: 512 * u + 512], proj_psum(slot)),
                        waits=wev, inc="v2")
                    p.add("vector", lambda e, u=u, slot=slot: e.tensor_copy(
                        kb[64:128, 512 * u: 512 * u + 512], proj_psum(slot)[0:64, :]),
                        waits=[("v2", now("v2"))], inc="v2")
                    p.add("vector", lambda e, u=u, slot=slot: e.tensor_copy(
                        kb[0:64, 512 * u: 512 * u + 512], proj_psum(slot)[64:128, :]),
                        waits=[("v2", now("v2"))], inc="pe")
                else:
                    tc, dvh = u // 2, u % 2
                    p.add("vector", lambda e, tc=tc, dvh=dvh, slot=slot: e.tensor_copy(
                        va[:, :].rearrange("p (t k c) -> p t k c", t=4, k=16, c=65)[:, tc, 8 * dvh:8 * dvh + 8, 0:64],
                        proj_psum(slot).rearrange("p (k c) -> p k c", k=8)),
                        waits=wev, inc="pe")
                nproj += 1
        PEG = cnt["pe"]

        for pr in range(NPAIR):
            P = NPAIR * g + pr
            b = pr
            for k in range(NSTEP):
                gstep = NSTEP * P + k
                ck, tkb = k // 4, k % 4
                kt, j2 = ck // 2, ck % 2
                par = gstep % 2
                wts = []
                if pr == 0 and k <= 1:
                    wts.append(("pe", PEG))
                if gstep >= 2:
                    wts.append(("exq", gstep - 1))
                top = ka if j2 == 0 else kb
                bot = kb if j2 == 0 else ka
                p.add("tensor", lambda e, top=top, kt=kt, tkb=tkb, b=b, par=par: e.matmul(
                    sps[par][:, 0:512],
                    top[0:64, 512 * kt + 128 * tkb: 512 * kt + 128 * tkb + 128],
                    qa[0:64, 512 * b: 512 * b + 512],
                    start=True, stop=True, tile_position=(0, 0)),
                    waits=wts, inc=None)
                p.add("tensor", lambda e, bot=bot, kt=kt, tkb=tkb, b=b, par=par: e.matmul(
                    sps[par][:, 512:1024],
                    bot[64:128, 512 * kt + 128 * tkb: 512 * kt + 128 * tkb + 128],
                    qa[64:128, 512 * b: 512 * b + 512],
                    start=True, stop=True, tile_position=(64, 0)),
                    inc="mm")
                if k >= 1:
                    emit_u(gstep - 1, first=(k == 1), last=False)
                if k == 12 and pending_fin[0] is not None:
                    pending_fin[0]()
                    pending_fin[0] = None
                ws = [("mm", now("mm"))]
                if gstep >= 3:
                    ws.append(("uc", gstep - 2))
                p.add("scalar", lambda e, par=par, slot3=gstep % 3: e.activation(
                    exb[:, 1024 * slot3: 1024 * slot3 + 1024], sps[par][:, :], AF.Exp),
                    waits=ws, inc="exq")
            emit_u(NSTEP * P + NSTEP - 1, first=False, last=True)
            p.add("vector", lambda e: e.tensor_copy(usb[0:65, 0:512], pue[0:65, :]),
                  waits=[("uc", NSTEP * (P + 1)), ("v2", now("v2"))], inc="v2")
            p.add("vector", lambda e: e.tensor_copy(usb[0:65, 512:1024], puo[0:65, :]),
                  waits=[("v2", now("v2"))], inc="fin")

            def mk_finb(P=P, b=b, gc=gc):
                def finb():
                    p.add("vector", lambda e: e.reciprocal(rcp[0:1, 0:512], usb[64:65, 0:512]),
                          waits=[("fin", P + 1), ("bc", 2 * P), ("v2", now("v2"))], inc="v2")
                    p.add("vector", lambda e: e.reciprocal(rcp[0:1, 512:1024], usb[64:65, 512:1024]),
                          waits=[("v2", now("v2"))], inc="rc")
                    p.add("tensor", lambda e: e.matmul(
                        pun[:, :], onesrf[:, :], rcp[0:1, 0:512], start=True, stop=True),
                        waits=[("rc", P + 1), ("yd", P)], inc="bc")
                    p.add("vector", lambda e: e.tensor_tensor(
                        t1s[0:64, :], usb[0:64, 0:512], pun[0:64, :], AO.mult),
                        waits=[("bc", now("bc")), ("v2", now("v2"))], inc="v2")
                    p.add("vector", lambda e: e.tensor_tensor(
                        xl[0:64, 2048 * b + gc: 2048 * b + gc + 512],
                        t1s[0:64, :], xl[0:64, 2048 * b + gc: 2048 * b + gc + 512], AO.add),
                        waits=[("v2", now("v2"))], inc="v2")
                    p.add("tensor", lambda e: e.matmul(
                        pun[:, :], onesrf[:, :], rcp[0:1, 512:1024], start=True, stop=True),
                        waits=[("v2", now("v2"))], inc="bc")
                    p.add("vector", lambda e: e.tensor_copy(t1s[64:128, :], usb[0:64, 512:1024]),
                          waits=[("bc", now("bc")), ("v2", now("v2"))], inc="v2")
                    p.add("vector", lambda e: e.tensor_tensor(
                        t1s[64:128, :], t1s[64:128, :], pun[64:128, :], AO.mult),
                        waits=[("v2", now("v2"))], inc="v2")
                    p.add("vector", lambda e, b=b, gc=gc: e.tensor_tensor(
                        xl[64:128, 2048 * b + gc: 2048 * b + gc + 512],
                        t1s[64:128, :], xl[64:128, 2048 * b + gc: 2048 * b + gc + 512], AO.add),
                        waits=[("v2", now("v2"))], inc="yd")
                return finb
            if P == NG * NPAIR - 1:
                mk_finb()()
            else:
                pending_fin[0] = mk_finb()
        MM_END = cnt["mm"]
    p.emit()
    esA2.close()

    if stage == 2:
        emit_dump(xf, xl)
        esA.close(); es.close()
        return nc

    esA3 = ExitStack()
    wcf = sb(esA3, "wcf", [128, 2 * 2048], F32)
    wcb = sb(esA3, "wcb", [128, 2 * 2048], BF16)
    tribs = sb(esA3, "tribs", [16, 2048], BF16)
    tsb8c = sb(esA3, "tsb8c", [8, 512], F32)
    tex = sb(esA3, "tex", [1, 512], F32)
    zz = sb(esA3, "zz", [128, 289], F32)

    p = Plan()
    p.add("sync", lambda e: e.dma_start(out=tribs[:], in_=trib_d.ap()[:, :]), inc="st", k=16)
    TRIBS_ST = cnt["st"]
    p.add("gpsimd", lambda e: e.memset(zz[:], 0.0), inc="gp")
    NGPZ = cnt["gp"]
    p.add("vector", lambda e: e.reduce_sum(sc2[:, 0:1], xl[:, :], axis=AX.X), inc="ve")
    p.add("scalar", lambda e: e.activation(stgb[:, 0:1], xl[:, 0:1], AF.Square,
                                           accum_out=sc2[:, 1:2]), inc="se")
    p.ev["scalar"].pop(); cnt["se"] -= 1
    for chx in range(16):
        p.add("scalar", lambda e, chx=chx: e.activation(
            stgb[:, :], xl[:, 1024 * chx: 1024 * chx + 1024], AF.Square,
            accum_out=rt[:, chx:chx + 1]), inc="se")
    p.add("vector", lambda e: e.reduce_sum(sc2[:, 1:2], rt[:, 0:16], axis=AX.X),
          waits=[("se", now("se"))], inc="ve")
    p.add("tensor", lambda e: e.matmul(pue[0:1, 0:2], onescf[:, :], sc2[:, :],
                                       start=True, stop=True),
          waits=[("ve", now("ve"))], inc="te")
    p.add("vector", lambda e: e.tensor_copy(sg1[0:1, 0:2], pue[0:1, 0:2]),
          waits=[("te", now("te"))], inc="ve")
    p.add("sync", lambda e: e.dma_start(out=st_in.ap()[:, :], in_=sg1[0:1, 0:2]),
          waits=[("ve", now("ve"))], inc="st", k=16)
    p.add("gpsimd", lambda e: e.collective_compute(
        "AllGather", mybir.AluOpType.bypass, replica_groups=[list(range(NCORE))],
        ins=[st_in.ap().opt()], outs=[st_out.ap().opt()]),
        waits=[("st", now("st"))], inc="cc")
    p.add("sync", lambda e: e.dma_start(out=ssb[:, :], in_=st_out.ap()[:, :]),
          waits=[("cc", now("cc"))], inc="st", k=16)
    p.add("tensor", lambda e: e.matmul(pue[0:1, 0:2], bms[:, :], ssb[:, :],
                                       start=True, stop=True),
          waits=[("st", now("st"))], inc="te")
    p.add("vector", lambda e: e.tensor_single_scalar(sg1[0:1, 2:4], pue[0:1, 0:2],
                                              1.0 / GN_N, AO.mult),
          waits=[("te", now("te"))], inc="ve")
    p.add("scalar", lambda e: e.activation(sg1[0:1, 4:5], sg1[0:1, 2:3], AF.Square),
          waits=[("ve", now("ve"))], inc="se")
    p.add("vector", lambda e: e.tensor_tensor(sg1[0:1, 5:6], sg1[0:1, 3:4],
                                              sg1[0:1, 4:5], AO.subtract),
          waits=[("se", now("se"))], inc="ve")
    p.add("scalar", lambda e: e.activation(sg1[0:1, 6:7], sg1[0:1, 5:6], AF.Sqrt,
                                           bias=epsv[0:1, 0:1]),
          waits=[("ve", now("ve"))], inc="se")
    p.add("vector", lambda e: e.reciprocal(sg1[0:1, 8:9], sg1[0:1, 6:7]),
          waits=[("se", now("se"))], inc="ve")
    p.add("vector", lambda e: e.tensor_tensor(sg1[0:1, 9:10], sg1[0:1, 2:3],
                                              sg1[0:1, 8:9], AO.mult),
          waits=[("ve", now("ve"))], inc="ve")
    p.add("tensor", lambda e: e.matmul(pue[:, 0:2], onesrf[:, :], sg1[0:1, 8:10],
                                       start=True, stop=True),
          waits=[("ve", now("ve"))], inc="te")
    p.add("vector", lambda e: e.tensor_copy(rsm[:, :], pue[:, 0:2]),
          waits=[("te", now("te"))], inc="ve")
    for d in range(DC):
        p.add("vector", lambda e, d=d: e.scalar_tensor_tensor(
            xfc(d, 0, 2048), xlc(d, 0, 2048), rsm[:, 0:1], xfc(d, 0, 2048),
            AO.mult, AO.add),
            waits=[("ve", now("ve"))], inc="ve")
        p.add("vector", lambda e, d=d: e.tensor_single_scalar(xfc(d, 0, 2048), xfc(d, 0, 2048), rsm[:, 1:2], AO.subtract),
            waits=[("ve", now("ve"))], inc="ve")
    emit_ln(p)
    VE_LN2 = cnt["ve"]

    for i in range(NT):
        for d4 in range(2):
            for j in range(4):
                dd = 4 * d4 + j
                w = []
                if j == 0:
                    w = [("ve", now("ve"))]
                    if i > 0 or d4 > 0:
                        w.append(("st", now("st")))
                p.add("tensor", lambda e, d4=d4, j=j, dd=dd, i=i: e.transpose(
                    sps[1][:, 512 * d4 + 128 * j: 512 * d4 + 128 * j + 128],
                    xf[:, 2048 * dd + 128 * i: 2048 * dd + 128 * i + 128], idfs[:, :]),
                    waits=w, inc=("te" if j == 3 else None))
            p.add("vector", lambda e, d4=d4: e.tensor_copy(
                stg[:, 512 * d4: 512 * d4 + 512], sps[1][:, 512 * d4: 512 * d4 + 512]),
                waits=[("te", now("te"))], inc="ve")
        p.add("sync", lambda e, i=i: e.dma_start(
            out=xspill.ap()[128 * i: 128 * i + 128, :], in_=stg[:, :]),
            waits=[("ve", now("ve"))], inc="st", k=16)
        for d4 in range(2):
            for j in range(4):
                dd = 4 * d4 + j
                w = []
                if j == 0:
                    w = [("ve", now("ve"))]
                    if i > 0 or d4 > 0:
                        w.append(("g2", now("g2")))
                p.add("tensor", lambda e, dd=dd, j=j, i=i: e.transpose(
                    pbf[:, 128 * j: 128 * j + 128],
                    xl[:, 2048 * dd + 128 * i: 2048 * dd + 128 * i + 128], idbs[:, :]),
                    waits=w, inc=("te" if j == 3 else None))
            p.add("vector", lambda e, d4=d4: e.tensor_copy(
                stgb[:, 512 * d4: 512 * d4 + 512], pbf[:, :]),
                waits=[("te", now("te"))], inc="ve")
        p.add("scalar", lambda e, i=i: e.dma_start(
            out=xt_in.ap()[128 * i: 128 * i + 128, :], in_=stgb[:, :]),
            waits=[("ve", now("ve"))], inc="g2", k=16)

    G2_PUB = cnt["g2"]

    p.add("gpsimd", lambda e: e.dma_start(
        out=t_in.ap().rearrange("(p f) -> p f", p=128), in_=zz[:, :]),
        waits=[("gp", NGPZ)], inc="g2", k=16)
    G2_TZ = cnt["g2"]

    for i in range(NT):
        for d in range(DC):
            w = [("ve", VE_LN2)] if d == 0 else []
            p.add("tensor", lambda e, d=d, i=i: e.matmul(
                puo[:, 8 * i: 8 * i + 8],
                xl[:, 2048 * d + 128 * i: 2048 * d + 128 * i + 128],
                wgb[:, 8 * d: 8 * d + 8],
                start=(d == 0), stop=(d == DC - 1)),
                waits=w, inc=("te" if d == DC - 1 else None))
    TE_LOG = cnt["te"]
    def b3d(t):
        return t[:, :].rearrange("p (t e) -> p t e", e=8)
    def bc16(t, c0):
        return bass.AP(t, c0, [[16, 128], [1, 16], [0, 8]])
    p.add("scalar", lambda e: e.activation(rt[:, :], puo[:, 0:128], AF.Exp),
          waits=[("te", TE_LOG)], inc="se")
    p.add("vector", lambda e: e.reduce_max(r1[:, :], b3d(rt), axis=AX.X),
          waits=[("se", now("se")), ("v2", now("v2"))], inc="v2")
    p.add("vector", lambda e: e.tensor_tensor(b3d(re1), b3d(rt), bc16(r1, 0), AO.is_equal),
          waits=[("v2", now("v2"))], inc="v2")
    p.add("vector", lambda e: e.tensor_tensor(re2[:, :], re1[:, :], rt[:, :], AO.mult),
          waits=[("v2", now("v2"))], inc="v2")
    p.add("vector", lambda e: e.tensor_tensor(re2[:, :], rt[:, :], re2[:, :], AO.subtract),
          waits=[("v2", now("v2"))], inc="v2")
    p.add("vector", lambda e: e.reduce_max(r2[:, :], b3d(re2), axis=AX.X),
          waits=[("v2", now("v2"))], inc="v2")
    p.add("vector", lambda e: e.tensor_tensor(b3d(re2), b3d(re2), bc16(r2, 0), AO.is_equal),
          waits=[("v2", now("v2"))], inc="v2")
    p.add("vector", lambda e: e.tensor_tensor(rg1[:, :], r1[:, :], r2[:, :], AO.add),
          waits=[("v2", now("v2"))], inc="v2")
    p.add("vector", lambda e: e.reciprocal(rg2[:, :], rg1[:, :]),
          waits=[("v2", now("v2"))], inc="v2")
    p.add("vector", lambda e: e.tensor_tensor(rg1[:, :], r1[:, :], rg2[:, :], AO.mult),
          waits=[("v2", now("v2"))], inc="v2")
    p.add("vector", lambda e: e.tensor_tensor(rg2[:, :], r2[:, :], rg2[:, :], AO.mult),
          waits=[("v2", now("v2"))], inc="v2")
    p.add("vector", lambda e: e.tensor_tensor(roh[:, :], re1[:, :], re2[:, :], AO.add),
          waits=[("v2", now("v2"))], inc="v2")
    V2_OH = cnt["v2"]
    for i in range(NT):
        p.add("tensor", lambda e, i=i: e.matmul(
            pue[:, 8 * i: 8 * i + 8], tris[:, :], roh[:, 8 * i: 8 * i + 8],
            start=(i == 0), stop=False),
            waits=[("v2", V2_OH)] + ([("ve", VE_LN2)] if i == 0 else []), inc=None)
        p.add("tensor", lambda e, i=i: e.matmul(
            sps[0][0:1, 8 * i: 8 * i + 8], onesc[:, :], roh[:, 8 * i: 8 * i + 8],
            start=True, stop=True),
            inc="te")
    TE_TOT = cnt["te"]
    p.add("vector", lambda e: e.tensor_copy(tex[0:1, 0:128], sps[0][0:1, 0:128]),
          waits=[("te", TE_TOT)], inc="ve")
    p.add("sync", lambda e: e.dma_start(
        out=tex_d.ap()[0:128].rearrange("(a f) -> a f", a=1), in_=tex[0:1, 0:128]),
        waits=[("ve", now("ve"))], inc="st", k=16)
    p.add("sync", lambda e: e.dma_start(
        out=tot16[:, :], in_=tex_d.ap()[0:128].rearrange("(i c) -> i c", i=16)),
        waits=[("st", now("st"))], inc="st", k=16)
    p.add("vector", lambda e: e.tensor_copy(tot16b[:, :], tot16[:, :]),
          waits=[("st", now("st"))], inc="ve")
    p.add("tensor", lambda e: e.matmul(sps[0][0:1, 128:136], onescf[0:16, :], tot16[:, :],
                                       start=True, stop=True),
          waits=[("ve", now("ve"))], inc="te")
    p.add("vector", lambda e: e.tensor_copy(sg1[0:1, 0:8], sps[0][0:1, 128:136]),
          waits=[("te", now("te"))], inc="ve")
    p.add("sync", lambda e: e.dma_start(out=cnt_in.ap()[:, :], in_=sg1[0:1, 0:8]),
          waits=[("ve", now("ve"))], inc="st", k=16)
    p.add("gpsimd", lambda e: e.collective_compute(
        "AllGather", mybir.AluOpType.bypass, replica_groups=[list(range(NCORE))],
        ins=[cnt_in.ap().opt()], outs=[cnt_out.ap().opt()]),
        waits=[("st", now("st"))], inc="cc")
    p.add("sync", lambda e: e.dma_start(out=cnts[:, :], in_=cnt_out.ap()[:, :]),
          waits=[("cc", now("cc"))], inc="st", k=16)
    p.add("vector", lambda e: e.tensor_copy(cntsb[:, :], cnts[:, :]),
          waits=[("st", now("st"))], inc="ve")
    VE_CNTS = cnt["ve"]
    for i in range(NT):
        p.add("tensor", lambda e, i=i: e.matmul(
            pue[:, 8 * i: 8 * i + 8], tribs[:, 128 * i: 128 * i + 128], tot16b[:, :],
            start=False, stop=False),
            waits=([("ve", VE_CNTS), ("st", TRIBS_ST)] if i == 0 else []), inc=None)
        p.add("tensor", lambda e, i=i: e.matmul(
            pue[:, 8 * i: 8 * i + 8], pms[:, :], cntsb[:, :],
            start=False, stop=True),
            inc="te")
    TE_CFULL = cnt["te"]
    for j, (eq, pcol, ecol) in enumerate([(re1, 0, 32), (re2, 16, 48)]):
        p.add("vector", lambda e, eq=eq: e.tensor_tensor(rt[:, :], eq[:, :], pue[:, 0:128], AO.mult),
              waits=[("te", TE_CFULL), ("v2", now("v2"))], inc="v2")
        p.add("vector", lambda e, pcol=pcol: e.reduce_sum(
            posj[:, pcol:pcol + 16], b3d(rt), axis=AX.X),
            waits=[("v2", now("v2"))], inc="v2")
        p.add("vector", lambda e, eq=eq: e.tensor_tensor(rt[:, :], eq[:, :], io8[:, :], AO.mult),
              waits=[("v2", now("v2"))], inc="v2")
        p.add("vector", lambda e, ecol=ecol: e.reduce_sum(
            posj[:, ecol:ecol + 16], b3d(rt), axis=AX.X),
            waits=[("v2", now("v2"))], inc="v2")
    for j, (pcol, ecol, gcol, garbT) in enumerate([(0, 32, rg1, garbA), (16, 48, rg2, garbB)]):
        p.add("vector", lambda e, pcol=pcol: e.tensor_single_scalar(
            re1[:, 0:16], posj[:, pcol:pcol + 16], float(CAP), AO.is_lt),
            waits=[("v2", now("v2"))], inc="v2")
        p.add("vector", lambda e, gcol=gcol, j=j: e.tensor_tensor(
            cfs[:, 16 * j: 16 * j + 16], gcol[:, :], re1[:, 0:16], AO.mult),
            waits=[("v2", now("v2"))], inc="v2")
        p.add("vector", lambda e, pcol=pcol, ecol=ecol: e.scalar_tensor_tensor(
            re1[:, 16:32], posj[:, ecol:ecol + 16], float(CAPPAD), posj[:, pcol:pcol + 16],
            AO.mult, AO.add),
            waits=[("v2", now("v2"))], inc="v2")
        p.add("vector", lambda e, pcol=pcol: e.tensor_single_scalar(
            re2[:, 64:80], posj[:, pcol:pcol + 16], 2048.0, AO.is_lt),
            waits=[("v2", now("v2"))], inc="v2")
        p.add("vector", lambda e: e.tensor_tensor(
            re2[:, 64:80], re2[:, 64:80], re1[:, 0:16], AO.mult),
            waits=[("v2", now("v2"))], inc="v2")
        p.add("vector", lambda e, pcol=pcol, ecol=ecol: e.scalar_tensor_tensor(
            re2[:, 80:96], posj[:, ecol:ecol + 16], 2048.0, posj[:, pcol:pcol + 16],
            AO.mult, AO.add),
            waits=[("v2", now("v2"))], inc="v2")
        p.add("vector", lambda e: e.tensor_tensor(
            re2[:, 80:96], re2[:, 80:96], re2[:, 64:80], AO.mult),
            waits=[("v2", now("v2"))], inc="v2")
        p.add("vector", lambda e, j=j: e.tensor_copy(
            sidg[:, 16 * j: 16 * j + 16], re2[:, 80:96]),
            waits=[("v2", now("v2"))], inc="v2")
        for (kk, b0, b1v, ll) in [(1, 2048.0, 4096.0, 2048.0), (2, 4096.0, None, 512.0)]:
            p.add("vector", lambda e, pcol=pcol, b0=b0: e.tensor_single_scalar(
                re2[:, 64:80], posj[:, pcol:pcol + 16], b0, AO.is_ge),
                waits=[("v2", now("v2"))], inc="v2")
            if b1v is not None:
                p.add("vector", lambda e, pcol=pcol, b1v=b1v: e.tensor_single_scalar(
                    re2[:, 96:112], posj[:, pcol:pcol + 16], b1v, AO.is_lt),
                    waits=[("v2", now("v2"))], inc="v2")
                p.add("vector", lambda e: e.tensor_tensor(
                    re2[:, 64:80], re2[:, 64:80], re2[:, 96:112], AO.mult),
                    waits=[("v2", now("v2"))], inc="v2")
            p.add("vector", lambda e: e.tensor_tensor(
                re2[:, 64:80], re2[:, 64:80], re1[:, 0:16], AO.mult),
                waits=[("v2", now("v2"))], inc="v2")
            p.add("vector", lambda e, pcol=pcol, b0=b0: e.tensor_single_scalar(
                re2[:, 80:96], posj[:, pcol:pcol + 16], b0, AO.subtract),
                waits=[("v2", now("v2"))], inc="v2")
            p.add("vector", lambda e, ecol=ecol, ll=ll: e.scalar_tensor_tensor(
                re2[:, 80:96], posj[:, ecol:ecol + 16], ll, re2[:, 80:96],
                AO.mult, AO.add),
                waits=[("v2", now("v2"))], inc="v2")
            p.add("vector", lambda e: e.tensor_single_scalar(
                re2[:, 80:96], re2[:, 80:96], 65536.0, AO.subtract),
                waits=[("v2", now("v2"))], inc="v2")
            p.add("vector", lambda e: e.tensor_tensor(
                re2[:, 80:96], re2[:, 80:96], re2[:, 64:80], AO.mult),
                waits=[("v2", now("v2"))], inc="v2")
            p.add("vector", lambda e: e.tensor_single_scalar(
                re2[:, 80:96], re2[:, 80:96], 65536.0, AO.add),
                waits=[("v2", now("v2"))], inc="v2")
            p.add("vector", lambda e, j=j, kk=kk: e.tensor_copy(
                sidg[:, 32 * kk + 16 * j: 32 * kk + 16 * j + 16], re2[:, 80:96]),
                waits=[("v2", now("v2"))], inc="v2")
        p.add("vector", lambda e, ecol=ecol: e.tensor_single_scalar(
            re1[:, 48:64], posj[:, ecol:ecol + 16], 4.0, AO.is_ge),
            waits=[("v2", now("v2"))], inc="v2")
        p.add("vector", lambda e: e.scalar_tensor_tensor(
            re1[:, 16:32], re1[:, 48:64], 64.0, re1[:, 16:32], AO.mult, AO.add),
            waits=[("v2", now("v2"))], inc="v2")
        p.add("vector", lambda e, garbT=garbT: e.tensor_tensor(
            re1[:, 16:32], re1[:, 16:32], garbT[:, :], AO.subtract),
            waits=[("v2", now("v2"))], inc="v2")
        p.add("vector", lambda e: e.tensor_tensor(
            re1[:, 16:32], re1[:, 16:32], re1[:, 0:16], AO.mult),
            waits=[("v2", now("v2"))], inc="v2")
        p.add("vector", lambda e, garbT=garbT: e.tensor_tensor(
            re1[:, 16:32], re1[:, 16:32], garbT[:, :], AO.add),
            waits=[("v2", now("v2"))], inc="v2")
        p.add("vector", lambda e, j=j: e.tensor_copy(
            sidu[:, 16 * j: 16 * j + 16], re1[:, 16:32]),
            waits=[("v2", now("v2"))], inc="v2")
    V2S = cnt["v2"]
    for i in range(NT):
        for j in range(2):
            p.add("gpsimd", lambda e, i=i, j=j: e.indirect_dma_start(
                out=t_in.ap()[:, None],
                out_offset=IOOA(ap=sidu[:, 16 * j + i: 16 * j + i + 1], axis=0),
                in_=tkr[:, i:i + 1], in_offset=None, compute_op=AO.add),
                waits=[("v2", V2S), ("g2", G2_TZ)], inc="g2", k=16)
    p.add("gpsimd", lambda e: e.collective_compute(
        "AllReduce", mybir.AluOpType.add, replica_groups=[list(range(NCORE))],
        ins=[t_in.ap().opt()], outs=[t_out.ap().opt()]),
        waits=[("g2", now("g2"))], inc="cc")
    CC_T = cnt["cc"]
    p.add("gpsimd", lambda e: e.collective_compute(
        "AllGather", mybir.AluOpType.bypass, replica_groups=[list(range(NCORE))],
        ins=[xt_in.ap().opt()], outs=[xt_out.ap().opt()]),
        waits=[("g2", G2_PUB)], inc="cc")
    CC_XT = cnt["cc"]
    NWCH = 32
    WSEM = [("wc", "ws"), ("wc2", "ws2")]

    def wsrc_ap(kk):
        if kk < 16:
            r, h = kk // 2, kk % 2
            return w1_d.ap()[128 * r: 128 * r + 128, 2048 * h: 2048 * h + 2048]
        m = kk - 16
        return w2_d.ap()[256 * m: 256 * m + 256, :].rearrange("(a p) d -> p a d", a=2)

    def wdst(kk):
        return (w1s, 2048 * kk) if kk < 16 else (w2s, 2048 * (kk - 16))

    def wload(kk):
        sl = kk % 2
        outap = wcf[:, 2048 * sl: 2048 * sl + 2048]
        if kk >= 16:
            outap = outap.rearrange("p (a d) -> p a d", a=2)
        p.add("gpsimd", lambda e, kk=kk, outap=outap: e.dma_start(
            out=outap, in_=wsrc_ap(kk)),
            waits=([("wv", kk - 1)] if kk >= 2 else []), inc=WSEM[sl][0], k=16)

    wload(0)
    wload(1)
    for kk in range(NWCH):
        sl = kk % 2
        dT, do = wdst(kk)
        p.add("gpsimd", lambda e, dT=dT, do=do, sl=sl: e.dma_start(
            out=dT.ap()[:, do: do + 2048], in_=wcb[:, 2048 * sl: 2048 * sl + 2048]),
            waits=[("wv", kk + 1)], inc=WSEM[sl][1], k=16)
        if kk + 2 < NWCH:
            wload(kk + 2)

    def emit_wcast(kks):
        for kk in kks:
            sl = kk % 2
            w = [(WSEM[sl][0], 16 * (kk // 2 + 1))]
            if kk >= 2:
                w.append((WSEM[sl][1], 16 * ((kk - 2) // 2 + 1)))
            p.add("vector", lambda e, sl=sl: e.tensor_copy(
                wcb[:, 2048 * sl: 2048 * sl + 2048], wcf[:, 2048 * sl: 2048 * sl + 2048]),
                waits=w, inc="wv")

    te_prev_mm = now("te")
    for c in range(9):
        if c < 8:
            emit_wcast(range(4 * c, 4 * c + 4))
        p.add("sync", lambda e, c=c: e.dma_start(
            out=tsb8c[0:4, :],
            in_=bass.AP(t_out, 512 * c, [[CAPPAD, 4], [1, 512]])),
            waits=[("cc", CC_T)] + ([("te", te_prev_mm)] if c > 0 else []), inc="st", k=16)
        p.add("sync", lambda e, c=c: e.dma_start(
            out=tsb8c[4:8, :],
            in_=bass.AP(t_out, THALF + 512 * c, [[CAPPAD, 4], [1, 512]])),
            inc="st", k=16)
        p.add("tensor", lambda e, c=c: e.matmul(
            sps[1][0:1, 0:512],
            eohs[:, :], tsb8c[:, :], start=True, stop=True),
            waits=[("st", now("st"))] + ([("ve", now("ve"))] if c > 0 else []), inc="te")
        te_prev_mm = now("te")
        p.add("vector", lambda e: e.tensor_single_scalar(tex[0:1, 0:512], sps[1][0:1, 0:512], float(TOK * NCORE - 1), AO.min),
            waits=[("te", now("te"))], inc="ve")
        p.add("sync", lambda e, c=c: e.dma_start(
            out=tex_d.ap()[512 * c: 512 * c + 512].rearrange("(a f) -> a f", a=1),
            in_=tex[0:1, 0:512]),
            waits=[("ve", now("ve"))], inc="st", k=16)
    p.add("sync", lambda e: e.dma_start(
        out=xidxf[:, :], in_=tex_d.ap()[:].rearrange("(s p) -> p s", p=128),
        allow_slow_non_contiguous=True),
        waits=[("st", now("st"))], inc="st", k=16)
    p.add("vector", lambda e: e.tensor_copy(xidxu[:, :], xidxf[:, :]),
          waits=[("st", now("st"))], inc="ve")
    p.emit()

    if stage == 3:
        emit_dump(xf, xl)
        pd = Plan()
        pd.add("sync", lambda e: e.dma_start(out=out_d.ap()[0:128, 0:36], in_=xidxf[:, :]),
               inc="st", k=16)
        pd.add("sync", lambda e: e.dma_start(out=out_d.ap()[128:256, 0:64], in_=posj[:, :]),
               inc="st", k=16)
        pd.add("sync", lambda e: e.dma_start(out=out_d.ap()[256:384, 0:32], in_=cfs[:, :]),
               inc="st", k=16)
        pd.add("sync", lambda e: e.dma_start(out=out_d.ap()[384:512, 0:16].rearrange("p c -> p c"), in_=tot16[:, :].rearrange("a b -> a b") if False else cnts[:, :]),
               inc="st", k=16) if False else None
        pd.add("sync", lambda e: e.dma_start(out=out_d.ap()[384:392, 0:8], in_=cnts[:, :]),
               inc="st", k=16)
        pd.add("sync", lambda e: e.dma_start(
            out=stg[:, 0:36], in_=bass.AP(t_in, 0, [[36, 128], [1, 36]])),
            inc="st", k=16)
        pd.add("sync", lambda e: e.dma_start(out=out_d.ap()[512:640, 0:36], in_=stg[:, 0:36]),
               waits=[("st", now("st"))], inc="st", k=16)
        pd.add("sync", lambda e: e.dma_start(
            out=stg[:, 0:36], in_=bass.AP(t_out, 0, [[36, 128], [1, 36]])),
            waits=[("st", now("st"))], inc="st", k=16)
        pd.add("sync", lambda e: e.dma_start(out=out_d.ap()[640:768, 0:36], in_=stg[:, 0:36]),
               waits=[("st", now("st"))], inc="st", k=16)
        pd.add("sync", lambda e: e.dma_start(
            out=stg[:, 0:36], in_=bass.AP(t_out, THALF, [[36, 128], [1, 36]])),
            waits=[("st", now("st"))], inc="st", k=16)
        pd.add("sync", lambda e: e.dma_start(out=out_d.ap()[768:896, 0:36], in_=stg[:, 0:36]),
               waits=[("st", now("st"))], inc="st", k=16)
        pd.add("vector", lambda e: e.tensor_copy(rt[:, 0:32], sidu[:, :]),
               waits=[("st", now("st"))], inc="ve")
        pd.add("sync", lambda e: e.dma_start(out=out_d.ap()[896:1024, 0:32], in_=rt[:, 0:32]),
               waits=[("ve", now("ve"))], inc="st", k=16)
        pd.add("vector", lambda e: e.tensor_copy(rt[:, 32:64], sidg[:, :]),
               waits=[("st", now("st"))], inc="ve")
        pd.add("sync", lambda e: e.dma_start(out=out_d.ap()[896:1024, 32:64], in_=rt[:, 32:64]),
               waits=[("ve", now("ve"))], inc="st", k=16)
        pd.add("sync", lambda e: e.drain(), waits=[("st", now("st"))])
        pd.emit()
        esA3.close(); esA.close(); es.close()
        return nc

    esA3.close()
    esA.close()
    esB = ExitStack()
    b1c = sb(esB, "b1c", [128, 32], F32)
    b2b = sb(esB, "b2b", [1, D], BF16)
    w1b = sb(esB, "w1b", [128, DC * HID], BF16)
    w2b = sb(esB, "w2b", [128, 32 * D], BF16)
    xg = sb(esB, "xg", [128, 6 * D], BF16)
    xfm = sb(esB, "xfm", [128, 2 * DC * 512], BF16)
    hsb = sb(esB, "hsb", [128, 32 * 512], BF16)
    ysb = sb(esB, "ysb", [128, 2048], BF16)
    gbuf = sb(esB, "gbuf", [128, 2 * D], BF16)

    p = Plan()
    BND = {}
    def _mkregs(e):
        BND[2] = e.alloc_register("bnd2")
        e.reg_mov(BND[2], NCORE * 2048 - 1)
        BND[3] = e.alloc_register("bnd3")
        return e.reg_mov(BND[3], NCORE * 512 - 1)
    p.add("gpsimd", _mkregs)
    if stage == 38:
        p.add("gpsimd", lambda e: e.engine_nop())
        p.emit()
        esB.close(); es.close()
        return nc
    p.add("sync", lambda e: e.dma_start(
        out=stg[0:32, 0:128],
        in_=b1_d.ap()[0:1, :].rearrange("a (h p) -> (a h) p", h=32)),
        inc="wc3", k=16)
    B1ROW_WC = cnt["wc3"]
    p.add("tensor", lambda e: e.transpose(pun[:, 0:32], stg[0:32, 0:128], idfs[0:32, 0:32]),
          waits=[("wc3", B1ROW_WC)], inc="te")
    B1TR = cnt["te"]
    p.add("vector", lambda e: e.tensor_copy(b1c[:, :], pun[:, 0:32]),
          waits=[("te", B1TR)], inc="v3")
    B1CV = cnt["v3"]
    W1ST = {}
    for cch in range(4):
        for r in range(8):
            off = 4096 * r + 1024 * cch
            p.add("sync", lambda e, off=off: e.dma_start(
                out=w1b[:, off: off + 1024], in_=w1s.ap()[:, off: off + 1024]),
                waits=([("ws", 16 * 16), ("ws2", 16 * 16)] if (cch == 0 and r == 0) else []), inc="st", k=16)
        W1ST[cch] = cnt["st"]
    p.add("sync", lambda e: e.dma_start(out=stg[0:1, :], in_=b2_d.ap()[:, :]),
          waits=[("te", B1TR)], inc="ws", k=16)
    p.add("vector", lambda e: e.tensor_copy(b2b[:, :], stg[0:1, :]),
          waits=[("ws", now("ws"))], inc="v3")
    VW = now("v3")
    p.add("sync", lambda e: e.dma_start(out=w2b[:, :], in_=w2s.ap()[:, :]),
          inc="st", k=16)
    W2ST = cnt["st"]
    if stage == 39:
        p.add("gpsimd", lambda e: e.engine_nop(), waits=[("v3", VW)])
        p.emit()
        esB.close(); es.close()
        return nc

    S3B = cnt["s3"]

    def emit_ag(idx):
        off, ln, blk = [(0, 2048, 4), (2048 * D, 2048, 8), (4096 * D, 512, 9)][idx - 1]
        p.add("gpsimd", lambda e, off=off, ln=ln: e.collective_compute(
            "AllGather", mybir.AluOpType.bypass, replica_groups=[list(range(NCORE))],
            ins=[bass.AP(y_in, off, [[1, ln * D]])],
            outs=[[yo1, yo2, yo3][idx - 1].ap().opt()]),
            waits=[("s3", S3B + 32 * blk)], inc="cc")
        return cnt["cc"]

    bcntA = {}

    def emit_passA(pairs):
        for ii, jj in pairs:
            sl = (2 * ii + jj) % 2
            w = [("cc", CC_Y1)]
            if 2 * ii + jj >= 2:
                w.append(("g3", bcntA[2 * ii + jj - 2]))
            p.add("gpsimd", lambda e, ii=ii, jj=jj, sl=sl: e.indirect_dma_start(
                out=gbuf[:, 1024 * sl: 1024 * sl + 1024], out_offset=None,
                in_=yo1.ap()[:, :],
                in_offset=IOOA(ap=sidg[:, 16 * jj + ii: 16 * jj + ii + 1], axis=0)),
                waits=w, inc="g3", k=16)
            p.add("gpsimd", lambda e, ii=ii, jj=jj, sl=sl: e.dma_start(
                out=yab.ap()[256 * ii + 128 * jj: 256 * ii + 128 * jj + 128, :],
                in_=gbuf[:, 1024 * sl: 1024 * sl + 1024]),
                waits=[("g3", now("g3"))], inc="g3", k=16)
            bcntA[2 * ii + jj] = cnt["g3"]

    PAIRS = [(ii, jj) for ii in range(NT) for jj in range(2)]

    def xg_slice(st_):
        return xg[:, 1024 * (st_ % 6): 1024 * (st_ % 6) + 1024]

    import os as _os
    _ng40 = int(_os.environ.get("NG40", NST))
    XG3 = {}
    for st_ in range(NST if stage != 40 else _ng40):
        w = [("cc", CC_XT), ("ve", now("ve"))]
        if st_ >= 6 and stage != 40:
            w.append(("t3", 2 * (st_ - 5)))
        p.add("gpsimd", lambda e, st_=st_: e.indirect_dma_start(
            out=xg_slice(st_), out_offset=None,
            in_=xt_out.ap()[:, :], in_offset=IOOA(ap=xidxu[:, st_:st_ + 1], axis=0)),
            waits=w, inc="g3", k=16)
        XG3[st_] = cnt["g3"]
        if stage != 40:
            if st_ == 21:
                CC_Y1 = emit_ag(1)
            elif st_ == 25:
                emit_passA(PAIRS[0:16])
            elif st_ == 29:
                emit_passA(PAIRS[16:32])
            elif st_ == 35:
                CC_Y2 = emit_ag(2)
                CC_Y3 = emit_ag(3)

    if stage == 40:
        p.add("gpsimd", lambda e: e.engine_nop(), waits=[("g3", now("g3"))])
        p.emit()
        esB.close(); es.close()
        return nc
    se_gelu = {}
    t2_mm2 = {}
    for sbk in range(9):
        rot = sbk % 2
        for sti in range(4):
            st_ = 4 * sbk + sti
            for d4 in range(2):
                for j in range(4):
                    dd = 4 * d4 + j
                    w = []
                    if j == 0:
                        w = [("g3", XG3[st_])]
                        if sbk > 0 or sti > 0 or d4 > 0:
                            w.append(("v3", now("v3")))
                    p.add("tensor", lambda e, st_=st_, dd=dd, j=j: e.transpose(
                        pbf[:, 128 * j: 128 * j + 128],
                        xg_slice(st_)[:, 128 * dd: 128 * dd + 128], idbs[:, :]),
                        waits=w, inc=("t3" if j == 3 else None))
                wxe = [("t3", now("t3"))]
                if sbk >= 2 and sti == 0 and d4 == 0 and (sbk - 2, "mm1") in t2_mm2:
                    wxe.append(("t2", t2_mm2[(sbk - 2, "mm1")]))
                p.add("vector", lambda e, rot=rot, sti=sti, d4=d4: e.tensor_copy(
                    xfm[:, :].rearrange("p (r d c) -> p r d c", r=2, d=DC)[:, rot, 4 * d4:4 * d4 + 4, 128 * sti:128 * sti + 128],
                    pbf[:, :].rearrange("p (a c) -> p a c", a=4)),
                    waits=wxe, inc="v3")
        XP = now("v3")
        if stage == 41:
            continue
        for hc in range(32):
            hh = hc % 2
            for d in range(DC):
                w = []
                if d == 0:
                    w = [("v3", XP)]
                    if hc >= 2:
                        w.append(("se", se_gelu[(sbk, hc - 2)]))
                    elif sbk > 0:
                        w.append(("se", se_gelu[(sbk - 1, 30 + hc)]))
                    if sbk == 0:
                        w.append(("st", W1ST[hc // 8]))
                        if hc == 0:
                            w.append(("v3", B1CV))
                p.add("tensor", lambda e, hc=hc, d=d, rot=rot, hh=hh: e.matmul(
                    sps[0][:, 512 * hh: 512 * hh + 512],
                    w1b[:, 4096 * d + 128 * hc: 4096 * d + 128 * hc + 128],
                    xfm[:, 4096 * rot + 512 * d: 4096 * rot + 512 * d + 512],
                    start=(d == 0), stop=(d == DC - 1)),
                    waits=w, inc=("t2" if d == DC - 1 else None))
            wg2 = [("t2", now("t2"))]
            if sbk > 0 and (sbk - 1, "mm2") in t2_mm2 and hc == 0:
                wg2.append(("t2", t2_mm2[(sbk - 1, "mm2")]))
            p.add("scalar", lambda e, hc=hc, hh=hh: e.activation(
                hsb[:, 512 * hc: 512 * hc + 512], sps[0][:, 512 * hh: 512 * hh + 512],
                AF.Gelu_apprx_tanh, bias=b1c[:, hc:hc + 1]),
                waits=wg2, inc="se")
            se_gelu[(sbk, hc)] = cnt["se"]
        t2_mm2[(sbk, "mm1")] = cnt["t2"]
        for dh in range(2):
            for hc in range(32):
                for sti in range(4):
                    outp = [sps[1][:, 0:512], sps[1][:, 512:1024], pue[:, :], puo[:, :]][sti]
                    w = []
                    if hc == 0 and sti == 0:
                        w = [("se", se_gelu[(sbk, 31)]), ("v3", now("v3"))]
                        if sbk == 0 and dh == 0:
                            w.append(("v3", VW))
                            w.append(("st", W2ST))
                    p.add("tensor", lambda e, outp=outp, hc=hc, sti=sti, dh=dh: e.matmul(
                        outp,
                        hsb[:, 512 * hc + 128 * sti: 512 * hc + 128 * sti + 128],
                        w2b[:, 1024 * hc + 512 * dh: 1024 * hc + 512 * dh + 512],
                        start=(hc == 0), stop=False),
                        waits=w, inc=None)
            for sti in range(4):
                outp = [sps[1][:, 0:512], sps[1][:, 512:1024], pue[:, :], puo[:, :]][sti]
                p.add("tensor", lambda e, outp=outp, dh=dh: e.matmul(
                    outp, onesw[0:1, 0:128], b2b[0:1, 512 * dh: 512 * dh + 512],
                    start=False, stop=True),
                    inc=("t2" if sti == 3 else None))
            for sti in range(4):
                outp = [sps[1][:, 0:512], sps[1][:, 512:1024], pue[:, :], puo[:, :]][sti]
                p.add("vector", lambda e, outp=outp, sti=sti: e.tensor_copy(
                    ysb[:, 512 * sti: 512 * sti + 512], outp),
                    waits=([("t2", now("t2")), ("s3", now("s3"))] if sti == 0 else [("v3", now("v3"))]),
                    inc="v3")
            p.add("sync", lambda e, sbk=sbk, dh=dh: e.dma_start(
                out=y_in.ap()[512 * sbk: 512 * sbk + 512, 512 * dh: 512 * dh + 512].rearrange("(s p) c -> p s c", s=4),
                in_=ysb[:, :].rearrange("p (s c) -> p s c", s=4)),
                waits=[("v3", now("v3"))], inc="s3", k=16)
        t2_mm2[(sbk, "mm2")] = cnt["t2"]
    if stage == 41:
        p.add("gpsimd", lambda e: e.engine_nop(), waits=[("v3", now("v3"))])
        p.emit()
        esB.close(); es.close()
        return nc
    if stage == 4:
        p.add("gpsimd", lambda e: e.engine_nop(), waits=[("cc", CC_Y3), ("g3", now("g3"))])
        p.emit()
        esB.close(); es.close()
        return nc
    p.emit()

    esB.close()
    esC = ExitStack()
    ygt = sb(esC, "ygt", [128, 16 * D], BF16)
    xtb = sb(esC, "xtb", [128, 4 * D], F32)
    ob = sb(esC, "ob", [128, 4 * D], F32)

    p = Plan()
    V3B = cnt["v3"]
    ldc = {}

    def emit_ld(i):
        rot8 = i % 8
        wld = [("g3", bcntA[2 * i + 1])]
        if i >= 8:
            wld.append(("v3", V3B + 2 * (i - 7)))
        p.add("sync", lambda e, i=i, rot8=rot8: e.dma_start(
            out=ygt[:, 2048 * rot8: 2048 * rot8 + 2048],
            in_=yab.ap()[256 * i: 256 * i + 256, :].rearrange("(a p) d -> p a d", a=2)),
            waits=wld, inc="st", k=16)
        ldc[i] = cnt["st"]

    for i in range(8):
        emit_ld(i)
    for i in range(NT):
        rot8 = i % 8
        rot = i % 4
        for j in range(2):
            p.add("gpsimd", lambda e, i=i, j=j, rot8=rot8: e.indirect_dma_start(
                out=ygt[:, 2048 * rot8 + 1024 * j: 2048 * rot8 + 1024 * j + 1024],
                out_offset=None, in_=yo2.ap()[:, :],
                in_offset=IOOA(ap=sidg[:, 32 + 16 * j + i: 33 + 16 * j + i], axis=0),
                bounds_check=BND[2], oob_is_err=False),
                waits=[("cc", CC_Y2), ("st", ldc[i]), ("g3", now("g3"))], inc="g3", k=16)
            p.add("gpsimd", lambda e, i=i, j=j, rot8=rot8: e.indirect_dma_start(
                out=ygt[:, 2048 * rot8 + 1024 * j: 2048 * rot8 + 1024 * j + 1024],
                out_offset=None, in_=yo3.ap()[:, :],
                in_offset=IOOA(ap=sidg[:, 64 + 16 * j + i: 65 + 16 * j + i], axis=0),
                bounds_check=BND[3], oob_is_err=False),
                waits=[("cc", CC_Y3), ("g3", now("g3"))], inc="g3", k=16)
        gcc = cnt["g3"]
        p.add("sync", lambda e, i=i, rot=rot: e.dma_start(
            out=xtb[:, 1024 * rot: 1024 * rot + 1024],
            in_=xspill.ap()[128 * i: 128 * i + 128, :]),
            waits=([("v3", V3B + 2 * (i - 3))] if i >= 4 else []), inc="st", k=16)
        p.add("vector", lambda e, i=i, rot=rot, rot8=rot8: e.scalar_tensor_tensor(
            ob[:, 1024 * rot: 1024 * rot + 1024],
            ygt[:, 2048 * rot8: 2048 * rot8 + 1024], cfs[:, i:i + 1],
            xtb[:, 1024 * rot: 1024 * rot + 1024], AO.mult, AO.add),
            waits=[("g3", gcc), ("st", now("st")), ("v3", now("v3"))], inc="v3")
        p.add("vector", lambda e, i=i, rot=rot, rot8=rot8: e.scalar_tensor_tensor(
            ob[:, 1024 * rot: 1024 * rot + 1024],
            ygt[:, 2048 * rot8 + 1024: 2048 * rot8 + 2048], cfs[:, 16 + i:17 + i],
            ob[:, 1024 * rot: 1024 * rot + 1024], AO.mult, AO.add),
            waits=[("v3", now("v3"))], inc="v3")
        p.add("sync", lambda e, i=i, rot=rot: e.dma_start(
            out=out_d.ap()[128 * i: 128 * i + 128, :],
            in_=ob[:, 1024 * rot: 1024 * rot + 1024]),
            waits=[("v3", now("v3"))], inc="io", k=16)
        if i + 8 < NT:
            emit_ld(i + 8)
    p.add("sync", lambda e: e.drain(), waits=[("io", now("io"))])
    p.emit()

    esC.close(); es.close()
    return nc


_CACHE = {}


def _get_nc(stage=STAGE):
    if stage not in _CACHE:
        _CACHE[stage] = _build(stage)
    return _CACHE[stage]


def make_consts(core):
    import ml_dtypes
    bf = ml_dtypes.bfloat16
    tri = (np.arange(128)[:, None] < np.arange(128)[None, :]).astype(bf)
    trib = np.zeros((16, 2048), bf)
    for i in range(16):
        trib[:, 128 * i:128 * i + 128] = (np.arange(16)[:, None] < i).astype(bf)
    pmask = np.broadcast_to((np.arange(8) < core)[:, None], (8, 128)).astype(bf)
    bmask = ((np.arange(8) // 4) == (core // 4)).astype(np.float32)[:, None]
    eoh = (np.arange(8) == core).astype(np.float32)[:, None]
    iota8 = np.broadcast_to(np.tile(np.arange(8, dtype=np.float32), 16)[None, :], (128, 128)).copy()
    tokrow = (TOK * core + 128 * np.arange(16, dtype=np.float32)[None, :]
              + np.arange(128, dtype=np.float32)[:, None]).astype(np.float32)
    garbA = np.broadcast_to((GARB + np.arange(16, dtype=np.float32))[None, :], (128, 16)).copy()
    garbB = garbA + 16
    return dict(
        idf=np.eye(128, dtype=np.float32),
        idb=np.eye(128, dtype=np.float32).astype(bf),
        tri=np.ascontiguousarray(tri),
        trib=trib,
        pmask=np.ascontiguousarray(pmask),
        bmask=bmask, eoh=eoh, iota8=iota8, tokrow=np.ascontiguousarray(tokrow),
        garbA=garbA, garbB=garbB,
    )


def kernel(L0_attn_x, L0_attn_Wq, L0_attn_Wk, L0_attn_Wv,
           L0_ffn_Wg, L0_ffn_W1, L0_ffn_b1, L0_ffn_W2, L0_ffn_b2):
    from concourse.bass_utils import run_bass_kernel_spmd
    x = np.ascontiguousarray(np.asarray(L0_attn_x, dtype=np.float32)).reshape(NCORE * TOK, D)
    wq = np.ascontiguousarray(np.asarray(L0_attn_Wq, dtype=np.float32))
    wk = np.ascontiguousarray(np.asarray(L0_attn_Wk, dtype=np.float32))
    wv = np.ascontiguousarray(np.asarray(L0_attn_Wv, dtype=np.float32))
    wg = np.ascontiguousarray(np.asarray(L0_ffn_Wg, dtype=np.float32))
    w1 = np.asarray(L0_ffn_W1, dtype=np.float32)
    b1 = np.asarray(L0_ffn_b1, dtype=np.float32)
    w2 = np.asarray(L0_ffn_W2, dtype=np.float32)
    b2 = np.asarray(L0_ffn_b2, dtype=np.float32)
    nc = _get_nc(STAGE)
    in_maps = []
    for c in range(NCORE):
        m = dict(xin=np.ascontiguousarray(x[TOK * c: TOK * (c + 1)]),
                 wq=wq, wk=wk, wv=wv, wg=wg,
                 w1=np.ascontiguousarray(w1[c]),
                 b1=np.ascontiguousarray(b1[c])[None, :],
                 w2=np.ascontiguousarray(w2[c]),
                 b2=np.ascontiguousarray(b2[c])[None, :])
        m.update(make_consts(c))
        in_maps.append(m)
    res = run_bass_kernel_spmd(nc, in_maps, core_ids=list(range(NCORE)))
    out = np.concatenate([np.asarray(res.results[c]["out"]) for c in range(NCORE)], 0)
    return out.reshape(2, 8192, D).astype(np.float32)
